# version 03
# baseline (speedup 1.0000x reference)
import numpy as np
import ml_dtypes
from contextlib import ExitStack

import concourse.bass as bass
import concourse.mybir as mybir
from concourse.bass_utils import run_bass_kernel_spmd

F32 = mybir.dt.float32
BF16 = mybir.dt.bfloat16
AF = mybir.ActivationFunctionType
ALU = mybir.AluOpType
AX = mybir.AxisListType
BFNP = ml_dtypes.bfloat16

T = 2048
D = 1024
NCHUNK = 8
TBS = [0, 512, 1024, 1536]
NTB = 4


def pack_lhsT(W):
    K, M = W.shape
    kc = K // 128 if K >= 128 else 1
    kp = min(K, 128)
    mc = (M + 127) // 128
    mp = min(M, 128)
    assert K % kp == 0 and M % mp == 0
    arr = W.reshape(kc, kp, mc, mp).transpose(2, 1, 0, 3)
    arr = arr.reshape(mc, kp, kc * mp)
    if kp < 128:
        arr = np.concatenate(
            [arr, np.zeros((mc, 128 - kp, kc * mp), dtype=arr.dtype)], axis=1)
    return np.ascontiguousarray(arr)


def pack_rhs(W):
    K, M = W.shape
    kp = min(K, 128)
    return np.ascontiguousarray(W.reshape(K // kp, kp, M))


def pack_cols(v):
    n = v.shape[0] // 128
    return np.ascontiguousarray(v.reshape(n, 128).T)


def deinterleave_perm(n, head=64):
    p = np.zeros(n, dtype=np.int64)
    for h in range(n // head):
        for j in range(head):
            p[h * head + j] = h * head + (2 * j if j < head // 2 else 2 * (j - head // 2) + 1)
    return p


def host_masks():
    kp = np.arange(128)[:, None]
    qf = np.arange(512)[None, :]
    mc4 = np.stack([(qf - kp >= 128 * j).astype(np.float32) for j in range(4)])
    m8 = []
    for j in range(-4, 4):
        d = qf - kp - 128 * j
        m8.append(((d >= 0) & (d <= 512)).astype(np.float32))
    return mc4.astype(BFNP), np.stack(m8).astype(BFNP)


def host_l0_xpos_tables():
    DH = 64
    WINDOW = 512
    half = np.arange(0, DH, 2, dtype=np.float64)
    inv_freq = 1.0 / 10000.0 ** (half / DH)
    t = np.arange(T, dtype=np.float64)
    freqs = t[:, None] * inv_freq[None, :]
    freqs = np.concatenate([freqs, freqs], 1)
    base = (half + 0.4 * DH) / (1.4 * DH)
    power = (t - T // 2) / float(WINDOW // 2)
    scale = base[None, :] ** power[:, None]
    scale = np.concatenate([scale, scale], 1)
    cos, sin = np.cos(freqs), np.sin(freqs)
    sgn = np.where(np.arange(DH)[None, :] < 32, 1.0, -1.0)
    CSq = (cos * scale).T
    SSq = (sin * scale).T * sgn.T
    CSk = (cos / scale).T
    SSk = (sin / scale).T * sgn.T
    out = []
    for A in (CSq, SSq, CSk, SSk):
        out.append(np.ascontiguousarray(np.vstack([A, A]).astype(np.float32)))
    return out


def host_l1_rope_tables():
    DH = 64
    inv_freq = 1.0 / 10000.0 ** (np.arange(0, DH, 2, dtype=np.float64) / DH)
    t = np.arange(T, dtype=np.float64)
    fr = t[:, None] * inv_freq[None, :]
    c = np.cos(fr).T
    s = np.sin(fr).T
    CS = np.vstack([c, c])
    SS = np.vstack([s, -s])
    CS = np.vstack([CS, CS]).astype(np.float32)
    SS = np.vstack([SS, SS]).astype(np.float32)
    return np.ascontiguousarray(CS), np.ascontiguousarray(SS)


def host_consts():
    d = {}
    mc4, m8 = host_masks()
    d["maskc4"] = mc4
    d["mask8"] = m8
    CSq, SSq, CSk, SSk = host_l0_xpos_tables()
    d["l0_CSq"], d["l0_SSq"], d["l0_CSk"], d["l0_SSk"] = CSq, SSq, CSk, SSk
    CS, SS = host_l1_rope_tables()
    d["l1_CS"], d["l1_SS"] = CS, SS
    ident = np.eye(128, dtype=np.float32)
    d["ident"] = ident
    pi = deinterleave_perm(128)
    dm = np.zeros((128, 128), dtype=np.float32)
    for j in range(128):
        dm[pi[j], j] = 1.0
    d["deint"] = dm
    sel2 = np.zeros((128, 2), dtype=np.float32)
    sel2[0:64, 0] = 1.0
    sel2[64:128, 1] = 1.0
    d["sel2"] = sel2
    selB = np.zeros((128, 128), dtype=np.float32)
    selB[0, 0:64] = 1.0
    selB[1, 64:128] = 1.0
    d["selB"] = selB
    sel8 = np.zeros((128, 1024), dtype=np.float32)
    for e in range(8):
        sel8[e, e * 128:(e + 1) * 128] = 1.0
    d["sel8"] = sel8
    return d


def prep_core_inputs(b, I):
    m = {}
    m["xT"] = np.ascontiguousarray(I["L0_attn_x"][b].T)
    m["l0a_qw"] = pack_lhsT(I["L0_attn_Qw"])
    m["l0a_kw"] = pack_lhsT(I["L0_attn_Kw"])
    m["l0a_vw"] = pack_rhs(I["L0_attn_Vw"])
    for l in range(2):
        wqkv = I["L0_ffn_wqkv"][l]
        m[f"l0f{l}_wq"] = pack_lhsT(wqkv[:, 0:512])
        m[f"l0f{l}_wk"] = pack_lhsT(wqkv[:, 512:1024])
        m[f"l0f{l}_wv"] = pack_rhs(wqkv[:, 1024:1536])
        m[f"l0f{l}_wo"] = pack_lhsT(I["L0_ffn_wo"][l])
        m[f"l0f{l}_qs"] = np.ascontiguousarray(
            np.tile(I["L0_ffn_q_scale"][l], 2).reshape(128, 1))
        m[f"l0f{l}_ks"] = np.ascontiguousarray(
            np.tile(I["L0_ffn_k_scale"][l], 2).reshape(128, 1))
        m[f"l0f{l}_qsr"] = np.ascontiguousarray(np.pad(
            I["L0_ffn_q_scale"][l].reshape(1, 64), ((0, 127), (0, 0))))
        m[f"l0f{l}_ksr"] = np.ascontiguousarray(np.pad(
            I["L0_ffn_k_scale"][l].reshape(1, 64), ((0, 127), (0, 0))))
        m[f"l0f{l}_lng"] = pack_cols(I["L0_ffn_ln_g"][l])
        m[f"l0f{l}_lnb"] = pack_cols(I["L0_ffn_ln_b"][l])
        w1 = I["L0_ffn_w1"][l]
        b1 = I["L0_ffn_b1"][l]
        w1p = np.zeros((1024, 5632), dtype=np.float32)
        b1p = np.zeros((5632,), dtype=np.float32)
        w1p[:, 0:2730] = w1[:, 0:2730]
        w1p[:, 2816:5546] = w1[:, 2730:5460]
        b1p[0:2730] = b1[0:2730]
        b1p[2816:5546] = b1[2730:5460]
        m[f"l0f{l}_w1"] = pack_lhsT(w1p)
        m[f"l0f{l}_b1"] = pack_cols(b1p)
        w2p = np.zeros((2816, 1024), dtype=np.float32)
        w2p[0:2730] = I["L0_ffn_w2"][l]
        m[f"l0f{l}_w2"] = pack_lhsT(w2p)
        m[f"l0f{l}_b2"] = pack_cols(I["L0_ffn_b2"][l])
    pi = deinterleave_perm(1024)
    m["l1a_wqa"] = pack_lhsT(I["L1_attn_Wqa"])
    m["l1a_wqb"] = pack_lhsT(I["L1_attn_Wqb"][:, pi])
    m["l1a_wkva"] = pack_lhsT(I["L1_attn_Wkva"])
    kvb = I["L1_attn_Wkvb"]
    kperm = np.zeros(256, dtype=np.int64)
    vperm = np.zeros(256, dtype=np.int64)
    for h in range(4):
        for j in range(64):
            dd = 2 * j if j < 32 else 2 * (j - 32) + 1
            kperm[64 * h + j] = 128 * h + 2 * dd
            vperm[64 * h + j] = 128 * h + 2 * j + 1
    dup = np.concatenate([np.tile(np.arange(64 * g, 64 * (g + 1)), 2)
                          for g in range(4)])
    m["l1a_wk"] = np.ascontiguousarray(np.pad(kvb[:, kperm][:, dup],
                                              ((0, 96), (0, 0))))
    m["l1a_wv"] = np.ascontiguousarray(np.pad(kvb[:, vperm],
                                              ((0, 96), (0, 0))))
    m["l1a_wout"] = pack_lhsT(I["L1_attn_Wout"])
    m["l1a_bout"] = pack_cols(I["L1_attn_bout"])
    ipw = I["L1_ffn_in_proj_w"]
    ipb = I["L1_ffn_in_proj_b"]
    Wq, Wk, Wv = ipw[0:1024], ipw[1024:2048], ipw[2048:3072]
    m["l1f_wq"] = pack_lhsT(np.ascontiguousarray(Wq.T[pi]))
    m["l1f_wk"] = pack_lhsT(np.ascontiguousarray(Wk.T[pi]))
    m["l1f_wv"] = pack_rhs(np.ascontiguousarray(Wv.T))
    m["l1f_bq"] = pack_cols(ipb[0:1024])
    m["l1f_bk"] = pack_cols(ipb[1024:2048])
    m["l1f_bv"] = np.ascontiguousarray(np.pad(ipb[2048:3072].reshape(1, 1024),
                                              ((0, 127), (0, 0))))
    m["l1f_wo"] = pack_lhsT(np.ascontiguousarray(I["L1_ffn_out_proj_w"].T))
    m["l1f_bo"] = pack_cols(I["L1_ffn_out_proj_b"])
    m["l1f_gw"] = np.ascontiguousarray(
        I["L1_ffn_gate_w"].T.reshape(8, 128, 8))
    m["l1f_gb"] = np.ascontiguousarray(
        np.pad(I["L1_ffn_gate_b"], (0, 120)).reshape(128, 1))
    ew = I["L1_ffn_expert_w"]
    m["l1f_ew"] = np.stack([pack_lhsT(np.ascontiguousarray(ew[e].T))
                            for e in range(8)]).reshape(64, 128, 1024)
    m["l1f_eb"] = np.concatenate([pack_cols(I["L1_ffn_expert_b"][e])
                                  for e in range(8)], axis=1)
    m["l1f_ln1w"] = pack_cols(I["L1_ffn_ln1_w"])
    m["l1f_ln1b"] = pack_cols(I["L1_ffn_ln1_b"])
    m["l1f_ln2w"] = pack_cols(I["L1_ffn_ln2_w"])
    m["l1f_ln2b"] = pack_cols(I["L1_ffn_ln2_b"])
    return m



class Tracker:

    def __init__(self, nc, stack):
        self.nc = nc
        self.eng = {
            "pe": nc.tensor, "act": nc.scalar, "dve": nc.vector,
            "pool": nc.gpsimd, "sq": nc.sync, "gq": nc.gpsimd,
        }
        self.step = {"pe": 1, "act": 1, "dve": 1, "pool": 1, "sq": 16, "gq": 16}
        self.sem = {}
        self.cum = {}
        self.instrs = {}
        self.incs = {}
        self.waited = {}
        for e in self.eng:
            self.sem[e] = stack.enter_context(nc.semaphore("s_" + e))
            self.cum[e] = 0
            self.instrs[e] = []
            self.incs[e] = []
        self.writes = {}
        self.reads = {}

    def _need(self, e, idx):
        if e in ("sq", "gq"):
            return self.cum[e]
        arr = self.incs[e]
        import bisect
        pos = bisect.bisect_left(arr, (idx, -1))
        if pos < len(arr):
            return arr[pos][1]
        self.cum[e] += self.step[e]
        v = self.cum[e]
        self.instrs[e][idx].then_inc(self.sem[e], self.step[e])
        arr.append((idx, v))
        return v

    def _edges(self, regions, kinds):
        deps = []
        for (key, lo, hi) in regions:
            for store in kinds:
                recs = store.get(key)
                if not recs:
                    continue
                for (l2, h2, e2, i2) in recs:
                    if l2 < hi and lo < h2:
                        deps.append((e2, i2))
        return deps

    def emit(self, e, thunk, reads=(), writes=()):
        deps = []
        deps += self._edges(reads, (self.writes,))
        deps += self._edges(writes, (self.writes, self.reads))
        byeng = {}
        for (e2, i2) in deps:
            if e2 == e and e in ("pe", "sq", "gq"):
                continue
            if e2 == e and i2 == len(self.instrs[e]) - 0:
                continue
            if byeng.get(e2, -1) < i2:
                byeng[e2] = i2
        for e2, i2 in sorted(byeng.items()):
            v = self._need(e2, i2)
            k = (e, e2)
            if self.waited.get(k, -1) >= v:
                continue
            self.eng[e].wait_ge(self.sem[e2], v)
            self.waited[k] = v
        instr = thunk()
        idx = len(self.instrs[e])
        self.instrs[e].append(instr)
        if e in ("sq", "gq"):
            self.cum[e] += self.step[e]
            instr.then_inc(self.sem[e], self.step[e])
            self.incs[e].append((idx, self.cum[e]))
        for (key, lo, hi) in writes:
            wl = self.writes.setdefault(key, [])
            out = []
            for r in wl:
                if r[0] < hi and lo < r[1]:
                    if r[0] < lo:
                        out.append((r[0], lo, r[2], r[3]))
                    if r[1] > hi:
                        out.append((hi, r[1], r[2], r[3]))
                else:
                    out.append(r)
            out.append((lo, hi, e, idx))
            self.writes[key] = out
            rl = self.reads.get(key)
            if rl:
                self.reads[key] = [r for r in rl if not (r[0] < hi and lo < r[1])]
        for (key, lo, hi) in reads:
            self.reads.setdefault(key, []).append((lo, hi, e, idx))
        return instr

    def fin(self):
        for e in ("sq", "gq"):
            if self.cum[e] < self.step[e] * len(self.instrs[e]):
                if self.instrs[e]:
                    self._need(e, len(self.instrs[e]) - 1)
        for e in ("sq", "gq"):
            if self.cum[e] > 0:
                self.eng[e].wait_ge(self.sem[e], self.cum[e])


class Ctx:
    def __init__(self, nc, stack):
        self.nc = nc
        self.stack = stack
        self.tk = Tracker(nc, stack)
        self.dram = {}
        self.uid = 0

    def sb(self, name, shape, dtype):
        t = self.stack.enter_context(self.nc.sbuf_tensor(name, shape, dtype))
        return t

    def ps(self, name, shape=(128, 512), dtype=F32):
        return self.stack.enter_context(self.nc.psum_tensor(name, list(shape), dtype))

    def name(self, base):
        self.uid += 1
        return f"{base}_{self.uid}"


def R(buf, lo=0, hi=1 << 30):
    return (id(buf), lo, hi)



def load_consts(c: Ctx, which=()):
    tk, nc = c.tk, c.nc
    d = {}
    for v in (1e-5, 1e-24):
        t = c.sb(c.name("constb"), [128, 1], F32)
        tk.emit("dve", lambda t=t, v=v: nc.vector.memset(t[:, :], float(v)),
                writes=[R(t)])
        nc.const_aps.register(F32, float(v), t[:, :])
    ones = c.sb("onesbuf", [128, 8], F32)
    tk.emit("dve", lambda: nc.vector.memset(ones[:, 0:4], 1.0), writes=[R(ones, 0, 4)])
    d["ones_col_f"] = ones[:, 0:1]
    onesb = c.sb("onesbufb", [128, 8], BF16)
    tk.emit("dve", lambda: nc.vector.memset(onesb[:, 0:4], 1.0), writes=[R(onesb, 0, 4)])
    d["ones_col_b"] = onesb[:, 0:1]
    onesr = c.sb("onesrow", [1, 128], F32)
    tk.emit("dve", lambda: nc.vector.memset(onesr[:, :], 1.0), writes=[R(onesr)])
    d["ones_row_f"] = onesr[0:1, :]
    onesrb = c.sb("onesrowb", [1, 128], BF16)
    tk.emit("dve", lambda: nc.vector.memset(onesrb[:, :], 1.0), writes=[R(onesrb)])
    d["ones_row_b"] = onesrb[0:1, :]
    if "maskc4" in which:
        mc4 = c.sb("maskc4_s", [128, 4 * 512], BF16)
        for j in range(4):
            tk.emit("sq", lambda j=j: nc.sync.dma_start(
                out=mc4[:, j * 512:(j + 1) * 512], in_=c.dram["maskc4"][j]),
                writes=[R(mc4, j * 512, (j + 1) * 512)])
        d["maskc4"] = mc4
    if "mask8" in which:
        m8 = c.sb("mask8_s", [128, 8 * 512], BF16)
        for j in range(8):
            tk.emit("sq", lambda j=j: nc.sync.dma_start(
                out=m8[:, j * 512:(j + 1) * 512], in_=c.dram["mask8"][j]),
                writes=[R(m8, j * 512, (j + 1) * 512)])
        d["mask8"] = m8
    if "deintb" in which:
        deintb = c.sb("deintb_s", [128, 128], BF16)
        tk.emit("gq", lambda: nc.gpsimd.dma_start(
            out=deintb[:, :], in_=c.dram["deint"][:, :]), writes=[R(deintb)])
        d["deintb"] = deintb
    if "sel2" in which:
        sel2 = c.sb("sel2_s", [128, 2], BF16)
        tk.emit("gq", lambda: nc.gpsimd.dma_start(
            out=sel2[:, :], in_=c.dram["sel2"][:, :]), writes=[R(sel2)])
        d["sel2"] = sel2
        selB = c.sb("selB_s", [128, 128], BF16)
        tk.emit("gq", lambda: nc.gpsimd.dma_start(
            out=selB[:, :], in_=c.dram["selB"][:, :]), writes=[R(selB)])
        d["selB"] = selB
    return d


class Rows:
    def __init__(self, c, name, ncols):
        self.t = c.sb(name, [128, ncols], F32)
        self.ncols = ncols

    def slot(self, base, col, w=512):
        return self.t[base:base + 1, col:col + w], R(self.t, col, col + w)


class LNState:
    def __init__(self, c, psum_pool):
        self.rows = Rows(c, c.name("lnrows"), 1024)
        self.sq = [c.sb(c.name("lnsq"), [128, 512], BF16) for _ in range(2)]
        self.tmp = [c.sb(c.name("lntmp"), [128, 512], F32) for _ in range(4)]
        self.muB = c.sb(c.name("lnmuB"), [128, 512], F32)
        self.rB = c.sb(c.name("lnrB"), [128, 512], F32)
        self.ps = psum_pool


def ln_tb(c, C, L, src, tb, dsts, gamma=None, beta=None, eps=1e-5, src_tb=None):
    tk, nc = c.tk, c.nc
    st = tb if src_tb is None else src_tb
    srcf32 = src[0].dtype == F32
    psA, psB, psC = L.ps[0], L.ps[1], L.ps[2]
    onescol = C["ones_col_f"] if srcf32 else C["ones_col_b"]
    onesb = C["ones_col_b"]
    onesrow = C["ones_row_f"]
    for ci in range(8):
        s = src[ci]
        tk.emit("pe", lambda s=s, ci=ci: nc.tensor.matmul(
            psA[0:1, 0:512], onescol, s[:, st:st + 512],
            start=(ci == 0), stop=(ci == 7)),
            reads=[R(s, st, st + 512)], writes=[R(psA, 0, 512)])
        sq = L.sq[ci % 2]
        tk.emit("act", lambda s=s, sq=sq: nc.scalar.activation(
            sq[:, :], s[:, st:st + 512], AF.Square),
            reads=[R(s, st, st + 512)], writes=[R(sq)])
        tk.emit("pe", lambda sq=sq, ci=ci: nc.tensor.matmul(
            psB[0:1, 0:512], onesb, sq[:, :],
            start=(ci == 0), stop=(ci == 7)),
            reads=[R(sq)], writes=[R(psB, 0, 512)])
    mu, rmu = L.rows.slot(0, 0)
    msq, rmsq = L.rows.slot(32, 0)
    var, rvar = L.rows.slot(0, 512)
    tmq, rtmq = L.rows.slot(32, 512)
    tk.emit("act", lambda: nc.scalar.mul(mu, psA[0:1, 0:512], 1.0 / 1024.0),
            reads=[R(psA, 0, 512)], writes=[rmu])
    tk.emit("act", lambda: nc.scalar.mul(msq, psB[0:1, 0:512], 1.0 / 1024.0),
            reads=[R(psB, 0, 512)], writes=[rmsq])
    tk.emit("dve", lambda: nc.vector.tensor_mul(tmq, mu, mu),
            reads=[rmu], writes=[rtmq])
    tk.emit("dve", lambda: nc.vector.tensor_sub(var, msq, tmq),
            reads=[rmsq, rtmq], writes=[rvar])
    tk.emit("act", lambda: nc.scalar.activation(var, var, AF.Ln, bias=float(eps)),
            reads=[rvar], writes=[rvar])
    tk.emit("act", lambda: nc.scalar.activation(var, var, AF.Exp, scale=-0.5),
            reads=[rvar], writes=[rvar])
    tk.emit("pe", lambda: nc.tensor.matmul(psC[:, 0:512], onesrow, mu,
                                           start=True, stop=True),
            reads=[rmu], writes=[R(psC, 0, 512)])
    tk.emit("dve", lambda: nc.vector.tensor_copy(L.muB[:, :], psC[:, 0:512]),
            reads=[R(psC, 0, 512)], writes=[R(L.muB)])
    tk.emit("pe", lambda: nc.tensor.matmul(psC[:, 0:512], onesrow, var,
                                           start=True, stop=True),
            reads=[rvar], writes=[R(psC, 0, 512)])
    tk.emit("dve", lambda: nc.vector.tensor_copy(L.rB[:, :], psC[:, 0:512]),
            reads=[R(psC, 0, 512)], writes=[R(L.rB)])
    for ci in range(8):
        s = src[ci]
        e = "dve" if (ci % 2 == 0) else "pool"
        eng = nc.vector if (ci % 2 == 0) else nc.gpsimd
        t1 = L.tmp[ci % 2]
        tk.emit(e, lambda s=s, t1=t1, eng=eng: eng.tensor_sub(
            t1[:, :], s[:, st:st + 512], L.muB[:, :]),
            reads=[R(s, st, st + 512), R(L.muB)], writes=[R(t1)])
        if gamma is None:
            for dst in dsts:
                dd = dst[ci]
                tk.emit(e, lambda t1=t1, dd=dd, eng=eng: eng.tensor_mul(
                    dd[:, tb:tb + 512], t1[:, :], L.rB[:, :]),
                    reads=[R(t1), R(L.rB)], writes=[R(dd, tb, tb + 512)])
        else:
            t2 = L.tmp[2 + ci % 2]
            tk.emit(e, lambda t1=t1, t2=t2, eng=eng: eng.tensor_mul(
                t2[:, :], t1[:, :], L.rB[:, :]),
                reads=[R(t1), R(L.rB)], writes=[R(t2)])
            for dst in dsts:
                dd = dst[ci]
                tk.emit(e, lambda t2=t2, dd=dd, ci=ci, eng=eng: eng.tensor_scalar(
                    dd[:, tb:tb + 512], t2[:, :], gamma[:, ci:ci + 1],
                    beta[:, ci:ci + 1], ALU.mult, ALU.add),
                    reads=[R(t2), R(gamma, ci, ci + 1), R(beta, ci, ci + 1)],
                    writes=[R(dd, tb, tb + 512)])



class WStream:
    def __init__(self, c, dram_name, n_oc, P, K, nbuf=2, base=0):
        self.c = c
        self.src = c.dram[dram_name]
        self.n_oc = n_oc
        self.P, self.K = P, K
        self.nbuf = nbuf
        self.base = base
        self.bufs = [c.sb(c.name(dram_name + "_wb"), [P, K], BF16)
                     for _ in range(nbuf)]
        self.fetched = -1

    def fetch(self, oc):
        if oc >= self.n_oc or oc <= self.fetched:
            return
        self.fetched = oc
        b = self.bufs[oc % self.nbuf]
        tk, nc = self.c.tk, self.c.nc
        tk.emit("gq", lambda: nc.gpsimd.dma_start(out=b[:, :],
                in_=self.src[self.base + oc]),
                writes=[R(b)])

    def get(self, oc):
        return self.bufs[oc % self.nbuf], R(self.bufs[oc % self.nbuf])


def load_pack_resident(c, dram_name, n_oc, P, K):
    tk, nc = c.tk, c.nc
    out = []
    for oc in range(n_oc):
        t = c.sb(c.name(dram_name + "_r"), [P, K], BF16)
        tk.emit("gq", lambda t=t, oc=oc: nc.gpsimd.dma_start(
            out=t[:, :], in_=c.dram[dram_name][oc]), writes=[R(t)])
        out.append(t)
    return out


def load_f32(c, dram_name, shape, qn="sq"):
    t = c.sb(c.name(dram_name + "_f"), list(shape), F32)
    tk, nc = c.tk, c.nc
    src = c.dram[dram_name]
    tk.emit(qn, lambda: tk.eng[qn].dma_start(out=t[:, :], in_=src[:, :]),
            writes=[R(t)])
    return t


def load_chunks_f32(c, dram_name, dst=None, cast_bf16=None):
    tk, nc = c.tk, c.nc
    src = c.dram[dram_name]
    if dst is not None:
        for ci in range(8):
            tk.emit("sq", lambda ci=ci: nc.sync.dma_start(
                out=dst[ci][:, :], in_=src[ci * 128:(ci + 1) * 128, :]),
                writes=[R(dst[ci])])
    if cast_bf16 is not None:
        for ci in range(8):
            tk.emit("gq", lambda ci=ci: nc.gpsimd.dma_start(
                out=cast_bf16[ci][:, :], in_=src[ci * 128:(ci + 1) * 128, :]),
                writes=[R(cast_bf16[ci])])


def store_chunks_f32(c, dram_name, src, tbs=None):
    tk, nc = c.tk, c.nc
    dst = c.dram[dram_name]
    for ci in range(8):
        tk.emit("sq", lambda ci=ci: nc.sync.dma_start(
            out=dst[ci * 128:(ci + 1) * 128, :], in_=src[ci][:, :]),
            reads=[R(src[ci])])


def mk_chunks(c, base, dtype, n=8, F=T):
    return [c.sb(c.name(base), [128, F], dtype) for _ in range(n)]



def negc_from_norms(c, C, ps, rows, q_tiles, k_tiles, scale, sqtmp):
    tk, nc = c.tk, c.nc
    psN = ps[0]
    acc, racc = rows.slot(0, 512, 8)
    for (j, tiles) in ((0, q_tiles), (1, k_tiles)):
        for itb, tb in enumerate(TBS):
            for i, qt in enumerate(tiles):
                sq = sqtmp[i % 2]
                tk.emit("act", lambda qt=qt, sq=sq, tb=tb: nc.scalar.activation(
                    sq[:, :], qt[:, tb:tb + 512], AF.Square),
                    reads=[R(qt, tb, tb + 512)], writes=[R(sq)])
                tk.emit("pe", lambda sq=sq, i=i, n=len(tiles): nc.tensor.matmul(
                    psN[0:1, 0:512], C["ones_col_b"], sq[:, :],
                    start=(i == 0), stop=(i == n - 1)),
                    reads=[R(sq)], writes=[R(psN, 0, 512)])
            col = j * 4 + itb
            tk.emit("dve", lambda col=col: nc.vector.reduce_max(
                acc[0:1, col - j * 4 + j * 4:col + 1] if False else acc[0:1, col:col + 1],
                psN[0:1, 0:512], axis=AX.X),
                reads=[R(psN, 0, 512)], writes=[(racc[0], racc[1] + col, racc[1] + col + 1)])
    mq, rmq = rows.slot(0, 520, 1)
    mk, rmk = rows.slot(0, 521, 1)
    cc, rcc = rows.slot(0, 522, 1)
    tk.emit("dve", lambda: nc.vector.reduce_max(mq, acc[0:1, 0:4], axis=AX.X),
            reads=[(racc[0], racc[1], racc[1] + 4)], writes=[rmq])
    tk.emit("dve", lambda: nc.vector.reduce_max(mk, acc[0:1, 4:8], axis=AX.X),
            reads=[(racc[0], racc[1] + 4, racc[1] + 8)], writes=[rmk])
    tk.emit("dve", lambda: nc.vector.tensor_mul(cc, mq, mk),
            reads=[rmq, rmk], writes=[rcc])
    tk.emit("act", lambda: nc.scalar.activation(cc, cc, AF.Ln),
            reads=[rcc], writes=[rcc])
    tk.emit("act", lambda: nc.scalar.activation(cc, cc, AF.Exp, scale=0.5),
            reads=[rcc], writes=[rcc])
    negc = c.sb(c.name("negc"), [128, 1], F32)
    tk.emit("pe", lambda: nc.tensor.matmul(psN[0:128, 0:1], C["ones_row_f"],
                                           cc, start=True, stop=True),
            reads=[rcc], writes=[R(psN, 0, 1)])
    tk.emit("act", lambda: nc.scalar.mul(negc[:, :], psN[0:128, 0:1], -float(scale)),
            reads=[R(psN, 0, 1)], writes=[R(negc)])
    return negc



def attn64(c, C, cfg):
    tk, nc = c.tk, c.nc
    negc = cfg["negc"]
    for qb in range(4):
        ktl = cfg["ktlist"](qb)
        for h in range(cfg["nh"]):
            qt, q_base = cfg["qmap"](h)
            ktn, k_base = cfg["kmap"](h)
            psO = cfg["psO"][h % len(cfg["psO"])]
            nkt = len(ktl)
            for i, (kt, mask) in enumerate(ktl):
                psS = cfg["psS"][i % 2]
                tk.emit("pe", lambda psS=psS, kt=kt, ktn=ktn, k_base=k_base,
                        qt=qt, q_base=q_base, qb=qb: nc.tensor.matmul(
                    psS[:, 0:512],
                    ktn[k_base:k_base + 64, kt * 128:(kt + 1) * 128],
                    qt[q_base:q_base + 64, qb * 512:(qb + 1) * 512],
                    start=True, stop=True),
                    reads=[R(ktn, kt * 128, (kt + 1) * 128),
                           R(qt, qb * 512, (qb + 1) * 512)],
                    writes=[R(psS, 0, 512)])
                E = cfg["ebuf"][i % 2]
                tk.emit("act", lambda psS=psS, E=E: nc.scalar.activation(
                    E[:, :], psS[:, 0:512], AF.Exp, bias=negc[:, :],
                    scale=float(cfg["scale"])),
                    reads=[R(psS, 0, 512), R(negc)], writes=[R(E)])
                if mask is not None:
                    mt, mlo, mhi = mask
                    tk.emit("dve", lambda E=E, mt=mt, mlo=mlo: nc.vector.tensor_mul(
                        E[:, :], E[:, :], mt[:, mlo:mlo + 512]),
                        reads=[R(E), (id(mt), mlo, mhi)], writes=[R(E)])
                vap, vreg = cfg["vsl"](kt, h)
                tk.emit("pe", lambda vap=vap, E=E, i=i, psO=psO, nkt=nkt: nc.tensor.matmul(
                    psO[0:65, 0:512], vap, E[:, :],
                    start=(i == 0), stop=(i == nkt - 1)),
                    reads=[vreg, R(E)], writes=[R(psO, 0, 512)])
            arow, rarow = cfg["rows"].slot(0, 0)
            tk.emit("dve", lambda psO=psO, arow=arow: nc.vector.reciprocal(
                arow, psO[64:65, 0:512]),
                reads=[R(psO, 0, 512)], writes=[rarow])
            psA = cfg["psA"]
            tk.emit("pe", lambda arow=arow: nc.tensor.matmul(
                psA[0:64, 0:512], C["ones_row_f"][0:1, 0:64], arow,
                start=True, stop=True),
                reads=[rarow], writes=[R(psA, 0, 512)])
            aB = cfg["aB"][h % 2]
            tk.emit("act", lambda aB=aB: nc.scalar.copy(aB[0:64, :], psA[0:64, 0:512]),
                    reads=[R(psA, 0, 512)], writes=[R(aB)])
            cfg["out"](h, qb, psO, aB)



def stage_ln0(c, C):
    tk, nc = c.tk, c.nc
    x = mk_chunks(c, "x", F32)
    load_chunks_f32(c, "xT", dst=x)
    xn = mk_chunks(c, "xn", F32)
    ps = [c.ps(c.name("psln")) for _ in range(3)]
    L = LNState(c, ps)
    for tb in TBS:
        ln_tb(c, C, L, x, tb, [xn])
    store_chunks_f32(c, "xnT", xn)


def stage_l0_attn(c, C):
    tk, nc = c.tk, c.nc
    xnb = mk_chunks(c, "xnb", BF16)
    load_chunks_f32(c, "xnT", cast_bf16=xnb)
    qw = WStream(c, "l0a_qw", 8, 128, 1024, nbuf=2)
    kw = WStream(c, "l0a_kw", 8, 128, 1024, nbuf=2)
    vw = load_pack_resident(c, "l0a_vw", 8, 128, 1024)
    qT = mk_chunks(c, "l0a_qT", BF16)
    kT = mk_chunks(c, "l0a_kT", BF16)
    vtok = [c.sb(c.name("l0a_vtok"), [128, 1024], BF16) for _ in range(16)]
    ps = [c.ps(c.name("ps")) for _ in range(7)]
    rows = Rows(c, "l0a_rows", 1024)
    sqt = [c.sb(c.name("sqt"), [128, 512], BF16) for _ in range(2)]
    for (W, dst) in ((qw, qT), (kw, kT)):
        W.fetch(0); W.fetch(1)
        for oc in range(8):
            wb, wreg = W.get(oc)
            for itb, tb in enumerate(TBS):
                psq = ps[itb % 2]
                for dc in range(8):
                    tk.emit("pe", lambda wb=wb, dc=dc, tb=tb, psq=psq: nc.tensor.matmul(
                        psq[:, 0:512], wb[:, dc * 128:(dc + 1) * 128],
                        xnb[dc][:, tb:tb + 512], start=(dc == 0), stop=(dc == 7)),
                        reads=[(id(wb), dc * 128, (dc + 1) * 128), R(xnb[dc], tb, tb + 512)],
                        writes=[R(psq, 0, 512)])
                tk.emit("act", lambda psq=psq, dst=dst, oc=oc, tb=tb: nc.scalar.copy(
                    dst[oc][:, tb:tb + 512], psq[:, 0:512]),
                    reads=[R(psq, 0, 512)], writes=[R(dst[oc], tb, tb + 512)])
            W.fetch(oc + 2)
    for tt in range(16):
        for vh in range(2):
            psv = ps[2 + (tt * 2 + vh) % 2]
            for dc in range(8):
                tk.emit("pe", lambda tt=tt, vh=vh, dc=dc, psv=psv: nc.tensor.matmul(
                    psv[:, 0:512], xnb[dc][:, tt * 128:(tt + 1) * 128],
                    vw[dc][:, vh * 512:(vh + 1) * 512],
                    start=(dc == 0), stop=(dc == 7)),
                    reads=[R(xnb[dc], tt * 128, (tt + 1) * 128),
                           (id(vw[dc]), vh * 512, (vh + 1) * 512)],
                    writes=[R(psv, 0, 512)])
            tk.emit("act", lambda psv=psv, tt=tt, vh=vh: nc.scalar.copy(
                vtok[tt][:, vh * 512:(vh + 1) * 512], psv[:, 0:512]),
                reads=[R(psv, 0, 512)], writes=[R(vtok[tt], vh * 512, (vh + 1) * 512)])
    negc = negc_from_norms(c, C, [ps[4]], rows, qT, kT, 1.0 / 32.0, sqt)
    if "DBGT" in c.dram:
        tk.emit("sq", lambda: nc.sync.dma_start(
            out=c.dram["DBGT"][0:128, 0:1], in_=negc[:, :],
            allow_slow_non_contiguous=True), reads=[R(negc)])
        tk.emit("gq", lambda: nc.gpsimd.dma_start(
            out=c.dram["DBGT"][128:256, 0:2048], in_=qT[0][:, :]), reads=[R(qT[0])])
        tk.emit("gq", lambda: nc.gpsimd.dma_start(
            out=c.dram["DBGT"][256:384, 0:2048], in_=kT[0][:, :]), reads=[R(kT[0])])
        tk.emit("gq", lambda: nc.gpsimd.dma_start(
            out=c.dram["DBGT"][384:512, 0:1024], in_=vtok[0][:, :]), reads=[R(vtok[0])])
        tk.emit("sq", lambda: nc.sync.dma_start(
            out=c.dram["DBGT"][768:896, 0:1024], in_=rows.t[:, :]),
            reads=[R(rows.t)])
    ebuf = c.sb("l0a_ebuf", [128, 16 * 512], BF16)
    aB = c.sb("l0a_aB", [128, 512], F32)
    a0t = [c.sb(c.name("a0t"), [128, 512], F32) for _ in range(2)]
    for qb in range(4):
        nkt = 4 * qb + 4
        for kt in range(nkt):
            psS = ps[kt % 2]
            for dc in range(8):
                tk.emit("pe", lambda kt=kt, dc=dc, qb=qb, psS=psS: nc.tensor.matmul(
                    psS[:, 0:512], kT[dc][:, kt * 128:(kt + 1) * 128],
                    qT[dc][:, qb * 512:(qb + 1) * 512],
                    start=(dc == 0), stop=(dc == 7)),
                    reads=[R(kT[dc], kt * 128, (kt + 1) * 128),
                           R(qT[dc], qb * 512, (qb + 1) * 512)],
                    writes=[R(psS, 0, 512)])
            tk.emit("act", lambda psS=psS, kt=kt: nc.scalar.activation(
                ebuf[:, kt * 512:(kt + 1) * 512], psS[:, 0:512], AF.Exp,
                bias=negc[:, :], scale=1.0 / 32.0),
                reads=[R(psS, 0, 512), R(negc)],
                writes=[R(ebuf, kt * 512, (kt + 1) * 512)])
            j = kt - 4 * qb
            if j >= 0:
                tk.emit("dve", lambda kt=kt, j=j: nc.vector.tensor_mul(
                    ebuf[:, kt * 512:(kt + 1) * 512],
                    ebuf[:, kt * 512:(kt + 1) * 512],
                    C["maskc4"][:, j * 512:(j + 1) * 512]),
                    reads=[R(ebuf, kt * 512, (kt + 1) * 512),
                           R(C["maskc4"], j * 512, (j + 1) * 512)],
                    writes=[R(ebuf, kt * 512, (kt + 1) * 512)])
            tk.emit("pe", lambda kt=kt: nc.tensor.matmul(
                ps[4][0:1, 0:512], C["ones_col_b"],
                ebuf[:, kt * 512:(kt + 1) * 512],
                start=(kt == 0), stop=(kt == nkt - 1)),
                reads=[R(ebuf, kt * 512, (kt + 1) * 512)],
                writes=[R(ps[4], 0, 512)])
        arow, rarow = rows.slot(0, 0)
        if "DBGT" in c.dram and qb == 0:
            tk.emit("gq", lambda: nc.gpsimd.dma_start(
                out=c.dram["DBGT"][512:640, 0:2048], in_=ebuf[:, 0:2048]),
                reads=[R(ebuf, 0, 2048)])
            lrow = c.sb("dbglrow", [1, 512], F32)
            tk.emit("dve", lambda: nc.vector.tensor_copy(lrow[:, :], ps[4][0:1, 0:512]),
                    reads=[R(ps[4], 0, 512)], writes=[R(lrow)])
            tk.emit("sq", lambda: nc.sync.dma_start(
                out=c.dram["DBGT"][640:641, 0:512], in_=lrow[:, :]), reads=[R(lrow)])
        tk.emit("dve", lambda: nc.vector.reciprocal(arow, ps[4][0:1, 0:512]),
                reads=[R(ps[4], 0, 512)], writes=[rarow])
        tk.emit("pe", lambda: nc.tensor.matmul(
            ps[5][:, 0:512], C["ones_row_f"], arow, start=True, stop=True),
            reads=[rarow], writes=[R(ps[5], 0, 512)])
        tk.emit("act", lambda: nc.scalar.copy(aB[:, :], ps[5][:, 0:512]),
                reads=[R(ps[5], 0, 512)], writes=[R(aB)])
        for dvc in range(8):
            psO = ps[2 + dvc % 2]
            for kt in range(nkt):
                tk.emit("pe", lambda dvc=dvc, kt=kt, psO=psO: nc.tensor.matmul(
                    psO[:, 0:512], vtok[kt][:, dvc * 128:(dvc + 1) * 128],
                    ebuf[:, kt * 512:(kt + 1) * 512],
                    start=(kt == 0), stop=(kt == nkt - 1)),
                    reads=[R(vtok[kt], dvc * 128, (dvc + 1) * 128),
                           R(ebuf, kt * 512, (kt + 1) * 512)],
                    writes=[R(psO, 0, 512)])
            a0 = a0t[dvc % 2]
            tk.emit("dve", lambda psO=psO, a0=a0: nc.vector.tensor_mul(
                a0[:, :], psO[:, 0:512], aB[:, :]),
                reads=[R(psO, 0, 512), R(aB)], writes=[R(a0)])
            tk.emit("sq", lambda a0=a0, dvc=dvc, qb=qb: nc.sync.dma_start(
                out=c.dram["A0T"][dvc * 128:(dvc + 1) * 128, qb * 512:(qb + 1) * 512],
                in_=a0[:, :]),
                reads=[R(a0)])



def _tracker_barrier(self):
    vals = {}
    for e in self.eng:
        if self.instrs[e]:
            vals[e] = self._need(e, len(self.instrs[e]) - 1)
    for f in self.eng:
        for e, v in vals.items():
            if e == f:
                continue
            k = (f, e)
            if self.waited.get(k, -1) >= v:
                continue
            self.eng[f].wait_ge(self.sem[e], v)
            self.waited[k] = v

Tracker.barrier = _tracker_barrier

from contextlib import contextmanager

@contextmanager
def scope(c):
    st = ExitStack()
    old = c.stack
    c.stack = st
    try:
        yield
    finally:
        st.close()
        c.stack = old
        c.tk.barrier()


def reload_pack(c, bufs, dram_name, base=0):
    tk, nc = c.tk, c.nc
    for i, t in enumerate(bufs):
        tk.emit("gq", lambda t=t, i=i: nc.gpsimd.dma_start(
            out=t[:, :], in_=c.dram[dram_name][base + i]), writes=[R(t)])


def rope_tile(c, eng, src, s_lo, dst, dst_ap_cols, CS, SS, tb, tmpR):
    tk, nc = c.tk, c.nc
    E = nc.vector if eng == "dve" else nc.gpsimd
    c0, c1 = dst_ap_cols
    for (o, i) in ((0, 32), (32, 0), (64, 96), (96, 64)):
        tk.emit(eng, lambda o=o, i=i: E.tensor_mul(
            tmpR[o:o + 32, 0:512], src[i:i + 32, s_lo:s_lo + 512],
            SS[i:i + 32, tb:tb + 512]),
            reads=[R(src, s_lo, s_lo + 512), R(SS, tb, tb + 512)],
            writes=[R(tmpR, 0, 512)])
    tk.emit(eng, lambda: E.tensor_mul(
        dst[:, c0:c1], src[:, s_lo:s_lo + 512], CS[:, tb:tb + 512]),
        reads=[R(src, s_lo, s_lo + 512), R(CS, tb, tb + 512)],
        writes=[R(dst, c0, c1)])
    tk.emit(eng, lambda: E.tensor_add(dst[:, c0:c1], dst[:, c0:c1], tmpR[:, 0:512]),
            reads=[R(dst, c0, c1), R(tmpR, 0, 512)], writes=[R(dst, c0, c1)])


def shift_cast(c, y, hb, shift_from=4):
    tk, nc = c.tk, c.nc
    for ci in range(8):
        e = "dve" if ci % 2 == 0 else "pool"
        E = nc.vector if ci % 2 == 0 else nc.gpsimd
        if ci < shift_from:
            tk.emit(e, lambda ci=ci, E=E: E.tensor_copy(hb[ci][:, :], y[ci][:, :]),
                    reads=[R(y[ci])], writes=[R(hb[ci])])
        else:
            tk.emit(e, lambda ci=ci, E=E: E.tensor_copy(
                hb[ci][:, 1:2048], y[ci][:, 0:2047]),
                reads=[R(y[ci])], writes=[R(hb[ci])])
            tk.emit(e, lambda ci=ci, E=E: E.memset(hb[ci][:, 0:1], 0.0),
                    reads=[], writes=[R(hb[ci], 0, 1)])


def stage_l0_ffn(c, C):
    tk, nc = c.tk, c.nc
    ps = [c.ps(c.name("ps")) for _ in range(8)]
    y = mk_chunks(c, "y", F32)
    load_chunks_f32(c, "xnT", dst=y)
    rows = Rows(c, "l0f_rows", 1024)
    for l in range(2):
        p = f"l0f{l}_"
        layer_scope = scope(c)
        layer_scope.__enter__()
        qsr = load_f32(c, p + "qsr", [128, 64])
        ksr = load_f32(c, p + "ksr", [128, 64])
        t0, rt0 = rows.slot(0, 0, 64)
        m0, rm0 = rows.slot(0, 100, 1)
        tk.emit("dve", lambda: nc.vector.tensor_mul(t0, qsr[0:1, :], ksr[0:1, :]),
                reads=[R(qsr), R(ksr)], writes=[rt0])
        tk.emit("dve", lambda: nc.vector.reduce_max(
            m0, t0, axis=AX.X, apply_absolute_value=True),
            reads=[rt0], writes=[rm0])
        negc = c.sb(c.name("negc"), [128, 1], F32)
        tk.emit("pe", lambda: nc.tensor.matmul(ps[6][0:128, 0:1], C["ones_row_f"],
                                               m0, start=True, stop=True),
                reads=[rm0], writes=[R(ps[6], 0, 1)])
        tk.emit("act", lambda: nc.scalar.mul(negc[:, :], ps[6][0:128, 0:1], -8.0),
                reads=[R(ps[6], 0, 1)], writes=[R(negc)])
        qs_col = load_f32(c, p + "qs", [128, 1])
        ks_col = load_f32(c, p + "ks", [128, 1])
        qeff = mk_chunks(c, "qeff", BF16, n=4)
        keff = mk_chunks(c, "keff", BF16, n=4)
        vtok = [c.sb(c.name("vtok"), [128, 8, 65], BF16) for _ in range(16)]
        with scope(c):
            CSq = load_f32(c, "l0_CSq", [128, 2048])
            SSq = load_f32(c, "l0_SSq", [128, 2048])
            CSk = load_f32(c, "l0_CSk", [128, 2048])
            SSk = load_f32(c, "l0_SSk", [128, 2048])
            wq = load_pack_resident(c, p + "wq", 4, 128, 1024)
            wk = load_pack_resident(c, p + "wk", 4, 128, 1024)
            wv = load_pack_resident(c, p + "wv", 8, 128, 512)
            hb = mk_chunks(c, "hb", BF16, n=8, F=512)
            qraw = [c.sb(c.name("qraw"), [128, 512], BF16) for _ in range(2)]
            sqt = [c.sb(c.name("sqt"), [128, 512], BF16) for _ in range(2)]
            rln = c.sb(c.name("rln"), [2, 512], BF16)
            rBt = [c.sb(c.name("rBt"), [128, 512], BF16) for _ in range(2)]
            t0b = [c.sb(c.name("t0b"), [128, 512], BF16) for _ in range(2)]
            tmpR = [c.sb(c.name("tmpR"), [128, 512], F32) for _ in range(2)]
            for itb, tb in enumerate(TBS):
                for ci in range(8):
                    e = "dve" if ci % 2 == 0 else "pool"
                    E = nc.vector if ci % 2 == 0 else nc.gpsimd
                    if ci < 4:
                        tk.emit(e, lambda ci=ci, E=E, tb=tb: E.tensor_copy(
                            hb[ci][:, :], y[ci][:, tb:tb + 512]),
                            reads=[R(y[ci], tb, tb + 512)], writes=[R(hb[ci])])
                    elif tb > 0:
                        tk.emit(e, lambda ci=ci, E=E, tb=tb: E.tensor_copy(
                            hb[ci][:, :], y[ci][:, tb - 1:tb + 511]),
                            reads=[R(y[ci], tb - 1, tb + 511)], writes=[R(hb[ci])])
                    else:
                        tk.emit(e, lambda ci=ci, E=E: E.tensor_copy(
                            hb[ci][:, 1:512], y[ci][:, 0:511]),
                            reads=[R(y[ci], 0, 511)], writes=[R(hb[ci])])
                        tk.emit(e, lambda ci=ci, E=E: E.memset(hb[ci][:, 0:1], 0.0),
                                reads=[], writes=[R(hb[ci], 0, 1)])
                for (W, dst, scol, CSx, SSx) in ((wq, qeff, qs_col, CSq, SSq),
                                                 (wk, keff, ks_col, CSk, SSk)):
                    for oc in range(4):
                        ii = oc % 2
                        psq = ps[ii]
                        for dc in range(8):
                            tk.emit("pe", lambda W=W, oc=oc, dc=dc, psq=psq: nc.tensor.matmul(
                                psq[:, 0:512], W[oc][:, dc * 128:(dc + 1) * 128],
                                hb[dc][:, :], start=(dc == 0), stop=(dc == 7)),
                                reads=[(id(W[oc]), dc * 128, (dc + 1) * 128),
                                       R(hb[dc])],
                                writes=[R(psq, 0, 512)])
                        qr = qraw[ii]
                        tk.emit("dve", lambda psq=psq, qr=qr: nc.vector.tensor_copy(
                            qr[:, :], psq[:, 0:512]),
                            reads=[R(psq, 0, 512)], writes=[R(qr)])
                        sq = sqt[ii]
                        tk.emit("dve", lambda qr=qr, sq=sq: nc.vector.tensor_mul(
                            sq[:, :], qr[:, :], qr[:, :]),
                            reads=[R(qr)], writes=[R(sq)])
                        tk.emit("pe", lambda sq=sq: nc.tensor.matmul(
                            ps[2][0:2, 0:512], C["sel2"][:, :], sq[:, :],
                            start=True, stop=True),
                            reads=[R(sq)], writes=[R(ps[2], 0, 512)])
                        tk.emit("act", lambda: nc.scalar.activation(
                            rln[0:2, :], ps[2][0:2, 0:512], AF.Ln, bias=1e-24),
                            reads=[R(ps[2], 0, 512)], writes=[R(rln)])
                        tk.emit("act", lambda: nc.scalar.activation(
                            rln[0:2, :], rln[0:2, :], AF.Exp, scale=-0.5),
                            reads=[R(rln)], writes=[R(rln)])
                        tk.emit("pe", lambda: nc.tensor.matmul(
                            ps[3][:, 0:512], C["selB"][0:2, :], rln[0:2, :],
                            start=True, stop=True),
                            reads=[R(rln)], writes=[R(ps[3], 0, 512)])
                        rb = rBt[ii]
                        tk.emit("dve", lambda rb=rb: nc.vector.tensor_copy(
                            rb[:, :], ps[3][:, 0:512]),
                            reads=[R(ps[3], 0, 512)], writes=[R(rb)])
                        t0x = t0b[ii]
                        tk.emit("dve", lambda qr=qr, t0x=t0x, rb=rb, scol=scol: nc.vector.scalar_tensor_tensor(
                            t0x[:, :], qr[:, :], scol[:, :], rb[:, :],
                            ALU.mult, ALU.mult),
                            reads=[R(qr), R(scol), R(rb)], writes=[R(t0x)])
                        rope_tile(c, "pool" if ii else "dve", t0x, 0,
                                  dst[oc], (tb, tb + 512), CSx, SSx, tb, tmpR[ii])
                for tt in range(4 * itb, 4 * itb + 4):
                    lo = (tt - 4 * itb) * 128
                    tk.emit("dve", lambda tt=tt: nc.vector.memset(vtok[tt][:, :, :], 1.0),
                            writes=[R(vtok[tt])])
                    psv = ps[4 + tt % 2]
                    for dc in range(8):
                        tk.emit("pe", lambda tt=tt, dc=dc, psv=psv, lo=lo: nc.tensor.matmul(
                            psv[:, 0:512], hb[dc][:, lo:lo + 128],
                            wv[dc][:, :], start=(dc == 0), stop=(dc == 7)),
                            reads=[R(hb[dc], lo, lo + 128), (id(wv[dc]), 0, 512)],
                            writes=[R(psv, 0, 512)])
                    tk.emit("dve", lambda tt=tt, psv=psv: nc.vector.tensor_copy(
                        vtok[tt][:, :, 0:64],
                        psv[:, 0:512].rearrange("p (h d) -> p h d", h=8)),
                        reads=[R(psv, 0, 512)], writes=[R(vtok[tt])])
        with scope(c):
            OC = [c.sb(c.name("oc"), [128, 512], BF16) for _ in range(4)]
            ebuf = [c.sb(c.name("eb"), [128, 512], BF16) for _ in range(2)]
            aB = [c.sb(c.name("aB"), [128, 512], F32) for _ in range(2)]
            wo = load_pack_resident(c, p + "wo", 8, 128, 512)

            def _out(h, qb, psO, a):
                dst = OC[h // 2]
                b = 64 * (h % 2)
                tk.emit("dve", lambda dst=dst, b=b, psO=psO, a=a: nc.vector.tensor_mul(
                    dst[b:b + 64, :], psO[0:64, 0:512], a[0:64, :]),
                    reads=[R(psO, 0, 512), R(a)], writes=[R(dst)])

            def _ktl(qb):
                out = []
                for kt in range(max(0, 4 * qb - 4), 4 * qb + 4):
                    j = kt - 4 * qb + 4
                    out.append((kt, (C["mask8"], j * 512, (j + 1) * 512)))
                return out

            cfg = dict(nh=8,
                       qmap=lambda h: (qeff[h // 2], 64 * (h % 2)),
                       kmap=lambda h: (keff[h // 2], 64 * (h % 2)),
                       vsl=lambda kt, h: (vtok[kt][:, h, :], R(vtok[kt])),
                       ktlist=_ktl, scale=8.0, negc=negc,
                       psS=ps[0:2], psO=ps[2:5], psA=ps[5], rows=rows,
                       ebuf=ebuf, aB=aB, out=_out)
            for qb in range(4):
                cfg1 = dict(cfg)
                cfg1["ktlist"] = lambda q, qb=qb: _ktl(qb)
                _attn64_one_qb(c, C, cfg1, qb)
                for oc in range(8):
                    psW = ps[6 + oc % 2]
                    for dc in range(4):
                        tk.emit("pe", lambda oc=oc, dc=dc, psW=psW: nc.tensor.matmul(
                            psW[:, 0:512], wo[oc][:, dc * 128:(dc + 1) * 128],
                            OC[dc][:, :], start=(dc == 0), stop=(dc == 3)),
                            reads=[(id(wo[oc]), dc * 128, (dc + 1) * 128), R(OC[dc])],
                            writes=[R(psW, 0, 512)])
                    tk.emit("dve", lambda oc=oc, qb=qb, psW=psW: nc.vector.tensor_add(
                        y[oc][:, qb * 512:(qb + 1) * 512], psW[:, 0:512],
                        y[oc][:, qb * 512:(qb + 1) * 512]),
                        reads=[R(psW, 0, 512), R(y[oc], qb * 512, (qb + 1) * 512)],
                        writes=[R(y[oc], qb * 512, (qb + 1) * 512)])
        layer_scope.__exit__(None, None, None)
        with scope(c):
            hln = mk_chunks(c, "hln", BF16)
            with scope(c):
                h2b = mk_chunks(c, "h2b", BF16, n=8, F=512)
                lng = load_f32(c, p + "lng", [128, 8])
                lnb = load_f32(c, p + "lnb", [128, 8])
                L = LNState(c, ps[0:3])
                for itb, tb in enumerate(TBS):
                    for ci in range(8):
                        e = "dve" if ci % 2 == 0 else "pool"
                        E = nc.vector if ci % 2 == 0 else nc.gpsimd
                        if ci < 4:
                            tk.emit(e, lambda ci=ci, E=E, tb=tb: E.tensor_copy(
                                h2b[ci][:, :], y[ci][:, tb:tb + 512]),
                                reads=[R(y[ci], tb, tb + 512)], writes=[R(h2b[ci])])
                        elif tb > 0:
                            tk.emit(e, lambda ci=ci, E=E, tb=tb: E.tensor_copy(
                                h2b[ci][:, :], y[ci][:, tb - 1:tb + 511]),
                                reads=[R(y[ci], tb - 1, tb + 511)], writes=[R(h2b[ci])])
                        else:
                            tk.emit(e, lambda ci=ci, E=E: E.tensor_copy(
                                h2b[ci][:, 1:512], y[ci][:, 0:511]),
                                reads=[R(y[ci], 0, 511)], writes=[R(h2b[ci])])
                            tk.emit(e, lambda ci=ci, E=E: E.memset(h2b[ci][:, 0:1], 0.0),
                                    reads=[], writes=[R(h2b[ci], 0, 1)])
                    ln_tb(c, C, L, h2b, tb, [hln], gamma=lng, beta=lnb, src_tb=0)
            b1c = load_f32(c, p + "b1", [128, 44])
            b2c = load_f32(c, p + "b2", [128, 8])
            w1a = WStream(c, p + "w1", 22, 128, 1024, nbuf=2)
            w1g = WStream(c, p + "w1", 22, 128, 1024, nbuf=2, base=22)
            w2s = WStream(c, p + "w2", 8, 128, 2816, nbuf=2)
            h3 = [c.sb(c.name("h3"), [128, 1024], BF16) for _ in range(22)]
            at = [c.sb(c.name("at"), [128, 512], BF16) for _ in range(2)]
            gt = [c.sb(c.name("gt"), [128, 512], BF16) for _ in range(2)]
            for th in range(2):
                w1a.fetched = -1
                w1g.fetched = -1
                w2s.fetched = -1
                w1a.fetch(0); w1g.fetch(0); w1a.fetch(1); w1g.fetch(1)
                for i in range(22):
                    wa, wareg = w1a.get(i)
                    wg, wgreg = w1g.get(i)
                    for tb2 in range(2):
                        tb = th * 1024 + tb2 * 512
                        psA_, psG_ = ps[0 + tb2], ps[2 + tb2]
                        for dc in range(8):
                            tk.emit("pe", lambda wa=wa, dc=dc, tb=tb, psA_=psA_: nc.tensor.matmul(
                                psA_[:, 0:512], wa[:, dc * 128:(dc + 1) * 128],
                                hln[dc][:, tb:tb + 512], start=(dc == 0), stop=(dc == 7)),
                                reads=[(id(wa), dc * 128, (dc + 1) * 128),
                                       R(hln[dc], tb, tb + 512)],
                                writes=[R(psA_, 0, 512)])
                        for dc in range(8):
                            tk.emit("pe", lambda wg=wg, dc=dc, tb=tb, psG_=psG_: nc.tensor.matmul(
                                psG_[:, 0:512], wg[:, dc * 128:(dc + 1) * 128],
                                hln[dc][:, tb:tb + 512], start=(dc == 0), stop=(dc == 7)),
                                reads=[(id(wg), dc * 128, (dc + 1) * 128),
                                       R(hln[dc], tb, tb + 512)],
                                writes=[R(psG_, 0, 512)])
                        a_ = at[tb2]
                        g_ = gt[tb2]
                        tk.emit("act", lambda psA_=psA_, a_=a_, i=i: nc.scalar.add(
                            a_[:, :], psA_[:, 0:512], b1c[:, i:i + 1]),
                            reads=[R(psA_, 0, 512), R(b1c, i, i + 1)], writes=[R(a_)])
                        tk.emit("act", lambda psG_=psG_, g_=g_, i=i: nc.scalar.activation(
                            g_[:, :], psG_[:, 0:512], AF.Gelu, bias=b1c[:, 22 + i:23 + i]),
                            reads=[R(psG_, 0, 512), R(b1c, 22 + i, 23 + i)], writes=[R(g_)])
                        tk.emit("dve", lambda a_=a_, g_=g_, i=i, tb2=tb2: nc.vector.tensor_mul(
                            h3[i][:, tb2 * 512:(tb2 + 1) * 512], a_[:, :], g_[:, :]),
                            reads=[R(a_), R(g_)],
                            writes=[R(h3[i], tb2 * 512, (tb2 + 1) * 512)])
                    w1a.fetch(i + 2)
                    w1g.fetch(i + 2)
                w2s.fetch(0); w2s.fetch(1)
                for oc in range(8):
                    wb, wreg = w2s.get(oc)
                    for tb2 in range(2):
                        tb = th * 1024 + tb2 * 512
                        psY = ps[4 + (oc * 2 + tb2) % 2]
                        for i in range(22):
                            tk.emit("pe", lambda wb=wb, i=i, tb2=tb2, psY=psY: nc.tensor.matmul(
                                psY[:, 0:512], wb[:, i * 128:(i + 1) * 128],
                                h3[i][:, tb2 * 512:(tb2 + 1) * 512],
                                start=(i == 0), stop=(i == 21)),
                                reads=[(id(wb), i * 128, (i + 1) * 128),
                                       R(h3[i], tb2 * 512, (tb2 + 1) * 512)],
                                writes=[R(psY, 0, 512)])
                        tk.emit("dve", lambda oc=oc, tb=tb, psY=psY: nc.vector.scalar_tensor_tensor(
                            y[oc][:, tb:tb + 512], psY[:, 0:512], b2c[:, oc:oc + 1],
                            y[oc][:, tb:tb + 512], ALU.add, ALU.add),
                            reads=[R(psY, 0, 512), R(b2c, oc, oc + 1),
                                   R(y[oc], tb, tb + 512)],
                            writes=[R(y[oc], tb, tb + 512)])
                    w2s.fetch(oc + 2)
    store_chunks_f32(c, "F0T", y)


def _attn64_one_qb(c, C, cfg, qb):
    tk, nc = c.tk, c.nc
    negc = cfg["negc"]
    ktl = cfg["ktlist"](qb)
    qc0, qc1 = cfg.get("qcols", (qb * 512, (qb + 1) * 512))
    for h in range(cfg["nh"]):
        qt, q_base = cfg["qmap"](h)
        ktn, k_base = cfg["kmap"](h)
        psO = cfg["psO"][h % len(cfg["psO"])]
        nkt = len(ktl)
        for i, (kt, mask) in enumerate(ktl):
            psS = cfg["psS"][i % 2]
            tk.emit("pe", lambda psS=psS, kt=kt, ktn=ktn, k_base=k_base,
                    qt=qt, q_base=q_base: nc.tensor.matmul(
                psS[:, 0:512],
                ktn[k_base:k_base + 64, kt * 128:(kt + 1) * 128],
                qt[q_base:q_base + 64, qc0:qc1],
                start=True, stop=True),
                reads=[R(ktn, kt * 128, (kt + 1) * 128),
                       R(qt, qc0, qc1)],
                writes=[R(psS, 0, 512)])
            E = cfg["ebuf"][i % 2]
            tk.emit("act", lambda psS=psS, E=E: nc.scalar.activation(
                E[:, :], psS[:, 0:512], AF.Exp, bias=negc[:, :],
                scale=float(cfg["scale"])),
                reads=[R(psS, 0, 512), R(negc)], writes=[R(E)])
            if mask is not None:
                mt, mlo, mhi = mask
                tk.emit("dve", lambda E=E, mt=mt, mlo=mlo: nc.vector.tensor_mul(
                    E[:, :], E[:, :], mt[:, mlo:mlo + 512]),
                    reads=[R(E), (id(mt), mlo, mhi)], writes=[R(E)])
            vap, vreg = cfg["vsl"](kt, h)
            tk.emit("pe", lambda vap=vap, E=E, i=i, psO=psO, nkt=nkt: nc.tensor.matmul(
                psO[0:65, 0:512], vap, E[:, :],
                start=(i == 0), stop=(i == nkt - 1)),
                reads=[vreg, R(E)], writes=[R(psO, 0, 512)])
        arow, rarow = cfg["rows"].slot(0, 0)
        tk.emit("dve", lambda psO=psO, arow=arow: nc.vector.reciprocal(
            arow, psO[64:65, 0:512]),
            reads=[R(psO, 0, 512)], writes=[rarow])
        psA = cfg["psA"]
        tk.emit("pe", lambda arow=arow: nc.tensor.matmul(
            psA[0:64, 0:512], C["ones_row_f"][0:1, 0:64], arow,
            start=True, stop=True),
            reads=[rarow], writes=[R(psA, 0, 512)])
        aB = cfg["aB"][h % 2]
        tk.emit("act", lambda aB=aB: nc.scalar.copy(aB[0:64, :], psA[0:64, 0:512]),
                reads=[R(psA, 0, 512)], writes=[R(aB)])
        cfg["out"](h, qb, psO, aB)


def stage_mid(c, C):
    tk, nc = c.tk, c.nc
    ps = [c.ps(c.name("ps")) for _ in range(3)]
    x1 = mk_chunks(c, "x1", F32)
    load_chunks_f32(c, "xT", dst=x1)
    tmp = [c.sb(c.name("mtmp"), [128, 2048], F32) for _ in range(2)]
    for name in ("A0T", "F0T"):
        src = c.dram[name]
        for ci in range(8):
            t = tmp[ci % 2]
            tk.emit("sq", lambda ci=ci, t=t, src=src: nc.sync.dma_start(
                out=t[:, :], in_=src[ci * 128:(ci + 1) * 128, :]),
                writes=[R(t)])
            e = "dve" if ci % 2 == 0 else "pool"
            E = nc.vector if ci % 2 == 0 else nc.gpsimd
            tk.emit(e, lambda ci=ci, t=t, E=E: E.tensor_add(
                x1[ci][:, :], x1[ci][:, :], t[:, :]),
                reads=[R(x1[ci]), R(t)], writes=[R(x1[ci])])
    store_chunks_f32(c, "x1T", x1)
    xn1 = mk_chunks(c, "xn1", F32)
    L = LNState(c, ps)
    for tb in TBS:
        ln_tb(c, C, L, x1, tb, [xn1])
    store_chunks_f32(c, "xn1T", xn1)


def stage_fin(c, C):
    tk, nc = c.tk, c.nc
    acc = [c.sb(c.name("facc"), [128, 2048], F32) for _ in range(2)]
    tmp = [c.sb(c.name("ftmp"), [128, 2048], F32) for _ in range(2)]
    for ci in range(8):
        a = acc[ci % 2]
        tk.emit("sq", lambda ci=ci, a=a: nc.sync.dma_start(
            out=a[:, :], in_=c.dram["x1T"][ci * 128:(ci + 1) * 128, :]),
            writes=[R(a)])
        for name in ("A1T", "F1T"):
            t = tmp[ci % 2]
            tk.emit("sq", lambda ci=ci, t=t, name=name: nc.sync.dma_start(
                out=t[:, :], in_=c.dram[name][ci * 128:(ci + 1) * 128, :]),
                writes=[R(t)])
            e = "dve" if ci % 2 == 0 else "pool"
            E = nc.vector if ci % 2 == 0 else nc.gpsimd
            tk.emit(e, lambda a=a, t=t, E=E: E.tensor_add(a[:, :], a[:, :], t[:, :]),
                    reads=[R(a), R(t)], writes=[R(a)])
        tk.emit("sq", lambda ci=ci, a=a: nc.sync.dma_start(
            out=c.dram["outT"][ci * 128:(ci + 1) * 128, :], in_=a[:, :]),
            reads=[R(a)])


def stage_l1_attn(c, C):
    tk, nc = c.tk, c.nc
    ps = [c.ps(c.name("ps")) for _ in range(8)]
    rows = Rows(c, "l1a_rows", 1024)
    xn1b = mk_chunks(c, "xn1b", BF16)
    load_chunks_f32(c, "xn1T", cast_bf16=xn1b)
    CS = load_f32(c, "l1_CS", [128, 2048])
    SS = load_f32(c, "l1_SS", [128, 2048])
    wqa = load_pack_resident(c, "l1a_wqa", 1, 128, 512)[0]
    wkva = load_pack_resident(c, "l1a_wkva", 1, 128, 256)[0]
    wqb = load_pack_resident(c, "l1a_wqb", 8, 128, 128)
    wkrep = c.sb("wkrep", [128, 512], BF16)
    tk.emit("gq", lambda: nc.gpsimd.dma_start(out=wkrep[:, :], in_=c.dram["l1a_wk"][:, :]),
            writes=[R(wkrep)])
    wvp = c.sb("wvp", [128, 256], BF16)
    tk.emit("gq", lambda: nc.gpsimd.dma_start(out=wvp[:, :], in_=c.dram["l1a_wv"][:, :]),
            writes=[R(wvp)])
    t1T = c.sb("t1T", [64, 2048], BF16)
    t2T = c.sb("t2T", [32, 2048], BF16)
    for itb, tb in enumerate(TBS):
        ps0 = ps[itb % 2]
        for dc in range(8):
            tk.emit("pe", lambda dc=dc, tb=tb, ps0=ps0: nc.tensor.matmul(
                ps0[0:64, 0:512], wqa[:, dc * 64:(dc + 1) * 64],
                xn1b[dc][:, tb:tb + 512], start=(dc == 0), stop=(dc == 7)),
                reads=[(id(wqa), dc * 64, (dc + 1) * 64), R(xn1b[dc], tb, tb + 512)],
                writes=[R(ps0, 0, 512)])
        tk.emit("act", lambda tb=tb, ps0=ps0: nc.scalar.copy(
            t1T[:, tb:tb + 512], ps0[0:64, 0:512]),
            reads=[R(ps0, 0, 512)], writes=[R(t1T, tb, tb + 512)])
        ps1 = ps[2 + itb % 2]
        for dc in range(8):
            tk.emit("pe", lambda dc=dc, tb=tb, ps1=ps1: nc.tensor.matmul(
                ps1[0:32, 0:512], wkva[:, dc * 32:(dc + 1) * 32],
                xn1b[dc][:, tb:tb + 512], start=(dc == 0), stop=(dc == 7)),
                reads=[(id(wkva), dc * 32, (dc + 1) * 32), R(xn1b[dc], tb, tb + 512)],
                writes=[R(ps1, 0, 512)])
        tk.emit("act", lambda tb=tb, ps1=ps1: nc.scalar.copy(
            t2T[:, tb:tb + 512], ps1[0:32, 0:512]),
            reads=[R(ps1, 0, 512)], writes=[R(t2T, tb, tb + 512)])
    qeff = mk_chunks(c, "qeff", BF16, n=8)
    keff = mk_chunks(c, "keff", BF16, n=4)
    vtok = [c.sb(c.name("vtok"), [128, 4, 65], BF16) for _ in range(16)]
    qraw = [c.sb(c.name("qraw"), [128, 512], BF16) for _ in range(2)]
    tmpR = [c.sb(c.name("tmpR"), [128, 512], F32) for _ in range(2)]
    for oc in range(8):
        for itb, tb in enumerate(TBS):
            ii = (oc * 4 + itb) % 2
            psq = ps[ii]
            tk.emit("pe", lambda oc=oc, tb=tb, psq=psq: nc.tensor.matmul(
                psq[:, 0:512], wqb[oc][0:64, :], t1T[:, tb:tb + 512],
                start=True, stop=True),
                reads=[(id(wqb[oc]), 0, 128), R(t1T, tb, tb + 512)],
                writes=[R(psq, 0, 512)])
            qr = qraw[ii]
            tk.emit("dve", lambda psq=psq, qr=qr: nc.vector.tensor_copy(qr[:, :], psq[:, 0:512]),
                    reads=[R(psq, 0, 512)], writes=[R(qr)])
            rope_tile(c, "pool" if ii else "dve", qr, 0, qeff[oc],
                      (tb, tb + 512), CS, SS, tb, tmpR[ii])
    for oc in range(4):
        for itb, tb in enumerate(TBS):
            ii = (oc * 4 + itb) % 2
            psq = ps[2 + ii]
            tk.emit("pe", lambda oc=oc, tb=tb, psq=psq: nc.tensor.matmul(
                psq[:, 0:512], wkrep[0:32, oc * 128:(oc + 1) * 128],
                t2T[:, tb:tb + 512], start=True, stop=True),
                reads=[R(wkrep, oc * 128, (oc + 1) * 128), R(t2T, tb, tb + 512)],
                writes=[R(psq, 0, 512)])
            qr = qraw[ii]
            tk.emit("dve", lambda psq=psq, qr=qr: nc.vector.tensor_copy(qr[:, :], psq[:, 0:512]),
                    reads=[R(psq, 0, 512)], writes=[R(qr)])
            rope_tile(c, "pool" if ii else "dve", qr, 0, keff[oc],
                      (tb, tb + 512), CS, SS, tb, tmpR[ii])
    for tt in range(16):
        tk.emit("dve", lambda tt=tt: nc.vector.memset(vtok[tt][:, :, :], 1.0),
                writes=[R(vtok[tt])])
        psv = ps[4 + tt % 2]
        tk.emit("pe", lambda tt=tt, psv=psv: nc.tensor.matmul(
            psv[:, 0:256], t2T[:, tt * 128:(tt + 1) * 128], wvp[0:32, :],
            start=True, stop=True),
            reads=[R(t2T, tt * 128, (tt + 1) * 128), R(wvp)],
            writes=[R(psv, 0, 256)])
        tk.emit("dve", lambda tt=tt, psv=psv: nc.vector.tensor_copy(
            vtok[tt][:, :, 0:64],
            psv[:, 0:256].rearrange("p (h d) -> p h d", h=4)),
            reads=[R(psv, 0, 256)], writes=[R(vtok[tt])])
    sqt = [c.sb(c.name("sqt"), [128, 512], BF16) for _ in range(2)]
    negc = negc_from_norms(c, C, [ps[6]], rows, qeff, keff, 1.0 / 8.0, sqt)
    if "DBGT" in c.dram:
        tk.emit("gq", lambda: nc.gpsimd.dma_start(
            out=c.dram["DBGT"][0:128, 0:2048], in_=qeff[0][:, :]), reads=[R(qeff[0])])
        tk.emit("gq", lambda: nc.gpsimd.dma_start(
            out=c.dram["DBGT"][128:256, 0:2048], in_=keff[0][:, :]), reads=[R(keff[0])])
        tk.emit("gq", lambda: nc.gpsimd.dma_start(
            out=c.dram["DBGT"][256:384, 0:2048], in_=qeff[7][:, :]), reads=[R(qeff[7])])
        tk.emit("sq", lambda: nc.sync.dma_start(
            out=c.dram["DBGT"][384:512, 0:1], in_=negc[:, :],
            allow_slow_non_contiguous=True), reads=[R(negc)])
        tk.emit("gq", lambda: nc.gpsimd.dma_start(
            out=c.dram["DBGT"][512:576, 0:2048], in_=t1T[:, :]), reads=[R(t1T)])
        tk.emit("gq", lambda: nc.gpsimd.dma_start(
            out=c.dram["DBGT"][576:704, 0:260], in_=vtok[0][:, :, :]), reads=[R(vtok[0])])
    octx = mk_chunks(c, "octx", BF16, n=8, F=512)
    ebuf = [c.sb(c.name("eb"), [128, 512], BF16) for _ in range(2)]
    aBs = [c.sb(c.name("aB"), [128, 512], F32) for _ in range(2)]
    wout = load_pack_resident(c, "l1a_wout", 8, 128, 1024)
    boutc = load_f32(c, "l1a_bout", [128, 8])
    a1t = [c.sb(c.name("a1t"), [128, 512], F32) for _ in range(2)]

    def _out(h, qb, psO, a):
        dst = octx[h // 2]
        b = 64 * (h % 2)
        tk.emit("dve", lambda dst=dst, b=b, psO=psO, a=a: nc.vector.tensor_mul(
            dst[b:b + 64, :], psO[0:64, 0:512], a[0:64, :]),
            reads=[R(psO, 0, 512), R(a)], writes=[R(dst)])

    cfg = dict(nh=16,
               qmap=lambda h: (qeff[h // 2], 64 * (h % 2)),
               kmap=lambda h: (keff[h // 4], 64 * (h % 2)),
               vsl=lambda kt, h: (vtok[kt][:, h // 4, :], R(vtok[kt])),
               scale=1.0 / 8.0, negc=negc,
               psS=ps[0:2], psO=ps[2:5], psA=ps[5], rows=rows,
               ebuf=ebuf, aB=aBs, out=_out)
    for qb in range(4):
        cfg["ktlist"] = lambda q, qb=qb: [
            (kt, (C["maskc4"], (kt - 4 * qb) * 512, (kt - 4 * qb + 1) * 512)
             if kt >= 4 * qb else None)
            for kt in range(4 * qb + 4)]
        _attn64_one_qb(c, C, cfg, qb)
        for oc in range(8):
            psW = ps[6 + oc % 2]
            for dc in range(8):
                tk.emit("pe", lambda oc=oc, dc=dc, psW=psW: nc.tensor.matmul(
                    psW[:, 0:512], wout[oc][:, dc * 128:(dc + 1) * 128],
                    octx[dc][:, :], start=(dc == 0), stop=(dc == 7)),
                    reads=[(id(wout[oc]), dc * 128, (dc + 1) * 128), R(octx[dc])],
                    writes=[R(psW, 0, 512)])
            a1 = a1t[oc % 2]
            tk.emit("act", lambda oc=oc, psW=psW, a1=a1: nc.scalar.add(
                a1[:, :], psW[:, 0:512], boutc[:, oc:oc + 1]),
                reads=[R(psW, 0, 512), R(boutc, oc, oc + 1)], writes=[R(a1)])
            tk.emit("sq", lambda oc=oc, qb=qb, a1=a1: nc.sync.dma_start(
                out=c.dram["A1T"][oc * 128:(oc + 1) * 128, qb * 512:(qb + 1) * 512],
                in_=a1[:, :]),
                reads=[R(a1)])


def stage_l1_ffn(c, C):
    tk, nc = c.tk, c.nc
    ps = [c.ps(c.name("ps")) for _ in range(8)]
    rows = Rows(c, "l1f_rows", 1024)
    aft = c.nc.dram_tensor("AFt", [1024, 2048], F32, kind="Internal").ap()
    with scope(c):
        CS = load_f32(c, "l1_CS", [128, 2048])
        SS = load_f32(c, "l1_SS", [128, 2048])
        xb = mk_chunks(c, "xb", BF16, n=8, F=512)
        qin = mk_chunks(c, "qin", BF16, n=8, F=512)
        qe = mk_chunks(c, "qe", BF16, n=8, F=512)
        ke = mk_chunks(c, "ke", BF16, n=8)
        vtok = [c.sb(c.name("vtok"), [128, 16, 65], BF16) for _ in range(16)]
        qraw = [c.sb(c.name("qraw"), [128, 512], BF16) for _ in range(2)]
        tmpR = [c.sb(c.name("tmpR"), [128, 512], F32) for _ in range(2)]
        sqt = [c.sb(c.name("sqt"), [128, 512], BF16) for _ in range(2)]
        octx = mk_chunks(c, "octx", BF16, n=8, F=512)
        ebuf = [c.sb(c.name("eb"), [128, 512], BF16) for _ in range(2)]
        aBs = [c.sb(c.name("aB"), [128, 512], F32) for _ in range(2)]
        negc = c.sb("negcl1f", [128, 1], F32)
        wqs = WStream(c, "l1f_wq", 8, 128, 1024, nbuf=2)
        wks = WStream(c, "l1f_wk", 8, 128, 1024, nbuf=2)
        wos = WStream(c, "l1f_wo", 8, 128, 1024, nbuf=2)
        bqc = load_f32(c, "l1f_bq", [128, 8])
        bkc = load_f32(c, "l1f_bk", [128, 8])
        boc = load_f32(c, "l1f_bo", [128, 8])
        wvr = load_pack_resident(c, "l1f_wv", 8, 128, 1024)
        bvrow = c.sb("bvrow", [128, 1024], BF16)
        tk.emit("gq", lambda: nc.gpsimd.dma_start(out=bvrow[:, :], in_=c.dram["l1f_bv"][:, :]),
                writes=[R(bvrow)])
        kmax, rkmax = rows.slot(0, 520, 1)
        qmax, rqmax = rows.slot(0, 521, 1)
        cc, rcc = rows.slot(0, 522, 1)
        tmpm, rtmpm = rows.slot(0, 523, 1)

        def _norm_max(tiles, cols, dst, rdst, first):
            psN = ps[6]
            for i, qt in enumerate(tiles):
                sq = sqt[i % 2]
                tk.emit("act", lambda qt=qt, sq=sq: nc.scalar.activation(
                    sq[:, :], qt[:, cols[0]:cols[1]], AF.Square),
                    reads=[R(qt, cols[0], cols[1])], writes=[R(sq)])
                tk.emit("pe", lambda sq=sq, i=i, n=len(tiles): nc.tensor.matmul(
                    psN[0:1, 0:512], C["ones_col_b"], sq[:, :],
                    start=(i == 0), stop=(i == n - 1)),
                    reads=[R(sq)], writes=[R(psN, 0, 512)])
            tk.emit("dve", lambda: nc.vector.reduce_max(tmpm, psN[0:1, 0:512], axis=AX.X),
                    reads=[R(psN, 0, 512)], writes=[rtmpm])
            if first:
                tk.emit("dve", lambda: nc.vector.tensor_copy(dst, tmpm),
                        reads=[rtmpm], writes=[rdst])
            else:
                tk.emit("dve", lambda: nc.vector.tensor_max(dst, dst, tmpm),
                        reads=[rdst, rtmpm], writes=[rdst])

        for itb, tb in enumerate(TBS):
            qb = itb
            for dc in range(8):
                tk.emit("gq", lambda dc=dc, tb=tb: nc.gpsimd.dma_start(
                    out=xb[dc][:, :],
                    in_=c.dram["xn1T"][dc * 128:(dc + 1) * 128, tb:tb + 512]),
                    writes=[R(xb[dc])])
            for dc in range(8):
                ii = dc % 2
                psq = ps[ii]
                tk.emit("pe", lambda dc=dc, psq=psq: nc.tensor.matmul(
                    psq[:, 0:512], C["deintb"][:, :], xb[dc][:, :],
                    start=True, stop=True),
                    reads=[R(C["deintb"]), R(xb[dc])], writes=[R(psq, 0, 512)])
                qr = qraw[ii]
                tk.emit("dve", lambda psq=psq, qr=qr: nc.vector.tensor_copy(qr[:, :], psq[:, 0:512]),
                        reads=[R(psq, 0, 512)], writes=[R(qr)])
                rope_tile(c, "pool" if ii else "dve", qr, 0, qin[dc],
                          (0, 512), CS, SS, tb, tmpR[ii])
            for (W, dst, bc, full) in ((wqs, qe, bqc, False), (wks, ke, bkc, True)):
                W.fetched = -1
                W.fetch(0); W.fetch(1)
                for oc in range(8):
                    wb, wreg = W.get(oc)
                    psq = ps[oc % 2]
                    for dc in range(8):
                        tk.emit("pe", lambda wb=wb, dc=dc, psq=psq: nc.tensor.matmul(
                            psq[:, 0:512], wb[:, dc * 128:(dc + 1) * 128],
                            qin[dc][:, :], start=(dc == 0), stop=(dc == 7)),
                            reads=[(id(wb), dc * 128, (dc + 1) * 128), R(qin[dc])],
                            writes=[R(psq, 0, 512)])
                    c0, c1 = (tb, tb + 512) if full else (0, 512)
                    tk.emit("act", lambda psq=psq, dst=dst, oc=oc, bc=bc, c0=c0, c1=c1: nc.scalar.add(
                        dst[oc][:, c0:c1], psq[:, 0:512], bc[:, oc:oc + 1]),
                        reads=[R(psq, 0, 512), R(bc, oc, oc + 1)],
                        writes=[R(dst[oc], c0, c1)])
                    W.fetch(oc + 2)
            for tt in range(4 * itb, 4 * itb + 4):
                tk.emit("dve", lambda tt=tt: nc.vector.memset(vtok[tt][:, :, :], 1.0),
                        writes=[R(vtok[tt])])
                lo = (tt - 4 * itb) * 128
                for vh in range(2):
                    psv = ps[4 + (tt * 2 + vh) % 2]
                    for dc in range(8):
                        tk.emit("pe", lambda tt=tt, vh=vh, dc=dc, psv=psv, lo=lo: nc.tensor.matmul(
                            psv[:, 0:512], xb[dc][:, lo:lo + 128],
                            wvr[dc][:, vh * 512:(vh + 1) * 512],
                            start=(dc == 0), stop=False),
                            reads=[R(xb[dc], lo, lo + 128),
                                   (id(wvr[dc]), vh * 512, (vh + 1) * 512)],
                            writes=[R(psv, 0, 512)])
                    tk.emit("pe", lambda vh=vh, psv=psv: nc.tensor.matmul(
                        psv[:, 0:512], C["ones_row_b"], bvrow[0:1, vh * 512:(vh + 1) * 512],
                        start=False, stop=True),
                        reads=[R(bvrow, vh * 512, (vh + 1) * 512)],
                        writes=[R(psv, 0, 512)])
                    tk.emit("dve", lambda tt=tt, vh=vh, psv=psv: nc.vector.tensor_copy(
                        vtok[tt][:, vh * 8:(vh + 1) * 8, 0:64],
                        psv[:, 0:512].rearrange("p (h d) -> p h d", h=8)),
                        reads=[R(psv, 0, 512)], writes=[R(vtok[tt])])
            _norm_max(ke, (tb, tb + 512), kmax, rkmax, first=(itb == 0))
            _norm_max(qe, (0, 512), qmax, rqmax, first=True)
            tk.emit("dve", lambda: nc.vector.tensor_mul(cc, qmax, kmax),
                    reads=[rqmax, rkmax], writes=[rcc])
            tk.emit("act", lambda: nc.scalar.activation(cc, cc, AF.Ln, bias=1e-24),
                    reads=[rcc], writes=[rcc])
            tk.emit("act", lambda: nc.scalar.activation(cc, cc, AF.Exp, scale=0.5),
                    reads=[rcc], writes=[rcc])
            tk.emit("pe", lambda: nc.tensor.matmul(ps[6][0:128, 0:1], C["ones_row_f"],
                                                   cc, start=True, stop=True),
                    reads=[rcc], writes=[R(ps[6], 0, 1)])
            tk.emit("act", lambda: nc.scalar.mul(negc[:, :], ps[6][0:128, 0:1], -1.0 / 8.0),
                    reads=[R(ps[6], 0, 1)], writes=[R(negc)])
            cfg = dict(nh=16,
                       qmap=lambda h: (qe[h // 2], 64 * (h % 2)),
                       kmap=lambda h: (ke[h // 2], 64 * (h % 2)),
                       vsl=lambda kt, h: (vtok[kt][:, h, :], R(vtok[kt])),
                       scale=1.0 / 8.0, negc=negc,
                       psS=ps[0:2], psO=ps[2:5], psA=ps[5], rows=rows,
                       ebuf=ebuf, aB=aBs, out=None,
                       ktlist=lambda q: [
                           (kt, (C["maskc4"], (kt - 4 * qb) * 512, (kt - 4 * qb + 1) * 512)
                            if kt >= 4 * qb else None)
                           for kt in range(4 * qb + 4)],
                       qcols=(0, 512))

            def _out(h, q_, psO, a):
                dst = octx[h // 2]
                b = 64 * (h % 2)
                tk.emit("dve", lambda dst=dst, b=b, psO=psO, a=a: nc.vector.tensor_mul(
                    dst[b:b + 64, :], psO[0:64, 0:512], a[0:64, :]),
                    reads=[R(psO, 0, 512), R(a)], writes=[R(dst)])
            cfg["out"] = _out
            _attn64_one_qb(c, C, cfg, qb)
            wos.fetched = -1
            wos.fetch(0); wos.fetch(1)
            aoutt = [c.sb(c.name("aoutt"), [128, 512], F32) for _ in range(2)] \
                if itb == 0 else aoutt_hold
            aoutt_hold = aoutt
            for oc in range(8):
                wb, wreg = wos.get(oc)
                psW = ps[6 + oc % 2]
                for dc in range(8):
                    tk.emit("pe", lambda wb=wb, dc=dc, psW=psW: nc.tensor.matmul(
                        psW[:, 0:512], wb[:, dc * 128:(dc + 1) * 128],
                        octx[dc][:, :], start=(dc == 0), stop=(dc == 7)),
                        reads=[(id(wb), dc * 128, (dc + 1) * 128), R(octx[dc])],
                        writes=[R(psW, 0, 512)])
                ao = aoutt[oc % 2]
                tk.emit("act", lambda psW=psW, ao=ao, oc=oc: nc.scalar.add(
                    ao[:, :], psW[:, 0:512], boc[:, oc:oc + 1]),
                    reads=[R(psW, 0, 512), R(boc, oc, oc + 1)], writes=[R(ao)])
                tk.emit("sq", lambda ao=ao, oc=oc, tb=tb: nc.sync.dma_start(
                    out=aft[oc * 128:(oc + 1) * 128, tb:tb + 512], in_=ao[:, :]),
                    reads=[R(ao)])
                wos.fetch(oc + 2)
    x1fb = mk_chunks(c, "x1fb", BF16)
    with scope(c):
        xn1 = mk_chunks(c, "z", F32)
        load_chunks_f32(c, "xn1T", dst=xn1)
        tmp = [c.sb(c.name("ztmp"), [128, 2048], F32) for _ in range(2)]
        for ci in range(8):
            t = tmp[ci % 2]
            tk.emit("sq", lambda ci=ci, t=t: nc.sync.dma_start(
                out=t[:, :], in_=aft[ci * 128:(ci + 1) * 128, :]),
                writes=[R(t)])
            e = "dve" if ci % 2 == 0 else "pool"
            E = nc.vector if ci % 2 == 0 else nc.gpsimd
            tk.emit(e, lambda ci=ci, t=t, E=E: E.tensor_add(
                xn1[ci][:, :], xn1[ci][:, :], t[:, :]),
                reads=[R(xn1[ci]), R(t)], writes=[R(xn1[ci])])
        ln1w = load_f32(c, "l1f_ln1w", [128, 8])
        ln1b = load_f32(c, "l1f_ln1b", [128, 8])
        L = LNState(c, ps[0:3])
        for tb in TBS:
            ln_tb(c, C, L, xn1, tb, [xn1, x1fb], gamma=ln1w, beta=ln1b)
        for ci in range(8):
            tk.emit("sq", lambda ci=ci: nc.sync.dma_start(
                out=aft[ci * 128:(ci + 1) * 128, :], in_=xn1[ci][:, :]),
                reads=[R(xn1[ci])])
    glog = c.sb("glog", [8, 2048], F32)
    w8all = c.sb("w8all", [128, 128], F32)
    wrowb = c.sb("wrowb", [8, 2048], BF16)
    with scope(c):
        gw = load_pack_resident(c, "l1f_gw", 8, 128, 8)
        gb = load_f32(c, "l1f_gb", [128, 1])
        identf = load_f32(c, "ident", [128, 128])
        for itb, tb in enumerate(TBS):
            psG = ps[itb % 2]
            for dc in range(8):
                tk.emit("pe", lambda dc=dc, tb=tb, psG=psG: nc.tensor.matmul(
                    psG[0:8, 0:512], gw[dc][:, :], x1fb[dc][:, tb:tb + 512],
                    start=(dc == 0), stop=(dc == 7)),
                    reads=[(id(gw[dc]), 0, 8), R(x1fb[dc], tb, tb + 512)],
                    writes=[R(psG, 0, 512)])
            tk.emit("act", lambda tb=tb, psG=psG: nc.scalar.add(
                glog[0:8, tb:tb + 512], psG[0:8, 0:512], gb[0:8, 0:1]),
                reads=[R(psG, 0, 512), R(gb)], writes=[R(glog, tb, tb + 512)])
        g8 = c.sb("g8", [128, 8], F32)
        eqt = c.sb("eqt", [128, 8], F32)
        gtmp = c.sb("gtmp", [128, 4], F32)
        for tt in range(16):
            psT = ps[2 + tt % 2]
            tk.emit("pe", lambda tt=tt, psT=psT: nc.tensor.transpose(
                psT[0:128, 0:8], glog[0:8, tt * 128:(tt + 1) * 128],
                identf[0:8, 0:8]),
                reads=[R(glog, tt * 128, (tt + 1) * 128), R(identf)],
                writes=[R(psT, 0, 8)])
            v = nc.vector
            tk.emit("dve", lambda psT=psT: v.tensor_copy(g8[:, :], psT[0:128, 0:8]),
                    reads=[R(psT, 0, 8)], writes=[R(g8)])
            tk.emit("dve", lambda: v.reduce_max(gtmp[:, 0:1], g8[:, :], axis=AX.X),
                    reads=[R(g8)], writes=[R(gtmp, 0, 1)])
            tk.emit("dve", lambda: v.tensor_scalar_mul(gtmp[:, 1:2], gtmp[:, 0:1], -1.0),
                    reads=[R(gtmp, 0, 1)], writes=[R(gtmp, 1, 2)])
            tk.emit("act", lambda: nc.scalar.activation(
                g8[:, :], g8[:, :], AF.Exp, bias=gtmp[:, 1:2]),
                reads=[R(g8), R(gtmp, 1, 2)], writes=[R(g8)])
            tk.emit("dve", lambda: v.reduce_sum(gtmp[:, 2:3], g8[:, :], axis=AX.X),
                    reads=[R(g8)], writes=[R(gtmp, 2, 3)])
            tk.emit("dve", lambda: v.reciprocal(gtmp[:, 3:4], gtmp[:, 2:3]),
                    reads=[R(gtmp, 2, 3)], writes=[R(gtmp, 3, 4)])
            tk.emit("dve", lambda: v.tensor_scalar_mul(g8[:, :], g8[:, :], gtmp[:, 3:4]),
                    reads=[R(g8), R(gtmp, 3, 4)], writes=[R(g8)])
            tk.emit("dve", lambda: v.reduce_max(gtmp[:, 0:1], g8[:, :], axis=AX.X),
                    reads=[R(g8)], writes=[R(gtmp, 0, 1)])
            tk.emit("dve", lambda: v.tensor_single_scalar(
                eqt[:, :], g8[:, :], gtmp[:, 0:1], ALU.is_ge),
                reads=[R(g8), R(gtmp, 0, 1)], writes=[R(eqt)])
            tk.emit("dve", lambda: v.scalar_tensor_tensor(
                eqt[:, :], eqt[:, :], -2.0, g8[:, :], ALU.mult, ALU.add),
                reads=[R(eqt), R(g8)], writes=[R(eqt)])
            tk.emit("dve", lambda: v.reduce_max(gtmp[:, 1:2], eqt[:, :], axis=AX.X),
                    reads=[R(eqt)], writes=[R(gtmp, 1, 2)])
            tk.emit("dve", lambda: v.tensor_single_scalar(
                eqt[:, :], g8[:, :], gtmp[:, 1:2], ALU.is_ge),
                reads=[R(g8), R(gtmp, 1, 2)], writes=[R(eqt)])
            tk.emit("dve", lambda tt=tt: v.tensor_mul(
                w8all[:, tt * 8:(tt + 1) * 8], g8[:, :], eqt[:, :]),
                reads=[R(g8), R(eqt)], writes=[R(w8all, tt * 8, (tt + 1) * 8)])
            psT2 = ps[4 + tt % 2]
            tk.emit("pe", lambda tt=tt, psT2=psT2: nc.tensor.transpose(
                psT2[0:8, 0:128], w8all[:, tt * 8:(tt + 1) * 8], identf[:, :]),
                reads=[R(w8all, tt * 8, (tt + 1) * 8), R(identf)],
                writes=[R(psT2, 0, 128)])
            tk.emit("act", lambda tt=tt, psT2=psT2: nc.scalar.copy(
                wrowb[0:8, tt * 128:(tt + 1) * 128], psT2[0:8, 0:128]),
                reads=[R(psT2, 0, 128)], writes=[R(wrowb, tt * 128, (tt + 1) * 128)])
    macc = mk_chunks(c, "macc", F32)
    with scope(c):
        sel8 = c.sb("sel8_s", [128, 1024], BF16)
        tk.emit("gq", lambda: nc.gpsimd.dma_start(out=sel8[:, :], in_=c.dram["sel8"][:, :]),
                writes=[R(sel8)])
        ebc = load_f32(c, "l1f_eb", [128, 64])
        ewS = WStream(c, "l1f_ew", 64, 128, 1024, nbuf=2)
        wBfull = c.sb("wBfull", [128, 2048], BF16)
        tmpE = [c.sb(c.name("tmpE"), [128, 512], BF16) for _ in range(2)]
        ewS.fetch(0); ewS.fetch(1)
        for e in range(8):
            for itb, tb in enumerate(TBS):
                psB = ps[6]
                tk.emit("pe", lambda e=e, tb=tb, psB=psB: nc.tensor.matmul(
                    psB[:, 0:512], sel8[0:8, e * 128:(e + 1) * 128],
                    wrowb[0:8, tb:tb + 512], start=True, stop=True),
                    reads=[R(sel8, e * 128, (e + 1) * 128), R(wrowb, tb, tb + 512)],
                    writes=[R(psB, 0, 512)])
                tk.emit("dve", lambda tb=tb, psB=psB: nc.vector.tensor_copy(
                    wBfull[:, tb:tb + 512], psB[:, 0:512]),
                    reads=[R(psB, 0, 512)], writes=[R(wBfull, tb, tb + 512)])
            for oc in range(8):
                wb, wreg = ewS.get(e * 8 + oc)
                for itb, tb in enumerate(TBS):
                    psE = ps[(oc * 4 + itb) % 2]
                    for dc in range(8):
                        tk.emit("pe", lambda wb=wb, dc=dc, tb=tb, psE=psE: nc.tensor.matmul(
                            psE[:, 0:512], wb[:, dc * 128:(dc + 1) * 128],
                            x1fb[dc][:, tb:tb + 512], start=(dc == 0), stop=(dc == 7)),
                            reads=[(id(wb), dc * 128, (dc + 1) * 128),
                                   R(x1fb[dc], tb, tb + 512)],
                            writes=[R(psE, 0, 512)])
                    col = e * 8 + oc
                    if e == 0:
                        tk.emit("dve", lambda oc=oc, tb=tb, psE=psE, col=col: nc.vector.scalar_tensor_tensor(
                            macc[oc][:, tb:tb + 512], psE[:, 0:512], ebc[:, col:col + 1],
                            wBfull[:, tb:tb + 512], ALU.add, ALU.mult),
                            reads=[R(psE, 0, 512), R(ebc, col, col + 1),
                                   R(wBfull, tb, tb + 512)],
                            writes=[R(macc[oc], tb, tb + 512)])
                    else:
                        tE = tmpE[itb % 2]
                        tk.emit("dve", lambda tE=tE, psE=psE, col=col, tb=tb: nc.vector.scalar_tensor_tensor(
                            tE[:, :], psE[:, 0:512], ebc[:, col:col + 1],
                            wBfull[:, tb:tb + 512], ALU.add, ALU.mult),
                            reads=[R(psE, 0, 512), R(ebc, col, col + 1),
                                   R(wBfull, tb, tb + 512)],
                            writes=[R(tE)])
                        tk.emit("pool", lambda oc=oc, tb=tb, tE=tE: nc.gpsimd.tensor_add(
                            macc[oc][:, tb:tb + 512], macc[oc][:, tb:tb + 512], tE[:, :]),
                            reads=[R(macc[oc], tb, tb + 512), R(tE)],
                            writes=[R(macc[oc], tb, tb + 512)])
                ewS.fetch(e * 8 + oc + 2)
    with scope(c):
        tmp = [c.sb(c.name("z2tmp"), [128, 2048], F32) for _ in range(2)]
        for ci in range(8):
            t = tmp[ci % 2]
            tk.emit("sq", lambda ci=ci, t=t: nc.sync.dma_start(
                out=t[:, :], in_=aft[ci * 128:(ci + 1) * 128, :]),
                writes=[R(t)])
            e = "dve" if ci % 2 == 0 else "pool"
            E = nc.vector if ci % 2 == 0 else nc.gpsimd
            tk.emit(e, lambda ci=ci, t=t, E=E: E.tensor_add(
                macc[ci][:, :], macc[ci][:, :], t[:, :]),
                reads=[R(macc[ci]), R(t)], writes=[R(macc[ci])])
    with scope(c):
        ln2w = load_f32(c, "l1f_ln2w", [128, 8])
        ln2b = load_f32(c, "l1f_ln2b", [128, 8])
        L = LNState(c, ps[0:3])
        for tb in TBS:
            ln_tb(c, C, L, macc, tb, [macc], gamma=ln2w, beta=ln2b)
    store_chunks_f32(c, "F1T", macc)



_L0F_W = []
for _l in range(2):
    _L0F_W += [f"l0f{_l}_{s}" for s in
               ("wq", "wk", "wv", "wo", "qs", "ks", "qsr", "ksr",
                "lng", "lnb", "w1", "b1", "w2", "b2")]

STAGES = {
    "ln0": (stage_ln0, ["xT"], (), ["xnT"]),
    "l0attn": (stage_l0_attn, ["xnT", "l0a_qw", "l0a_kw", "l0a_vw", "maskc4"],
               ("maskc4",), ["A0T"]),
    "l0ffn": (stage_l0_ffn,
              ["xnT", "mask8", "sel2", "selB",
               "l0_CSq", "l0_SSq", "l0_CSk", "l0_SSk"] + _L0F_W,
              ("mask8", "sel2"), ["F0T"]),
    "mid": (stage_mid, ["xT", "A0T", "F0T"], (), ["x1T", "xn1T"]),
    "l1attn": (stage_l1_attn,
               ["xn1T", "l1a_wqa", "l1a_wqb", "l1a_wkva", "l1a_wk", "l1a_wv",
                "l1a_wout", "l1a_bout", "l1_CS", "l1_SS", "maskc4"],
               ("maskc4",), ["A1T"]),
    "l1ffn": (stage_l1_ffn,
              ["xn1T", "l1_CS", "l1_SS", "maskc4", "deint", "ident", "sel8",
               "l1f_wq", "l1f_wk", "l1f_wv", "l1f_bq", "l1f_bk", "l1f_bv",
               "l1f_wo", "l1f_bo", "l1f_gw", "l1f_gb", "l1f_ew", "l1f_eb",
               "l1f_ln1w", "l1f_ln1b", "l1f_ln2w", "l1f_ln2b"],
              ("maskc4", "deintb"), ["F1T"]),
    "fin": (stage_fin, ["x1T", "A1T", "F1T"], (), ["outT"]),
}

STAGE_ORDER = ["ln0", "l0attn", "l0ffn", "mid", "l1attn", "l1ffn", "fin"]

_INTERMEDIATE = {"xnT", "A0T", "F0T", "x1T", "xn1T", "A1T", "F1T", "outT"}

_NC_CACHE = {}


def _np_dtype_to_mybir(dt):
    if dt == np.dtype("float32"):
        return F32
    if dt == BFNP:
        return BF16
    raise ValueError(dt)


def build_stage_nc(stage, shapes):
    key = stage
    if key in _NC_CACHE:
        return _NC_CACHE[key]
    builder, ins, cwhich, outs = STAGES[stage]
    nc = bass.Bass("TRN2", target_bir_lowering=False)
    stack = ExitStack()
    c = Ctx(nc, stack)
    for name in ins:
        shp, dt = shapes[name]
        c.dram[name] = nc.dram_tensor(name, list(shp), dt, kind="ExternalInput").ap()
    for name in outs:
        c.dram[name] = nc.dram_tensor(name, [1024, 2048], F32, kind="ExternalOutput").ap()
    C = load_consts(c, cwhich)
    c.tk.barrier()
    builder(c, C)
    c.tk.fin()
    stack.close()
    _NC_CACHE[key] = nc
    return nc


def kernel(**inputs):
    consts = host_consts()
    pms = [prep_core_inputs(b, inputs) for b in range(8)]
    inter = [dict() for _ in range(8)]
    shapes = {}
    for k, v in list(pms[0].items()) + list(consts.items()):
        shapes[k] = (v.shape, _np_dtype_to_mybir(v.dtype))
    for k in _INTERMEDIATE:
        shapes[k] = ((1024, 2048), F32)
    for stage in STAGE_ORDER:
        builder, ins, cwhich, outs = STAGES[stage]
        nc = build_stage_nc(stage, shapes)
        in_maps = []
        for b in range(8):
            im = {}
            for name in ins:
                if name in consts:
                    im[name] = consts[name]
                elif name in _INTERMEDIATE:
                    im[name] = inter[b][name]
                else:
                    im[name] = pms[b][name]
            in_maps.append(im)
        res = run_bass_kernel_spmd(nc, in_maps, core_ids=list(range(8)))
        for b in range(8):
            for name in outs:
                inter[b][name] = res.results[b][name]
    out = np.stack([inter[b]["outT"].T for b in range(8)])
    return np.ascontiguousarray(out.astype(np.float32))



FUSED = True

_FUSED_CONSTS = ("maskc4", "mask8", "sel2", "deintb")


def build_fused_nc(shapes):
    if "fused" in _NC_CACHE:
        return _NC_CACHE["fused"]
    nc = bass.Bass("TRN2", target_bir_lowering=False)
    stack = ExitStack()
    c = Ctx(nc, stack)
    seen = set()
    for stage in STAGE_ORDER:
        builder, ins, cwhich, outs = STAGES[stage]
        for name in ins:
            if name in seen or name in _INTERMEDIATE:
                continue
            seen.add(name)
            shp, dt = shapes[name]
            c.dram[name] = nc.dram_tensor(name, list(shp), dt,
                                          kind="ExternalInput").ap()
    for name in _INTERMEDIATE:
        kind = "ExternalOutput" if name == "outT" else "Internal"
        c.dram[name] = nc.dram_tensor(name, [1024, 2048], F32, kind=kind).ap()
    C = load_consts(c, _FUSED_CONSTS)
    c.tk.barrier()
    for stage in STAGE_ORDER:
        builder, ins, cwhich, outs = STAGES[stage]
        with scope(c):
            builder(c, C)
    c.tk.fin()
    stack.close()
    _NC_CACHE["fused"] = nc
    return nc


def _kernel_fused(inputs):
    consts = host_consts()
    pms = [prep_core_inputs(b, inputs) for b in range(8)]
    shapes = {}
    for k, v in list(pms[0].items()) + list(consts.items()):
        shapes[k] = (v.shape, _np_dtype_to_mybir(v.dtype))
    nc = build_fused_nc(shapes)
    names = set()
    for stage in STAGE_ORDER:
        for name in STAGES[stage][1]:
            if name not in _INTERMEDIATE:
                names.add(name)
    in_maps = []
    for b in range(8):
        im = {}
        for name in names:
            im[name] = consts[name] if name in consts else pms[b][name]
        in_maps.append(im)
    res = run_bass_kernel_spmd(nc, in_maps, core_ids=list(range(8)))
    out = np.stack([res.results[b]["outT"].T for b in range(8)])
    return np.ascontiguousarray(out.astype(np.float32))


_kernel_multilaunch = kernel


def kernel(**inputs):
    if FUSED:
        return _kernel_fused(inputs)
    return _kernel_multilaunch(**inputs)


# version 07
# speedup vs baseline: 1.3092x; 1.3092x over previous
import numpy as np
import ml_dtypes
from contextlib import ExitStack

import concourse.bass as bass
import concourse.mybir as mybir
from concourse.bass_utils import run_bass_kernel_spmd

F32 = mybir.dt.float32
BF16 = mybir.dt.bfloat16
AF = mybir.ActivationFunctionType
ALU = mybir.AluOpType
AX = mybir.AxisListType
BFNP = ml_dtypes.bfloat16

T = 2048
D = 1024
NCHUNK = 8
TBS = [0, 512, 1024, 1536]
NTB = 4


def pack_lhsT(W):
    K, M = W.shape
    kc = K // 128 if K >= 128 else 1
    kp = min(K, 128)
    mc = (M + 127) // 128
    mp = min(M, 128)
    assert K % kp == 0 and M % mp == 0
    arr = W.reshape(kc, kp, mc, mp).transpose(2, 1, 0, 3)
    arr = arr.reshape(mc, kp, kc * mp)
    if kp < 128:
        arr = np.concatenate(
            [arr, np.zeros((mc, 128 - kp, kc * mp), dtype=arr.dtype)], axis=1)
    return np.ascontiguousarray(arr)


def pack_rhs(W):
    K, M = W.shape
    kp = min(K, 128)
    return np.ascontiguousarray(W.reshape(K // kp, kp, M))


def pack_cols(v):
    n = v.shape[0] // 128
    return np.ascontiguousarray(v.reshape(n, 128).T)


def deinterleave_perm(n, head=64):
    p = np.zeros(n, dtype=np.int64)
    for h in range(n // head):
        for j in range(head):
            p[h * head + j] = h * head + (2 * j if j < head // 2 else 2 * (j - head // 2) + 1)
    return p


def host_masks():
    kp = np.arange(128)[:, None]
    qf = np.arange(512)[None, :]
    mc4 = np.stack([(qf - kp >= 128 * j).astype(np.float32) for j in range(4)])
    m8 = []
    for j in range(-4, 4):
        d = qf - kp - 128 * j
        m8.append(((d >= 0) & (d <= 512)).astype(np.float32))
    return mc4.astype(BFNP), np.stack(m8).astype(BFNP)


def host_l0_xpos_tables():
    DH = 64
    WINDOW = 512
    half = np.arange(0, DH, 2, dtype=np.float64)
    inv_freq = 1.0 / 10000.0 ** (half / DH)
    t = np.arange(T, dtype=np.float64)
    freqs = t[:, None] * inv_freq[None, :]
    freqs = np.concatenate([freqs, freqs], 1)
    base = (half + 0.4 * DH) / (1.4 * DH)
    power = (t - T // 2) / float(WINDOW // 2)
    scale = base[None, :] ** power[:, None]
    scale = np.concatenate([scale, scale], 1)
    cos, sin = np.cos(freqs), np.sin(freqs)
    sgn = np.where(np.arange(DH)[None, :] < 32, 1.0, -1.0)
    CSq = (cos * scale).T
    SSq = (sin * scale).T * sgn.T
    CSk = (cos / scale).T
    SSk = (sin / scale).T * sgn.T
    out = []
    for A in (CSq, SSq, CSk, SSk):
        out.append(np.ascontiguousarray(np.vstack([A, A]).astype(np.float32)))
    return out


def host_l1_rope_tables():
    DH = 64
    inv_freq = 1.0 / 10000.0 ** (np.arange(0, DH, 2, dtype=np.float64) / DH)
    t = np.arange(T, dtype=np.float64)
    fr = t[:, None] * inv_freq[None, :]
    c = np.cos(fr).T
    s = np.sin(fr).T
    CS = np.vstack([c, c])
    SS = np.vstack([s, -s])
    CS = np.vstack([CS, CS]).astype(np.float32)
    SS = np.vstack([SS, SS]).astype(np.float32)
    return np.ascontiguousarray(CS), np.ascontiguousarray(SS)


def host_consts():
    d = {}
    mc4, m8 = host_masks()
    d["maskc4"] = mc4
    d["mask8"] = m8
    CSq, SSq, CSk, SSk = host_l0_xpos_tables()
    d["l0_CSq"], d["l0_SSq"], d["l0_CSk"], d["l0_SSk"] = CSq, SSq, CSk, SSk
    CS, SS = host_l1_rope_tables()
    d["l1_CS"], d["l1_SS"] = CS, SS
    ident = np.eye(128, dtype=np.float32)
    d["ident"] = ident
    pi = deinterleave_perm(128)
    dm = np.zeros((128, 128), dtype=np.float32)
    for j in range(128):
        dm[pi[j], j] = 1.0
    d["deint"] = dm
    sel2 = np.zeros((128, 2), dtype=np.float32)
    sel2[0:64, 0] = 1.0
    sel2[64:128, 1] = 1.0
    d["sel2"] = sel2
    selB = np.zeros((128, 128), dtype=np.float32)
    selB[0, 0:64] = 1.0
    selB[1, 64:128] = 1.0
    d["selB"] = selB
    sel8 = np.zeros((128, 1024), dtype=np.float32)
    for e in range(8):
        sel8[e, e * 128:(e + 1) * 128] = 1.0
    d["sel8"] = sel8
    return d


def prep_core_inputs(b, I):
    m = {}
    m["xT"] = np.ascontiguousarray(I["L0_attn_x"][b].T)
    m["l0a_qw"] = pack_lhsT(I["L0_attn_Qw"])
    m["l0a_kw"] = pack_lhsT(I["L0_attn_Kw"])
    m["l0a_vw"] = pack_rhs(I["L0_attn_Vw"])
    for l in range(2):
        wqkv = I["L0_ffn_wqkv"][l]
        m[f"l0f{l}_wq"] = pack_lhsT(wqkv[:, 0:512])
        m[f"l0f{l}_wk"] = pack_lhsT(wqkv[:, 512:1024])
        m[f"l0f{l}_wv"] = pack_rhs(wqkv[:, 1024:1536])
        m[f"l0f{l}_wo"] = pack_lhsT(I["L0_ffn_wo"][l])
        m[f"l0f{l}_qs"] = np.ascontiguousarray(
            np.tile(I["L0_ffn_q_scale"][l], 2).reshape(128, 1))
        m[f"l0f{l}_ks"] = np.ascontiguousarray(
            np.tile(I["L0_ffn_k_scale"][l], 2).reshape(128, 1))
        m[f"l0f{l}_qsr"] = np.ascontiguousarray(np.pad(
            I["L0_ffn_q_scale"][l].reshape(1, 64), ((0, 127), (0, 0))))
        m[f"l0f{l}_ksr"] = np.ascontiguousarray(np.pad(
            I["L0_ffn_k_scale"][l].reshape(1, 64), ((0, 127), (0, 0))))
        m[f"l0f{l}_lng"] = pack_cols(I["L0_ffn_ln_g"][l])
        m[f"l0f{l}_lnb"] = pack_cols(I["L0_ffn_ln_b"][l])
        w1 = I["L0_ffn_w1"][l]
        b1 = I["L0_ffn_b1"][l]
        w1p = np.zeros((1024, 5632), dtype=np.float32)
        b1p = np.zeros((5632,), dtype=np.float32)
        w1p[:, 0:2730] = w1[:, 0:2730]
        w1p[:, 2816:5546] = w1[:, 2730:5460]
        b1p[0:2730] = b1[0:2730]
        b1p[2816:5546] = b1[2730:5460]
        m[f"l0f{l}_w1"] = pack_lhsT(w1p)
        m[f"l0f{l}_b1"] = pack_cols(b1p)
        w2p = np.zeros((2816, 1024), dtype=np.float32)
        w2p[0:2730] = I["L0_ffn_w2"][l]
        m[f"l0f{l}_w2"] = pack_lhsT(w2p)
        m[f"l0f{l}_b2"] = pack_cols(I["L0_ffn_b2"][l])
    pi = deinterleave_perm(1024)
    m["l1a_wqa"] = pack_lhsT(I["L1_attn_Wqa"])
    m["l1a_wqb"] = pack_lhsT(I["L1_attn_Wqb"][:, pi])
    m["l1a_wkva"] = pack_lhsT(I["L1_attn_Wkva"])
    kvb = I["L1_attn_Wkvb"]
    kperm = np.zeros(256, dtype=np.int64)
    vperm = np.zeros(256, dtype=np.int64)
    for h in range(4):
        for j in range(64):
            dd = 2 * j if j < 32 else 2 * (j - 32) + 1
            kperm[64 * h + j] = 128 * h + 2 * dd
            vperm[64 * h + j] = 128 * h + 2 * j + 1
    dup = np.concatenate([np.tile(np.arange(64 * g, 64 * (g + 1)), 2)
                          for g in range(4)])
    m["l1a_wk"] = np.ascontiguousarray(np.pad(kvb[:, kperm][:, dup],
                                              ((0, 96), (0, 0))))
    m["l1a_wv"] = np.ascontiguousarray(np.pad(kvb[:, vperm],
                                              ((0, 96), (0, 0))))
    m["l1a_wout"] = pack_lhsT(I["L1_attn_Wout"])
    m["l1a_bout"] = pack_cols(I["L1_attn_bout"])
    ipw = I["L1_ffn_in_proj_w"]
    ipb = I["L1_ffn_in_proj_b"]
    Wq, Wk, Wv = ipw[0:1024], ipw[1024:2048], ipw[2048:3072]
    m["l1f_wq"] = pack_lhsT(np.ascontiguousarray(Wq.T[pi]))
    m["l1f_wk"] = pack_lhsT(np.ascontiguousarray(Wk.T[pi]))
    m["l1f_wv"] = pack_rhs(np.ascontiguousarray(Wv.T))
    m["l1f_bq"] = pack_cols(ipb[0:1024])
    m["l1f_bk"] = pack_cols(ipb[1024:2048])
    m["l1f_bv"] = np.ascontiguousarray(np.pad(ipb[2048:3072].reshape(1, 1024),
                                              ((0, 127), (0, 0))))
    m["l1f_wo"] = pack_lhsT(np.ascontiguousarray(I["L1_ffn_out_proj_w"].T))
    m["l1f_bo"] = pack_cols(I["L1_ffn_out_proj_b"])
    m["l1f_gw"] = np.ascontiguousarray(
        I["L1_ffn_gate_w"].T.reshape(8, 128, 8))
    m["l1f_gb"] = np.ascontiguousarray(
        np.pad(I["L1_ffn_gate_b"], (0, 120)).reshape(128, 1))
    ew = I["L1_ffn_expert_w"]
    m["l1f_ew"] = np.stack([pack_lhsT(np.ascontiguousarray(ew[e].T))
                            for e in range(8)]).reshape(64, 128, 1024)
    m["l1f_eb"] = np.concatenate([pack_cols(I["L1_ffn_expert_b"][e])
                                  for e in range(8)], axis=1)
    m["l1f_ln1w"] = pack_cols(I["L1_ffn_ln1_w"])
    m["l1f_ln1b"] = pack_cols(I["L1_ffn_ln1_b"])
    m["l1f_ln2w"] = pack_cols(I["L1_ffn_ln2_w"])
    m["l1f_ln2b"] = pack_cols(I["L1_ffn_ln2_b"])
    return m



class Tracker:

    def __init__(self, nc, stack):
        self.nc = nc
        self.eng = {
            "pe": nc.tensor, "act": nc.scalar, "dve": nc.vector,
            "pool": nc.gpsimd, "sq": nc.sync, "gq": nc.gpsimd,
        }
        self.step = {"pe": 1, "act": 1, "dve": 1, "pool": 1, "sq": 16, "gq": 16}
        self.sem = {}
        self.cum = {}
        self.instrs = {}
        self.incs = {}
        self.waited = {}
        for e in self.eng:
            self.sem[e] = stack.enter_context(nc.semaphore("s_" + e))
            self.cum[e] = 0
            self.instrs[e] = []
            self.incs[e] = []
        self.writes = {}
        self.reads = {}

    def _need(self, e, idx):
        if e in ("sq", "gq"):
            return self.cum[e]
        arr = self.incs[e]
        import bisect
        pos = bisect.bisect_left(arr, (idx, -1))
        if pos < len(arr):
            return arr[pos][1]
        self.cum[e] += self.step[e]
        v = self.cum[e]
        self.instrs[e][idx].then_inc(self.sem[e], self.step[e])
        arr.append((idx, v))
        return v

    def _edges(self, regions, kinds):
        deps = []
        for (key, lo, hi) in regions:
            for store in kinds:
                recs = store.get(key)
                if not recs:
                    continue
                for (l2, h2, e2, i2) in recs:
                    if l2 < hi and lo < h2:
                        deps.append((e2, i2))
        return deps

    def emit(self, e, thunk, reads=(), writes=()):
        deps = []
        deps += self._edges(reads, (self.writes,))
        deps += self._edges(writes, (self.writes, self.reads))
        byeng = {}
        for (e2, i2) in deps:
            if e2 == e and e in ("pe", "sq", "gq"):
                continue
            if e2 == e and i2 == len(self.instrs[e]) - 0:
                continue
            if byeng.get(e2, -1) < i2:
                byeng[e2] = i2
        for e2, i2 in sorted(byeng.items()):
            v = self._need(e2, i2)
            k = (e, e2)
            if self.waited.get(k, -1) >= v:
                continue
            self.eng[e].wait_ge(self.sem[e2], v)
            self.waited[k] = v
        instr = thunk()
        idx = len(self.instrs[e])
        self.instrs[e].append(instr)
        if e in ("sq", "gq"):
            self.cum[e] += self.step[e]
            instr.then_inc(self.sem[e], self.step[e])
            self.incs[e].append((idx, self.cum[e]))
        for (key, lo, hi) in writes:
            wl = self.writes.setdefault(key, [])
            out = []
            for r in wl:
                if r[0] < hi and lo < r[1]:
                    if r[0] < lo:
                        out.append((r[0], lo, r[2], r[3]))
                    if r[1] > hi:
                        out.append((hi, r[1], r[2], r[3]))
                else:
                    out.append(r)
            out.append((lo, hi, e, idx))
            self.writes[key] = out
            rl = self.reads.get(key)
            if rl:
                self.reads[key] = [r for r in rl if not (r[0] < hi and lo < r[1])]
        for (key, lo, hi) in reads:
            self.reads.setdefault(key, []).append((lo, hi, e, idx))
        return instr

    def fin(self):
        for e in ("sq", "gq"):
            if self.cum[e] < self.step[e] * len(self.instrs[e]):
                if self.instrs[e]:
                    self._need(e, len(self.instrs[e]) - 1)
        for e in ("sq", "gq"):
            if self.cum[e] > 0:
                self.eng[e].wait_ge(self.sem[e], self.cum[e])


class Ctx:
    def __init__(self, nc, stack):
        self.nc = nc
        self.stack = stack
        self.tk = Tracker(nc, stack)
        self.dram = {}
        self.uid = 0

    def sb(self, name, shape, dtype):
        t = self.stack.enter_context(self.nc.sbuf_tensor(name, shape, dtype))
        return t

    def ps(self, name, shape=(128, 512), dtype=F32):
        return self.stack.enter_context(self.nc.psum_tensor(name, list(shape), dtype))

    def name(self, base):
        self.uid += 1
        return f"{base}_{self.uid}"


def R(buf, lo=0, hi=1 << 30):
    return (id(buf), lo, hi)



def load_consts(c: Ctx, which=()):
    tk, nc = c.tk, c.nc
    d = {}
    for v in (1e-5, 1e-24):
        t = c.sb(c.name("constb"), [128, 1], F32)
        tk.emit("dve", lambda t=t, v=v: nc.vector.memset(t[:, :], float(v)),
                writes=[R(t)])
        nc.const_aps.register(F32, float(v), t[:, :])
    ones = c.sb("onesbuf", [128, 8], F32)
    tk.emit("dve", lambda: nc.vector.memset(ones[:, 0:4], 1.0), writes=[R(ones, 0, 4)])
    d["ones_col_f"] = ones[:, 0:1]
    onesb = c.sb("onesbufb", [128, 8], BF16)
    tk.emit("dve", lambda: nc.vector.memset(onesb[:, 0:4], 1.0), writes=[R(onesb, 0, 4)])
    d["ones_col_b"] = onesb[:, 0:1]
    onesr = c.sb("onesrow", [1, 128], F32)
    tk.emit("dve", lambda: nc.vector.memset(onesr[:, :], 1.0), writes=[R(onesr)])
    d["ones_row_f"] = onesr[0:1, :]
    onesrb = c.sb("onesrowb", [1, 128], BF16)
    tk.emit("dve", lambda: nc.vector.memset(onesrb[:, :], 1.0), writes=[R(onesrb)])
    d["ones_row_b"] = onesrb[0:1, :]
    if "maskc4" in which:
        mc4 = c.sb("maskc4_s", [128, 4 * 512], BF16)
        for j in range(4):
            tk.emit("sq", lambda j=j: nc.sync.dma_start(
                out=mc4[:, j * 512:(j + 1) * 512], in_=c.dram["maskc4"][j]),
                writes=[R(mc4, j * 512, (j + 1) * 512)])
        d["maskc4"] = mc4
    if "mask8" in which:
        m8 = c.sb("mask8_s", [128, 8 * 512], BF16)
        for j in range(8):
            tk.emit("sq", lambda j=j: nc.sync.dma_start(
                out=m8[:, j * 512:(j + 1) * 512], in_=c.dram["mask8"][j]),
                writes=[R(m8, j * 512, (j + 1) * 512)])
        d["mask8"] = m8
    if "deintb" in which:
        deintb = c.sb("deintb_s", [128, 128], BF16)
        tk.emit("gq", lambda: nc.gpsimd.dma_start(
            out=deintb[:, :], in_=c.dram["deint"][:, :]), writes=[R(deintb)])
        d["deintb"] = deintb
    if "sel2" in which:
        sel2 = c.sb("sel2_s", [128, 2], BF16)
        tk.emit("gq", lambda: nc.gpsimd.dma_start(
            out=sel2[:, :], in_=c.dram["sel2"][:, :]), writes=[R(sel2)])
        d["sel2"] = sel2
        selB = c.sb("selB_s", [128, 128], BF16)
        tk.emit("gq", lambda: nc.gpsimd.dma_start(
            out=selB[:, :], in_=c.dram["selB"][:, :]), writes=[R(selB)])
        d["selB"] = selB
    return d


class Rows:
    def __init__(self, c, name, ncols):
        self.t = c.sb(name, [128, ncols], F32)
        self.ncols = ncols

    def slot(self, base, col, w=512):
        return self.t[base:base + 1, col:col + w], R(self.t, col, col + w)


class LNState:
    def __init__(self, c, psum_pool):
        self.rows = Rows(c, c.name("lnrows"), 1024)
        self.sq = [c.sb(c.name("lnsq"), [128, 512], BF16) for _ in range(2)]
        self.tmp = [c.sb(c.name("lntmp"), [128, 512], F32) for _ in range(4)]
        self.muB = c.sb(c.name("lnmuB"), [128, 512], F32)
        self.rB = c.sb(c.name("lnrB"), [128, 512], F32)
        self.ps = psum_pool


def ln_tb(c, C, L, src, tb, dsts, gamma=None, beta=None, eps=1e-5, src_tb=None):
    tk, nc = c.tk, c.nc
    st = tb if src_tb is None else src_tb
    srcf32 = src[0].dtype == F32
    psA, psB, psC = L.ps[0], L.ps[1], L.ps[2]
    onescol = C["ones_col_f"] if srcf32 else C["ones_col_b"]
    onesb = C["ones_col_b"]
    onesrow = C["ones_row_f"]
    for ci in range(8):
        s = src[ci]
        tk.emit("pe", lambda s=s, ci=ci: nc.tensor.matmul(
            psA[0:1, 0:512], onescol, s[:, st:st + 512],
            start=(ci == 0), stop=(ci == 7)),
            reads=[R(s, st, st + 512)], writes=[R(psA, 0, 512)])
        sq = L.sq[ci % 2]
        tk.emit("act", lambda s=s, sq=sq: nc.scalar.activation(
            sq[:, :], s[:, st:st + 512], AF.Square),
            reads=[R(s, st, st + 512)], writes=[R(sq)])
        tk.emit("pe", lambda sq=sq, ci=ci: nc.tensor.matmul(
            psB[0:1, 0:512], onesb, sq[:, :],
            start=(ci == 0), stop=(ci == 7)),
            reads=[R(sq)], writes=[R(psB, 0, 512)])
    mu, rmu = L.rows.slot(0, 0)
    msq, rmsq = L.rows.slot(32, 0)
    var, rvar = L.rows.slot(0, 512)
    tmq, rtmq = L.rows.slot(32, 512)
    tk.emit("act", lambda: nc.scalar.mul(mu, psA[0:1, 0:512], 1.0 / 1024.0),
            reads=[R(psA, 0, 512)], writes=[rmu])
    tk.emit("act", lambda: nc.scalar.mul(msq, psB[0:1, 0:512], 1.0 / 1024.0),
            reads=[R(psB, 0, 512)], writes=[rmsq])
    tk.emit("dve", lambda: nc.vector.tensor_mul(tmq, mu, mu),
            reads=[rmu], writes=[rtmq])
    tk.emit("dve", lambda: nc.vector.tensor_sub(var, msq, tmq),
            reads=[rmsq, rtmq], writes=[rvar])
    tk.emit("act", lambda: nc.scalar.activation(var, var, AF.Ln, bias=float(eps)),
            reads=[rvar], writes=[rvar])
    tk.emit("act", lambda: nc.scalar.activation(var, var, AF.Exp, scale=-0.5),
            reads=[rvar], writes=[rvar])
    tk.emit("pe", lambda: nc.tensor.matmul(psC[:, 0:512], onesrow, mu,
                                           start=True, stop=True),
            reads=[rmu], writes=[R(psC, 0, 512)])
    tk.emit("dve", lambda: nc.vector.tensor_copy(L.muB[:, :], psC[:, 0:512]),
            reads=[R(psC, 0, 512)], writes=[R(L.muB)])
    tk.emit("pe", lambda: nc.tensor.matmul(psC[:, 0:512], onesrow, var,
                                           start=True, stop=True),
            reads=[rvar], writes=[R(psC, 0, 512)])
    tk.emit("dve", lambda: nc.vector.tensor_copy(L.rB[:, :], psC[:, 0:512]),
            reads=[R(psC, 0, 512)], writes=[R(L.rB)])
    for ci in range(8):
        s = src[ci]
        e = "dve" if (ci % 2 == 0) else "pool"
        eng = nc.vector if (ci % 2 == 0) else nc.gpsimd
        t1 = L.tmp[ci % 2]
        tk.emit(e, lambda s=s, t1=t1, eng=eng: eng.tensor_sub(
            t1[:, :], s[:, st:st + 512], L.muB[:, :]),
            reads=[R(s, st, st + 512), R(L.muB)], writes=[R(t1)])
        if gamma is None:
            for dst in dsts:
                dd = dst[ci]
                tk.emit(e, lambda t1=t1, dd=dd, eng=eng: eng.tensor_mul(
                    dd[:, tb:tb + 512], t1[:, :], L.rB[:, :]),
                    reads=[R(t1), R(L.rB)], writes=[R(dd, tb, tb + 512)])
        else:
            t2 = L.tmp[2 + ci % 2]
            tk.emit(e, lambda t1=t1, t2=t2, eng=eng: eng.tensor_mul(
                t2[:, :], t1[:, :], L.rB[:, :]),
                reads=[R(t1), R(L.rB)], writes=[R(t2)])
            for dst in dsts:
                dd = dst[ci]
                tk.emit(e, lambda t2=t2, dd=dd, ci=ci, eng=eng: eng.tensor_scalar(
                    dd[:, tb:tb + 512], t2[:, :], gamma[:, ci:ci + 1],
                    beta[:, ci:ci + 1], ALU.mult, ALU.add),
                    reads=[R(t2), R(gamma, ci, ci + 1), R(beta, ci, ci + 1)],
                    writes=[R(dd, tb, tb + 512)])



class WStream:
    def __init__(self, c, dram_name, n_oc, P, K, nbuf=2, base=0):
        self.c = c
        self.src = c.dram[dram_name]
        self.n_oc = n_oc
        self.P, self.K = P, K
        self.nbuf = nbuf
        self.base = base
        self.bufs = [c.sb(c.name(dram_name + "_wb"), [P, K], BF16)
                     for _ in range(nbuf)]
        self.fetched = -1

    def fetch(self, oc):
        if oc >= self.n_oc or oc <= self.fetched:
            return
        self.fetched = oc
        b = self.bufs[oc % self.nbuf]
        tk, nc = self.c.tk, self.c.nc
        tk.emit("gq", lambda: nc.gpsimd.dma_start(out=b[:, :],
                in_=self.src[self.base + oc]),
                writes=[R(b)])

    def get(self, oc):
        return self.bufs[oc % self.nbuf], R(self.bufs[oc % self.nbuf])


def load_pack_resident(c, dram_name, n_oc, P, K):
    tk, nc = c.tk, c.nc
    out = []
    for oc in range(n_oc):
        t = c.sb(c.name(dram_name + "_r"), [P, K], BF16)
        tk.emit("gq", lambda t=t, oc=oc: nc.gpsimd.dma_start(
            out=t[:, :], in_=c.dram[dram_name][oc]), writes=[R(t)])
        out.append(t)
    return out


def load_f32(c, dram_name, shape, qn="sq"):
    t = c.sb(c.name(dram_name + "_f"), list(shape), F32)
    tk, nc = c.tk, c.nc
    src = c.dram[dram_name]
    tk.emit(qn, lambda: tk.eng[qn].dma_start(out=t[:, :], in_=src[:, :]),
            writes=[R(t)])
    return t


def load_chunks_f32(c, dram_name, dst=None, cast_bf16=None):
    tk, nc = c.tk, c.nc
    src = c.dram[dram_name]
    if dst is not None:
        for ci in range(8):
            tk.emit("sq", lambda ci=ci: nc.sync.dma_start(
                out=dst[ci][:, :], in_=src[ci * 128:(ci + 1) * 128, :]),
                writes=[R(dst[ci])])
    if cast_bf16 is not None:
        for ci in range(8):
            tk.emit("gq", lambda ci=ci: nc.gpsimd.dma_start(
                out=cast_bf16[ci][:, :], in_=src[ci * 128:(ci + 1) * 128, :]),
                writes=[R(cast_bf16[ci])])


def store_chunks_f32(c, dram_name, src, tbs=None):
    tk, nc = c.tk, c.nc
    dst = c.dram[dram_name]
    for ci in range(8):
        tk.emit("sq", lambda ci=ci: nc.sync.dma_start(
            out=dst[ci * 128:(ci + 1) * 128, :], in_=src[ci][:, :]),
            reads=[R(src[ci])])


def mk_chunks(c, base, dtype, n=8, F=T):
    return [c.sb(c.name(base), [128, F], dtype) for _ in range(n)]



def negc_from_norms(c, C, ps, rows, q_tiles, k_tiles, scale, sqtmp):
    tk, nc = c.tk, c.nc
    psN = ps[0]
    acc, racc = rows.slot(0, 512, 8)
    for (j, tiles) in ((0, q_tiles), (1, k_tiles)):
        for itb, tb in enumerate(TBS):
            for i, qt in enumerate(tiles):
                sq = sqtmp[i % 2]
                tk.emit("act", lambda qt=qt, sq=sq, tb=tb: nc.scalar.activation(
                    sq[:, :], qt[:, tb:tb + 512], AF.Square),
                    reads=[R(qt, tb, tb + 512)], writes=[R(sq)])
                tk.emit("pe", lambda sq=sq, i=i, n=len(tiles): nc.tensor.matmul(
                    psN[0:1, 0:512], C["ones_col_b"], sq[:, :],
                    start=(i == 0), stop=(i == n - 1)),
                    reads=[R(sq)], writes=[R(psN, 0, 512)])
            col = j * 4 + itb
            tk.emit("dve", lambda col=col: nc.vector.reduce_max(
                acc[0:1, col - j * 4 + j * 4:col + 1] if False else acc[0:1, col:col + 1],
                psN[0:1, 0:512], axis=AX.X),
                reads=[R(psN, 0, 512)], writes=[(racc[0], racc[1] + col, racc[1] + col + 1)])
    mq, rmq = rows.slot(0, 520, 1)
    mk, rmk = rows.slot(0, 521, 1)
    cc, rcc = rows.slot(0, 522, 1)
    tk.emit("dve", lambda: nc.vector.reduce_max(mq, acc[0:1, 0:4], axis=AX.X),
            reads=[(racc[0], racc[1], racc[1] + 4)], writes=[rmq])
    tk.emit("dve", lambda: nc.vector.reduce_max(mk, acc[0:1, 4:8], axis=AX.X),
            reads=[(racc[0], racc[1] + 4, racc[1] + 8)], writes=[rmk])
    tk.emit("dve", lambda: nc.vector.tensor_mul(cc, mq, mk),
            reads=[rmq, rmk], writes=[rcc])
    tk.emit("act", lambda: nc.scalar.activation(cc, cc, AF.Ln),
            reads=[rcc], writes=[rcc])
    tk.emit("act", lambda: nc.scalar.activation(cc, cc, AF.Exp, scale=0.5),
            reads=[rcc], writes=[rcc])
    negc = c.sb(c.name("negc"), [128, 1], F32)
    tk.emit("pe", lambda: nc.tensor.matmul(psN[0:128, 0:1], C["ones_row_f"],
                                           cc, start=True, stop=True),
            reads=[rcc], writes=[R(psN, 0, 1)])
    tk.emit("act", lambda: nc.scalar.mul(negc[:, :], psN[0:128, 0:1], -float(scale)),
            reads=[R(psN, 0, 1)], writes=[R(negc)])
    return negc



def attn64(c, C, cfg):
    tk, nc = c.tk, c.nc
    negc = cfg["negc"]
    for qb in range(4):
        ktl = cfg["ktlist"](qb)
        for h in range(cfg["nh"]):
            qt, q_base = cfg["qmap"](h)
            ktn, k_base = cfg["kmap"](h)
            psO = cfg["psO"][h % len(cfg["psO"])]
            nkt = len(ktl)
            for i, (kt, mask) in enumerate(ktl):
                psS = cfg["psS"][i % 2]
                tk.emit("pe", lambda psS=psS, kt=kt, ktn=ktn, k_base=k_base,
                        qt=qt, q_base=q_base, qb=qb: nc.tensor.matmul(
                    psS[:, 0:512],
                    ktn[k_base:k_base + 64, kt * 128:(kt + 1) * 128],
                    qt[q_base:q_base + 64, qb * 512:(qb + 1) * 512],
                    start=True, stop=True),
                    reads=[R(ktn, kt * 128, (kt + 1) * 128),
                           R(qt, qb * 512, (qb + 1) * 512)],
                    writes=[R(psS, 0, 512)])
                E = cfg["ebuf"][i % 2]
                tk.emit("act", lambda psS=psS, E=E: nc.scalar.activation(
                    E[:, :], psS[:, 0:512], AF.Exp, bias=negc[:, :],
                    scale=float(cfg["scale"])),
                    reads=[R(psS, 0, 512), R(negc)], writes=[R(E)])
                if mask is not None:
                    mt, mlo, mhi = mask
                    tk.emit("dve", lambda E=E, mt=mt, mlo=mlo: nc.vector.tensor_mul(
                        E[:, :], E[:, :], mt[:, mlo:mlo + 512]),
                        reads=[R(E), (id(mt), mlo, mhi)], writes=[R(E)])
                vap, vreg = cfg["vsl"](kt, h)
                tk.emit("pe", lambda vap=vap, E=E, i=i, psO=psO, nkt=nkt: nc.tensor.matmul(
                    psO[0:65, 0:512], vap, E[:, :],
                    start=(i == 0), stop=(i == nkt - 1)),
                    reads=[vreg, R(E)], writes=[R(psO, 0, 512)])
            arow, rarow = cfg["rows"].slot(0, 0)
            tk.emit("dve", lambda psO=psO, arow=arow: nc.vector.reciprocal(
                arow, psO[64:65, 0:512]),
                reads=[R(psO, 0, 512)], writes=[rarow])
            psA = cfg["psA"]
            tk.emit("pe", lambda arow=arow: nc.tensor.matmul(
                psA[0:64, 0:512], C["ones_row_f"][0:1, 0:64], arow,
                start=True, stop=True),
                reads=[rarow], writes=[R(psA, 0, 512)])
            aB = cfg["aB"][h % 2]
            tk.emit("act", lambda aB=aB: nc.scalar.copy(aB[0:64, :], psA[0:64, 0:512]),
                    reads=[R(psA, 0, 512)], writes=[R(aB)])
            cfg["out"](h, qb, psO, aB)



def stage_ln0(c, C):
    tk, nc = c.tk, c.nc
    x = mk_chunks(c, "x", F32)
    load_chunks_f32(c, "xT", dst=x)
    xn = mk_chunks(c, "xn", F32)
    ps = [c.ps(c.name("psln")) for _ in range(3)]
    L = LNState(c, ps)
    for tb in TBS:
        ln_tb(c, C, L, x, tb, [xn])
    store_chunks_f32(c, "xnT", xn)


def stage_l0_attn(c, C):
    tk, nc = c.tk, c.nc
    xnb = mk_chunks(c, "xnb", BF16)
    load_chunks_f32(c, "xnT", cast_bf16=xnb)
    qw = WStream(c, "l0a_qw", 8, 128, 1024, nbuf=2)
    kw = WStream(c, "l0a_kw", 8, 128, 1024, nbuf=2)
    vw = load_pack_resident(c, "l0a_vw", 8, 128, 1024)
    qT = mk_chunks(c, "l0a_qT", BF16)
    kT = mk_chunks(c, "l0a_kT", BF16)
    vtok = [c.sb(c.name("l0a_vtok"), [128, 1024], BF16) for _ in range(16)]
    ps = [c.ps(c.name("ps")) for _ in range(7)]
    rows = Rows(c, "l0a_rows", 1024)
    sqt = [c.sb(c.name("sqt"), [128, 512], BF16) for _ in range(2)]
    for (W, dst) in ((qw, qT), (kw, kT)):
        W.fetch(0); W.fetch(1)
        for oc in range(8):
            wb, wreg = W.get(oc)
            for itb, tb in enumerate(TBS):
                psq = ps[itb % 2]
                for dc in range(8):
                    tk.emit("pe", lambda wb=wb, dc=dc, tb=tb, psq=psq: nc.tensor.matmul(
                        psq[:, 0:512], wb[:, dc * 128:(dc + 1) * 128],
                        xnb[dc][:, tb:tb + 512], start=(dc == 0), stop=(dc == 7)),
                        reads=[(id(wb), dc * 128, (dc + 1) * 128), R(xnb[dc], tb, tb + 512)],
                        writes=[R(psq, 0, 512)])
                tk.emit("act", lambda psq=psq, dst=dst, oc=oc, tb=tb: nc.scalar.copy(
                    dst[oc][:, tb:tb + 512], psq[:, 0:512]),
                    reads=[R(psq, 0, 512)], writes=[R(dst[oc], tb, tb + 512)])
            W.fetch(oc + 2)
    for tt in range(16):
        for vh in range(2):
            psv = ps[2 + (tt * 2 + vh) % 2]
            for dc in range(8):
                tk.emit("pe", lambda tt=tt, vh=vh, dc=dc, psv=psv: nc.tensor.matmul(
                    psv[:, 0:512], xnb[dc][:, tt * 128:(tt + 1) * 128],
                    vw[dc][:, vh * 512:(vh + 1) * 512],
                    start=(dc == 0), stop=(dc == 7)),
                    reads=[R(xnb[dc], tt * 128, (tt + 1) * 128),
                           (id(vw[dc]), vh * 512, (vh + 1) * 512)],
                    writes=[R(psv, 0, 512)])
            tk.emit("act", lambda psv=psv, tt=tt, vh=vh: nc.scalar.copy(
                vtok[tt][:, vh * 512:(vh + 1) * 512], psv[:, 0:512]),
                reads=[R(psv, 0, 512)], writes=[R(vtok[tt], vh * 512, (vh + 1) * 512)])
    negc = negc_from_norms(c, C, [ps[4]], rows, qT, kT, 1.0 / 32.0, sqt)
    if "DBGT" in c.dram:
        tk.emit("sq", lambda: nc.sync.dma_start(
            out=c.dram["DBGT"][0:128, 0:1], in_=negc[:, :],
            allow_slow_non_contiguous=True), reads=[R(negc)])
        tk.emit("gq", lambda: nc.gpsimd.dma_start(
            out=c.dram["DBGT"][128:256, 0:2048], in_=qT[0][:, :]), reads=[R(qT[0])])
        tk.emit("gq", lambda: nc.gpsimd.dma_start(
            out=c.dram["DBGT"][256:384, 0:2048], in_=kT[0][:, :]), reads=[R(kT[0])])
        tk.emit("gq", lambda: nc.gpsimd.dma_start(
            out=c.dram["DBGT"][384:512, 0:1024], in_=vtok[0][:, :]), reads=[R(vtok[0])])
        tk.emit("sq", lambda: nc.sync.dma_start(
            out=c.dram["DBGT"][768:896, 0:1024], in_=rows.t[:, :]),
            reads=[R(rows.t)])
    ebuf = c.sb("l0a_ebuf", [128, 16 * 512], BF16)
    aB = c.sb("l0a_aB", [128, 512], F32)
    a0t = [c.sb(c.name("a0t"), [128, 512], F32) for _ in range(2)]
    for qb in range(4):
        nkt = 4 * qb + 4

        def emit_l(kt, nkt=None):
            tk.emit("pe", lambda kt=kt: nc.tensor.matmul(
                ps[4][0:1, 0:512], C["ones_col_b"],
                ebuf[:, kt * 512:(kt + 1) * 512],
                start=(kt == 0), stop=(kt == nkt - 1)),
                reads=[R(ebuf, kt * 512, (kt + 1) * 512)],
                writes=[R(ps[4], 0, 512)])
        for kt in range(nkt):
            psS = ps[kt % 2]
            for dc in range(8):
                tk.emit("pe", lambda kt=kt, dc=dc, qb=qb, psS=psS: nc.tensor.matmul(
                    psS[:, 0:512], kT[dc][:, kt * 128:(kt + 1) * 128],
                    qT[dc][:, qb * 512:(qb + 1) * 512],
                    start=(dc == 0), stop=(dc == 7)),
                    reads=[R(kT[dc], kt * 128, (kt + 1) * 128),
                           R(qT[dc], qb * 512, (qb + 1) * 512)],
                    writes=[R(psS, 0, 512)])
            if kt > 0:
                emit_l(kt - 1, nkt)
            tk.emit("act", lambda psS=psS, kt=kt: nc.scalar.activation(
                ebuf[:, kt * 512:(kt + 1) * 512], psS[:, 0:512], AF.Exp,
                bias=negc[:, :], scale=1.0 / 32.0),
                reads=[R(psS, 0, 512), R(negc)],
                writes=[R(ebuf, kt * 512, (kt + 1) * 512)])
            j = kt - 4 * qb
            if j >= 0:
                tk.emit("dve", lambda kt=kt, j=j: nc.vector.tensor_mul(
                    ebuf[:, kt * 512:(kt + 1) * 512],
                    ebuf[:, kt * 512:(kt + 1) * 512],
                    C["maskc4"][:, j * 512:(j + 1) * 512]),
                    reads=[R(ebuf, kt * 512, (kt + 1) * 512),
                           R(C["maskc4"], j * 512, (j + 1) * 512)],
                    writes=[R(ebuf, kt * 512, (kt + 1) * 512)])
        emit_l(nkt - 1, nkt)
        arow, rarow = rows.slot(0, 0)
        if "DBGT" in c.dram and qb == 0:
            tk.emit("gq", lambda: nc.gpsimd.dma_start(
                out=c.dram["DBGT"][512:640, 0:2048], in_=ebuf[:, 0:2048]),
                reads=[R(ebuf, 0, 2048)])
            lrow = c.sb("dbglrow", [1, 512], F32)
            tk.emit("dve", lambda: nc.vector.tensor_copy(lrow[:, :], ps[4][0:1, 0:512]),
                    reads=[R(ps[4], 0, 512)], writes=[R(lrow)])
            tk.emit("sq", lambda: nc.sync.dma_start(
                out=c.dram["DBGT"][640:641, 0:512], in_=lrow[:, :]), reads=[R(lrow)])
        tk.emit("dve", lambda: nc.vector.reciprocal(arow, ps[4][0:1, 0:512]),
                reads=[R(ps[4], 0, 512)], writes=[rarow])
        tk.emit("pe", lambda: nc.tensor.matmul(
            ps[5][:, 0:512], C["ones_row_f"], arow, start=True, stop=True),
            reads=[rarow], writes=[R(ps[5], 0, 512)])
        tk.emit("act", lambda: nc.scalar.copy(aB[:, :], ps[5][:, 0:512]),
                reads=[R(ps[5], 0, 512)], writes=[R(aB)])
        for dvc in range(8):
            psO = ps[2 + dvc % 2]
            for kt in range(nkt):
                tk.emit("pe", lambda dvc=dvc, kt=kt, psO=psO: nc.tensor.matmul(
                    psO[:, 0:512], vtok[kt][:, dvc * 128:(dvc + 1) * 128],
                    ebuf[:, kt * 512:(kt + 1) * 512],
                    start=(kt == 0), stop=(kt == nkt - 1)),
                    reads=[R(vtok[kt], dvc * 128, (dvc + 1) * 128),
                           R(ebuf, kt * 512, (kt + 1) * 512)],
                    writes=[R(psO, 0, 512)])
            a0 = a0t[dvc % 2]
            tk.emit("dve", lambda psO=psO, a0=a0: nc.vector.tensor_mul(
                a0[:, :], psO[:, 0:512], aB[:, :]),
                reads=[R(psO, 0, 512), R(aB)], writes=[R(a0)])
            tk.emit("sq", lambda a0=a0, dvc=dvc, qb=qb: nc.sync.dma_start(
                out=c.dram["A0T"][dvc * 128:(dvc + 1) * 128, qb * 512:(qb + 1) * 512],
                in_=a0[:, :]),
                reads=[R(a0)])



def _tracker_barrier(self):
    vals = {}
    for e in self.eng:
        if self.instrs[e]:
            vals[e] = self._need(e, len(self.instrs[e]) - 1)
    for f in self.eng:
        for e, v in vals.items():
            if e == f:
                continue
            k = (f, e)
            if self.waited.get(k, -1) >= v:
                continue
            self.eng[f].wait_ge(self.sem[e], v)
            self.waited[k] = v

Tracker.barrier = _tracker_barrier

from contextlib import contextmanager

@contextmanager
def scope(c):
    st = ExitStack()
    old = c.stack
    c.stack = st
    try:
        yield
    finally:
        st.close()
        c.stack = old
        c.tk.barrier()


def reload_pack(c, bufs, dram_name, base=0):
    tk, nc = c.tk, c.nc
    for i, t in enumerate(bufs):
        tk.emit("gq", lambda t=t, i=i: nc.gpsimd.dma_start(
            out=t[:, :], in_=c.dram[dram_name][base + i]), writes=[R(t)])


def rope_tile(c, eng, src, s_lo, dst, dst_ap_cols, CS, SS, tb, tmpR):
    tk, nc = c.tk, c.nc
    E = nc.vector if eng == "dve" else nc.gpsimd
    c0, c1 = dst_ap_cols
    for (o, i) in ((0, 32), (32, 0), (64, 96), (96, 64)):
        tk.emit(eng, lambda o=o, i=i: E.tensor_mul(
            tmpR[o:o + 32, 0:512], src[i:i + 32, s_lo:s_lo + 512],
            SS[i:i + 32, tb:tb + 512]),
            reads=[R(src, s_lo, s_lo + 512), R(SS, tb, tb + 512)],
            writes=[R(tmpR, 0, 512)])
    tk.emit(eng, lambda: E.tensor_mul(
        dst[:, c0:c1], src[:, s_lo:s_lo + 512], CS[:, tb:tb + 512]),
        reads=[R(src, s_lo, s_lo + 512), R(CS, tb, tb + 512)],
        writes=[R(dst, c0, c1)])
    tk.emit(eng, lambda: E.tensor_add(dst[:, c0:c1], dst[:, c0:c1], tmpR[:, 0:512]),
            reads=[R(dst, c0, c1), R(tmpR, 0, 512)], writes=[R(dst, c0, c1)])


def shift_cast(c, y, hb, shift_from=4):
    tk, nc = c.tk, c.nc
    for ci in range(8):
        e = "dve" if ci % 2 == 0 else "pool"
        E = nc.vector if ci % 2 == 0 else nc.gpsimd
        if ci < shift_from:
            tk.emit(e, lambda ci=ci, E=E: E.tensor_copy(hb[ci][:, :], y[ci][:, :]),
                    reads=[R(y[ci])], writes=[R(hb[ci])])
        else:
            tk.emit(e, lambda ci=ci, E=E: E.tensor_copy(
                hb[ci][:, 1:2048], y[ci][:, 0:2047]),
                reads=[R(y[ci])], writes=[R(hb[ci])])
            tk.emit(e, lambda ci=ci, E=E: E.memset(hb[ci][:, 0:1], 0.0),
                    reads=[], writes=[R(hb[ci], 0, 1)])


def stage_l0_ffn(c, C):
    tk, nc = c.tk, c.nc
    ps = [c.ps(c.name("ps")) for _ in range(8)]
    y = mk_chunks(c, "y", F32)
    load_chunks_f32(c, "xnT", dst=y)
    rows = Rows(c, "l0f_rows", 1024)
    for l in range(2):
        p = f"l0f{l}_"
        layer_scope = scope(c)
        layer_scope.__enter__()
        qsr = load_f32(c, p + "qsr", [128, 64])
        ksr = load_f32(c, p + "ksr", [128, 64])
        t0, rt0 = rows.slot(0, 0, 64)
        m0, rm0 = rows.slot(0, 100, 1)
        tk.emit("dve", lambda: nc.vector.tensor_mul(t0, qsr[0:1, :], ksr[0:1, :]),
                reads=[R(qsr), R(ksr)], writes=[rt0])
        tk.emit("dve", lambda: nc.vector.reduce_max(
            m0, t0, axis=AX.X, apply_absolute_value=True),
            reads=[rt0], writes=[rm0])
        negc = c.sb(c.name("negc"), [128, 1], F32)
        tk.emit("pe", lambda: nc.tensor.matmul(ps[6][0:128, 0:1], C["ones_row_f"],
                                               m0, start=True, stop=True),
                reads=[rm0], writes=[R(ps[6], 0, 1)])
        tk.emit("act", lambda: nc.scalar.mul(negc[:, :], ps[6][0:128, 0:1], -8.0),
                reads=[R(ps[6], 0, 1)], writes=[R(negc)])
        qs_col = load_f32(c, p + "qs", [128, 1])
        ks_col = load_f32(c, p + "ks", [128, 1])
        qeff = mk_chunks(c, "qeff", BF16, n=4)
        keff = mk_chunks(c, "keff", BF16, n=4)
        vtok = [c.sb(c.name("vtok"), [128, 8, 65], BF16) for _ in range(16)]
        with scope(c):
            CSq = load_f32(c, "l0_CSq", [128, 2048])
            SSq = load_f32(c, "l0_SSq", [128, 2048])
            CSk = load_f32(c, "l0_CSk", [128, 2048])
            SSk = load_f32(c, "l0_SSk", [128, 2048])
            wq = load_pack_resident(c, p + "wq", 4, 128, 1024)
            wk = load_pack_resident(c, p + "wk", 4, 128, 1024)
            wv = load_pack_resident(c, p + "wv", 8, 128, 512)
            hb = mk_chunks(c, "hb", BF16, n=8, F=512)
            qraw = [c.sb(c.name("qraw"), [128, 512], BF16) for _ in range(2)]
            sqt = [c.sb(c.name("sqt"), [128, 512], BF16) for _ in range(2)]
            rln = c.sb(c.name("rln"), [2, 512], BF16)
            rBt = [c.sb(c.name("rBt"), [128, 512], BF16) for _ in range(2)]
            t0b = [c.sb(c.name("t0b"), [128, 512], BF16) for _ in range(2)]
            tmpR = [c.sb(c.name("tmpR"), [128, 512], F32) for _ in range(2)]
            for itb, tb in enumerate(TBS):
                for ci in range(8):
                    e = "dve" if ci % 2 == 0 else "pool"
                    E = nc.vector if ci % 2 == 0 else nc.gpsimd
                    if ci < 4:
                        tk.emit(e, lambda ci=ci, E=E, tb=tb: E.tensor_copy(
                            hb[ci][:, :], y[ci][:, tb:tb + 512]),
                            reads=[R(y[ci], tb, tb + 512)], writes=[R(hb[ci])])
                    elif tb > 0:
                        tk.emit(e, lambda ci=ci, E=E, tb=tb: E.tensor_copy(
                            hb[ci][:, :], y[ci][:, tb - 1:tb + 511]),
                            reads=[R(y[ci], tb - 1, tb + 511)], writes=[R(hb[ci])])
                    else:
                        tk.emit(e, lambda ci=ci, E=E: E.tensor_copy(
                            hb[ci][:, 1:512], y[ci][:, 0:511]),
                            reads=[R(y[ci], 0, 511)], writes=[R(hb[ci])])
                        tk.emit(e, lambda ci=ci, E=E: E.memset(hb[ci][:, 0:1], 0.0),
                                reads=[], writes=[R(hb[ci], 0, 1)])
                for (W, dst, scol, CSx, SSx) in ((wq, qeff, qs_col, CSq, SSq),
                                                 (wk, keff, ks_col, CSk, SSk)):
                    for oc in range(4):
                        ii = oc % 2
                        psq = ps[ii]
                        for dc in range(8):
                            tk.emit("pe", lambda W=W, oc=oc, dc=dc, psq=psq: nc.tensor.matmul(
                                psq[:, 0:512], W[oc][:, dc * 128:(dc + 1) * 128],
                                hb[dc][:, :], start=(dc == 0), stop=(dc == 7)),
                                reads=[(id(W[oc]), dc * 128, (dc + 1) * 128),
                                       R(hb[dc])],
                                writes=[R(psq, 0, 512)])
                        qr = qraw[ii]
                        tk.emit("dve", lambda psq=psq, qr=qr: nc.vector.tensor_copy(
                            qr[:, :], psq[:, 0:512]),
                            reads=[R(psq, 0, 512)], writes=[R(qr)])
                        sq = sqt[ii]
                        tk.emit("dve", lambda qr=qr, sq=sq: nc.vector.tensor_mul(
                            sq[:, :], qr[:, :], qr[:, :]),
                            reads=[R(qr)], writes=[R(sq)])
                        tk.emit("pe", lambda sq=sq: nc.tensor.matmul(
                            ps[2][0:2, 0:512], C["sel2"][:, :], sq[:, :],
                            start=True, stop=True),
                            reads=[R(sq)], writes=[R(ps[2], 0, 512)])
                        tk.emit("act", lambda: nc.scalar.activation(
                            rln[0:2, :], ps[2][0:2, 0:512], AF.Ln, bias=1e-24),
                            reads=[R(ps[2], 0, 512)], writes=[R(rln)])
                        tk.emit("act", lambda: nc.scalar.activation(
                            rln[0:2, :], rln[0:2, :], AF.Exp, scale=-0.5),
                            reads=[R(rln)], writes=[R(rln)])
                        tk.emit("pe", lambda: nc.tensor.matmul(
                            ps[3][:, 0:512], C["selB"][0:2, :], rln[0:2, :],
                            start=True, stop=True),
                            reads=[R(rln)], writes=[R(ps[3], 0, 512)])
                        rb = rBt[ii]
                        tk.emit("dve", lambda rb=rb: nc.vector.tensor_copy(
                            rb[:, :], ps[3][:, 0:512]),
                            reads=[R(ps[3], 0, 512)], writes=[R(rb)])
                        t0x = t0b[ii]
                        tk.emit("dve", lambda qr=qr, t0x=t0x, rb=rb, scol=scol: nc.vector.scalar_tensor_tensor(
                            t0x[:, :], qr[:, :], scol[:, :], rb[:, :],
                            ALU.mult, ALU.mult),
                            reads=[R(qr), R(scol), R(rb)], writes=[R(t0x)])
                        rope_tile(c, "pool" if ii else "dve", t0x, 0,
                                  dst[oc], (tb, tb + 512), CSx, SSx, tb, tmpR[ii])
                for tt in range(4 * itb, 4 * itb + 4):
                    lo = (tt - 4 * itb) * 128
                    tk.emit("dve", lambda tt=tt: nc.vector.memset(vtok[tt][:, :, :], 1.0),
                            writes=[R(vtok[tt])])
                    psv = ps[4 + tt % 2]
                    for dc in range(8):
                        tk.emit("pe", lambda tt=tt, dc=dc, psv=psv, lo=lo: nc.tensor.matmul(
                            psv[:, 0:512], hb[dc][:, lo:lo + 128],
                            wv[dc][:, :], start=(dc == 0), stop=(dc == 7)),
                            reads=[R(hb[dc], lo, lo + 128), (id(wv[dc]), 0, 512)],
                            writes=[R(psv, 0, 512)])
                    tk.emit("dve", lambda tt=tt, psv=psv: nc.vector.tensor_copy(
                        vtok[tt][:, :, 0:64],
                        psv[:, 0:512].rearrange("p (h d) -> p h d", h=8)),
                        reads=[R(psv, 0, 512)], writes=[R(vtok[tt])])
        with scope(c):
            OC = [c.sb(c.name("oc"), [128, 512], BF16) for _ in range(4)]
            ebuf = [c.sb(c.name("eb"), [128, 512], BF16) for _ in range(3)]
            aB = [c.sb(c.name("aB"), [128, 512], F32) for _ in range(2)]
            wo = load_pack_resident(c, p + "wo", 8, 128, 512)

            def _out(h, qb, psO, a):
                dst = OC[h // 2]
                b = 64 * (h % 2)
                tk.emit("dve", lambda dst=dst, b=b, psO=psO, a=a: nc.vector.tensor_mul(
                    dst[b:b + 64, :], psO[0:64, 0:512], a[0:64, :]),
                    reads=[R(psO, 0, 512), R(a)], writes=[R(dst)])

            def _ktl(qb):
                out = []
                for kt in range(max(0, 4 * qb - 4), 4 * qb + 4):
                    j = kt - 4 * qb + 4
                    out.append((kt, (C["mask8"], j * 512, (j + 1) * 512)))
                return out

            cfg = dict(nh=8,
                       qmap=lambda h: (qeff[h // 2], 64 * (h % 2)),
                       kmap=lambda h: (keff[h // 2], 64 * (h % 2)),
                       vsl=lambda kt, h: (vtok[kt][:, h, :], R(vtok[kt])),
                       ktlist=_ktl, scale=8.0, negc=negc,
                       psS=ps[0:2], psO=ps[2:5], psA=ps[5], rows=rows,
                       arows=Rows(c, c.name("arows"), 1024),
                       ebuf=ebuf, aB=aB, out=_out)
            for qb in range(4):
                cfg1 = dict(cfg)
                cfg1["ktlist"] = lambda q, qb=qb: _ktl(qb)
                _attn64_one_qb(c, C, cfg1, qb)
                for oc in range(8):
                    psW = ps[6 + oc % 2]
                    for dc in range(4):
                        tk.emit("pe", lambda oc=oc, dc=dc, psW=psW: nc.tensor.matmul(
                            psW[:, 0:512], wo[oc][:, dc * 128:(dc + 1) * 128],
                            OC[dc][:, :], start=(dc == 0), stop=(dc == 3)),
                            reads=[(id(wo[oc]), dc * 128, (dc + 1) * 128), R(OC[dc])],
                            writes=[R(psW, 0, 512)])
                    tk.emit("dve", lambda oc=oc, qb=qb, psW=psW: nc.vector.tensor_add(
                        y[oc][:, qb * 512:(qb + 1) * 512], psW[:, 0:512],
                        y[oc][:, qb * 512:(qb + 1) * 512]),
                        reads=[R(psW, 0, 512), R(y[oc], qb * 512, (qb + 1) * 512)],
                        writes=[R(y[oc], qb * 512, (qb + 1) * 512)])
        layer_scope.__exit__(None, None, None)
        with scope(c):
            hln = mk_chunks(c, "hln", BF16)
            with scope(c):
                h2b = mk_chunks(c, "h2b", BF16, n=8, F=512)
                lng = load_f32(c, p + "lng", [128, 8])
                lnb = load_f32(c, p + "lnb", [128, 8])
                L = LNState(c, ps[0:3])
                for itb, tb in enumerate(TBS):
                    for ci in range(8):
                        e = "dve" if ci % 2 == 0 else "pool"
                        E = nc.vector if ci % 2 == 0 else nc.gpsimd
                        if ci < 4:
                            tk.emit(e, lambda ci=ci, E=E, tb=tb: E.tensor_copy(
                                h2b[ci][:, :], y[ci][:, tb:tb + 512]),
                                reads=[R(y[ci], tb, tb + 512)], writes=[R(h2b[ci])])
                        elif tb > 0:
                            tk.emit(e, lambda ci=ci, E=E, tb=tb: E.tensor_copy(
                                h2b[ci][:, :], y[ci][:, tb - 1:tb + 511]),
                                reads=[R(y[ci], tb - 1, tb + 511)], writes=[R(h2b[ci])])
                        else:
                            tk.emit(e, lambda ci=ci, E=E: E.tensor_copy(
                                h2b[ci][:, 1:512], y[ci][:, 0:511]),
                                reads=[R(y[ci], 0, 511)], writes=[R(h2b[ci])])
                            tk.emit(e, lambda ci=ci, E=E: E.memset(h2b[ci][:, 0:1], 0.0),
                                    reads=[], writes=[R(h2b[ci], 0, 1)])
                    ln_tb(c, C, L, h2b, tb, [hln], gamma=lng, beta=lnb, src_tb=0)
            b1c = load_f32(c, p + "b1", [128, 44])
            b2c = load_f32(c, p + "b2", [128, 8])
            w1a = WStream(c, p + "w1", 22, 128, 1024, nbuf=3)
            w1g = WStream(c, p + "w1", 22, 128, 1024, nbuf=3, base=22)
            w2s = WStream(c, p + "w2", 8, 128, 2816, nbuf=3)
            h3 = [c.sb(c.name("h3"), [128, 1024], BF16) for _ in range(22)]
            at = [c.sb(c.name("at"), [128, 512], BF16) for _ in range(2)]
            gt = [c.sb(c.name("gt"), [128, 512], BF16) for _ in range(2)]
            for th in range(2):
                w1a.fetched = -1
                w1g.fetched = -1
                w2s.fetched = -1
                w1a.fetch(0); w1g.fetch(0); w1a.fetch(1); w1g.fetch(1)
                for i in range(22):
                    wa, wareg = w1a.get(i)
                    wg, wgreg = w1g.get(i)
                    for tb2 in range(2):
                        tb = th * 1024 + tb2 * 512
                        psA_, psG_ = ps[0 + tb2], ps[2 + tb2]
                        for dc in range(8):
                            tk.emit("pe", lambda wa=wa, dc=dc, tb=tb, psA_=psA_: nc.tensor.matmul(
                                psA_[:, 0:512], wa[:, dc * 128:(dc + 1) * 128],
                                hln[dc][:, tb:tb + 512], start=(dc == 0), stop=(dc == 7)),
                                reads=[(id(wa), dc * 128, (dc + 1) * 128),
                                       R(hln[dc], tb, tb + 512)],
                                writes=[R(psA_, 0, 512)])
                        for dc in range(8):
                            tk.emit("pe", lambda wg=wg, dc=dc, tb=tb, psG_=psG_: nc.tensor.matmul(
                                psG_[:, 0:512], wg[:, dc * 128:(dc + 1) * 128],
                                hln[dc][:, tb:tb + 512], start=(dc == 0), stop=(dc == 7)),
                                reads=[(id(wg), dc * 128, (dc + 1) * 128),
                                       R(hln[dc], tb, tb + 512)],
                                writes=[R(psG_, 0, 512)])
                        a_ = at[tb2]
                        g_ = gt[tb2]
                        tk.emit("act", lambda psA_=psA_, a_=a_, i=i: nc.scalar.add(
                            a_[:, :], psA_[:, 0:512], b1c[:, i:i + 1]),
                            reads=[R(psA_, 0, 512), R(b1c, i, i + 1)], writes=[R(a_)])
                        tk.emit("act", lambda psG_=psG_, g_=g_, i=i: nc.scalar.activation(
                            g_[:, :], psG_[:, 0:512], AF.Gelu, bias=b1c[:, 22 + i:23 + i]),
                            reads=[R(psG_, 0, 512), R(b1c, 22 + i, 23 + i)], writes=[R(g_)])
                        tk.emit("dve", lambda a_=a_, g_=g_, i=i, tb2=tb2: nc.vector.tensor_mul(
                            h3[i][:, tb2 * 512:(tb2 + 1) * 512], a_[:, :], g_[:, :]),
                            reads=[R(a_), R(g_)],
                            writes=[R(h3[i], tb2 * 512, (tb2 + 1) * 512)])
                    w1a.fetch(i + 2)
                    w1g.fetch(i + 2)
                w2s.fetch(0); w2s.fetch(1)
                for oc in range(8):
                    wb, wreg = w2s.get(oc)
                    for tb2 in range(2):
                        tb = th * 1024 + tb2 * 512
                        psY = ps[4 + (oc * 2 + tb2) % 2]
                        for i in range(22):
                            tk.emit("pe", lambda wb=wb, i=i, tb2=tb2, psY=psY: nc.tensor.matmul(
                                psY[:, 0:512], wb[:, i * 128:(i + 1) * 128],
                                h3[i][:, tb2 * 512:(tb2 + 1) * 512],
                                start=(i == 0), stop=(i == 21)),
                                reads=[(id(wb), i * 128, (i + 1) * 128),
                                       R(h3[i], tb2 * 512, (tb2 + 1) * 512)],
                                writes=[R(psY, 0, 512)])
                        tk.emit("dve", lambda oc=oc, tb=tb, psY=psY: nc.vector.scalar_tensor_tensor(
                            y[oc][:, tb:tb + 512], psY[:, 0:512], b2c[:, oc:oc + 1],
                            y[oc][:, tb:tb + 512], ALU.add, ALU.add),
                            reads=[R(psY, 0, 512), R(b2c, oc, oc + 1),
                                   R(y[oc], tb, tb + 512)],
                            writes=[R(y[oc], tb, tb + 512)])
                    w2s.fetch(oc + 2)
    store_chunks_f32(c, "F0T", y)


def _attn64_one_qb(c, C, cfg, qb):
    tk, nc = c.tk, c.nc
    negc = cfg["negc"]
    ktl = cfg["ktlist"](qb)
    qc0, qc1 = cfg.get("qcols", (qb * 512, (qb + 1) * 512))
    nkt = len(ktl)
    nheads = cfg["nh"]
    ebuf = cfg["ebuf"]
    ne = len(ebuf)

    def emit_S_exp(h, i):
        qt, q_base = cfg["qmap"](h)
        ktn, k_base = cfg["kmap"](h)
        kt, mask = ktl[i]
        psS = cfg["psS"][i % 2]
        tk.emit("pe", lambda: nc.tensor.matmul(
            psS[:, 0:512],
            ktn[k_base:k_base + 64, kt * 128:(kt + 1) * 128],
            qt[q_base:q_base + 64, qc0:qc1],
            start=True, stop=True),
            reads=[R(ktn, kt * 128, (kt + 1) * 128), R(qt, qc0, qc1)],
            writes=[R(psS, 0, 512)])
        E = ebuf[i % ne]
        tk.emit("act", lambda: nc.scalar.activation(
            E[:, :], psS[:, 0:512], AF.Exp, bias=negc[:, :],
            scale=float(cfg["scale"])),
            reads=[R(psS, 0, 512), R(negc)], writes=[R(E)])
        if mask is not None:
            mt, mlo, mhi = mask
            me = "dve" if i % 2 == 0 else "pool"
            Eng = nc.vector if i % 2 == 0 else nc.gpsimd
            tk.emit(me, lambda Eng=Eng: Eng.tensor_mul(
                E[:, :], E[:, :], mt[:, mlo:mlo + 512]),
                reads=[R(E), (id(mt), mlo, mhi)], writes=[R(E)])

    def emit_PV(h, i, psO):
        kt, mask = ktl[i]
        E = ebuf[i % ne]
        vap, vreg = cfg["vsl"](kt, h)
        tk.emit("pe", lambda: nc.tensor.matmul(
            psO[0:65, 0:512], vap, E[:, :],
            start=(i == 0), stop=(i == nkt - 1)),
            reads=[vreg, R(E)], writes=[R(psO, 0, 512)])

    arows = cfg.get("arows") or cfg["rows"]

    def emit_alpha(h, psO):
        arow, rarow = arows.slot(0, 512 * (h % 2))
        tk.emit("dve", lambda: nc.vector.reciprocal(arow, psO[64:65, 0:512]),
                reads=[R(psO, 0, 512)], writes=[rarow])
        psA = cfg["psA"][h % len(cfg["psA"])] if isinstance(cfg["psA"], list) \
            else cfg["psA"]
        tk.emit("pe", lambda: nc.tensor.matmul(
            psA[0:64, 0:512], C["ones_row_f"][0:1, 0:64], arow,
            start=True, stop=True),
            reads=[rarow], writes=[R(psA, 0, 512)])
        aB = cfg["aB"][h % 2]
        tk.emit("act", lambda: nc.scalar.copy(aB[0:64, :], psA[0:64, 0:512]),
                reads=[R(psA, 0, 512)], writes=[R(aB)])
        cfg["out"](h, qb, psO, aB)

    pend = None
    for h in range(nheads):
        psO = cfg["psO"][h % len(cfg["psO"])]
        for i in range(nkt):
            emit_S_exp(h, i)
            if i > 0:
                emit_PV(h, i - 1, psO)
        emit_PV(h, nkt - 1, psO)
        if pend is not None:
            emit_alpha(*pend)
        pend = (h, psO)
    emit_alpha(*pend)

def stage_mid(c, C):
    tk, nc = c.tk, c.nc
    ps = [c.ps(c.name("ps")) for _ in range(3)]
    x1 = mk_chunks(c, "x1", F32)
    load_chunks_f32(c, "xT", dst=x1)
    tmp = [c.sb(c.name("mtmp"), [128, 2048], F32) for _ in range(2)]
    for name in ("A0T", "F0T"):
        src = c.dram[name]
        for ci in range(8):
            t = tmp[ci % 2]
            tk.emit("sq", lambda ci=ci, t=t, src=src: nc.sync.dma_start(
                out=t[:, :], in_=src[ci * 128:(ci + 1) * 128, :]),
                writes=[R(t)])
            e = "dve" if ci % 2 == 0 else "pool"
            E = nc.vector if ci % 2 == 0 else nc.gpsimd
            tk.emit(e, lambda ci=ci, t=t, E=E: E.tensor_add(
                x1[ci][:, :], x1[ci][:, :], t[:, :]),
                reads=[R(x1[ci]), R(t)], writes=[R(x1[ci])])
    store_chunks_f32(c, "x1T", x1)
    xn1 = mk_chunks(c, "xn1", F32)
    L = LNState(c, ps)
    for tb in TBS:
        ln_tb(c, C, L, x1, tb, [xn1])
    store_chunks_f32(c, "xn1T", xn1)


def stage_fin(c, C):
    tk, nc = c.tk, c.nc
    acc = [c.sb(c.name("facc"), [128, 2048], F32) for _ in range(2)]
    tmp = [c.sb(c.name("ftmp"), [128, 2048], F32) for _ in range(2)]
    for ci in range(8):
        a = acc[ci % 2]
        tk.emit("sq", lambda ci=ci, a=a: nc.sync.dma_start(
            out=a[:, :], in_=c.dram["x1T"][ci * 128:(ci + 1) * 128, :]),
            writes=[R(a)])
        for name in ("A1T", "F1T"):
            t = tmp[ci % 2]
            tk.emit("sq", lambda ci=ci, t=t, name=name: nc.sync.dma_start(
                out=t[:, :], in_=c.dram[name][ci * 128:(ci + 1) * 128, :]),
                writes=[R(t)])
            e = "dve" if ci % 2 == 0 else "pool"
            E = nc.vector if ci % 2 == 0 else nc.gpsimd
            tk.emit(e, lambda a=a, t=t, E=E: E.tensor_add(a[:, :], a[:, :], t[:, :]),
                    reads=[R(a), R(t)], writes=[R(a)])
        tk.emit("sq", lambda ci=ci, a=a: nc.sync.dma_start(
            out=c.dram["outT"][ci * 128:(ci + 1) * 128, :], in_=a[:, :]),
            reads=[R(a)])


def stage_l1_attn(c, C):
    tk, nc = c.tk, c.nc
    ps = [c.ps(c.name("ps")) for _ in range(8)]
    rows = Rows(c, "l1a_rows", 1024)
    xn1b = mk_chunks(c, "xn1b", BF16)
    load_chunks_f32(c, "xn1T", cast_bf16=xn1b)
    CS = load_f32(c, "l1_CS", [128, 2048])
    SS = load_f32(c, "l1_SS", [128, 2048])
    wqa = load_pack_resident(c, "l1a_wqa", 1, 128, 512)[0]
    wkva = load_pack_resident(c, "l1a_wkva", 1, 128, 256)[0]
    wqb = load_pack_resident(c, "l1a_wqb", 8, 128, 128)
    wkrep = c.sb("wkrep", [128, 512], BF16)
    tk.emit("gq", lambda: nc.gpsimd.dma_start(out=wkrep[:, :], in_=c.dram["l1a_wk"][:, :]),
            writes=[R(wkrep)])
    wvp = c.sb("wvp", [128, 256], BF16)
    tk.emit("gq", lambda: nc.gpsimd.dma_start(out=wvp[:, :], in_=c.dram["l1a_wv"][:, :]),
            writes=[R(wvp)])
    t1T = c.sb("t1T", [64, 2048], BF16)
    t2T = c.sb("t2T", [32, 2048], BF16)
    for itb, tb in enumerate(TBS):
        ps0 = ps[itb % 2]
        for dc in range(8):
            tk.emit("pe", lambda dc=dc, tb=tb, ps0=ps0: nc.tensor.matmul(
                ps0[0:64, 0:512], wqa[:, dc * 64:(dc + 1) * 64],
                xn1b[dc][:, tb:tb + 512], start=(dc == 0), stop=(dc == 7)),
                reads=[(id(wqa), dc * 64, (dc + 1) * 64), R(xn1b[dc], tb, tb + 512)],
                writes=[R(ps0, 0, 512)])
        tk.emit("act", lambda tb=tb, ps0=ps0: nc.scalar.copy(
            t1T[:, tb:tb + 512], ps0[0:64, 0:512]),
            reads=[R(ps0, 0, 512)], writes=[R(t1T, tb, tb + 512)])
        ps1 = ps[2 + itb % 2]
        for dc in range(8):
            tk.emit("pe", lambda dc=dc, tb=tb, ps1=ps1: nc.tensor.matmul(
                ps1[0:32, 0:512], wkva[:, dc * 32:(dc + 1) * 32],
                xn1b[dc][:, tb:tb + 512], start=(dc == 0), stop=(dc == 7)),
                reads=[(id(wkva), dc * 32, (dc + 1) * 32), R(xn1b[dc], tb, tb + 512)],
                writes=[R(ps1, 0, 512)])
        tk.emit("act", lambda tb=tb, ps1=ps1: nc.scalar.copy(
            t2T[:, tb:tb + 512], ps1[0:32, 0:512]),
            reads=[R(ps1, 0, 512)], writes=[R(t2T, tb, tb + 512)])
    qeff = mk_chunks(c, "qeff", BF16, n=8)
    keff = mk_chunks(c, "keff", BF16, n=4)
    vtok = [c.sb(c.name("vtok"), [128, 4, 65], BF16) for _ in range(16)]
    qraw = [c.sb(c.name("qraw"), [128, 512], BF16) for _ in range(2)]
    tmpR = [c.sb(c.name("tmpR"), [128, 512], F32) for _ in range(2)]
    for oc in range(8):
        for itb, tb in enumerate(TBS):
            ii = (oc * 4 + itb) % 2
            psq = ps[ii]
            tk.emit("pe", lambda oc=oc, tb=tb, psq=psq: nc.tensor.matmul(
                psq[:, 0:512], wqb[oc][0:64, :], t1T[:, tb:tb + 512],
                start=True, stop=True),
                reads=[(id(wqb[oc]), 0, 128), R(t1T, tb, tb + 512)],
                writes=[R(psq, 0, 512)])
            qr = qraw[ii]
            tk.emit("dve", lambda psq=psq, qr=qr: nc.vector.tensor_copy(qr[:, :], psq[:, 0:512]),
                    reads=[R(psq, 0, 512)], writes=[R(qr)])
            rope_tile(c, "pool" if ii else "dve", qr, 0, qeff[oc],
                      (tb, tb + 512), CS, SS, tb, tmpR[ii])
    for oc in range(4):
        for itb, tb in enumerate(TBS):
            ii = (oc * 4 + itb) % 2
            psq = ps[2 + ii]
            tk.emit("pe", lambda oc=oc, tb=tb, psq=psq: nc.tensor.matmul(
                psq[:, 0:512], wkrep[0:32, oc * 128:(oc + 1) * 128],
                t2T[:, tb:tb + 512], start=True, stop=True),
                reads=[R(wkrep, oc * 128, (oc + 1) * 128), R(t2T, tb, tb + 512)],
                writes=[R(psq, 0, 512)])
            qr = qraw[ii]
            tk.emit("dve", lambda psq=psq, qr=qr: nc.vector.tensor_copy(qr[:, :], psq[:, 0:512]),
                    reads=[R(psq, 0, 512)], writes=[R(qr)])
            rope_tile(c, "pool" if ii else "dve", qr, 0, keff[oc],
                      (tb, tb + 512), CS, SS, tb, tmpR[ii])
    for tt in range(16):
        tk.emit("dve", lambda tt=tt: nc.vector.memset(vtok[tt][:, :, :], 1.0),
                writes=[R(vtok[tt])])
        psv = ps[4 + tt % 2]
        tk.emit("pe", lambda tt=tt, psv=psv: nc.tensor.matmul(
            psv[:, 0:256], t2T[:, tt * 128:(tt + 1) * 128], wvp[0:32, :],
            start=True, stop=True),
            reads=[R(t2T, tt * 128, (tt + 1) * 128), R(wvp)],
            writes=[R(psv, 0, 256)])
        tk.emit("dve", lambda tt=tt, psv=psv: nc.vector.tensor_copy(
            vtok[tt][:, :, 0:64],
            psv[:, 0:256].rearrange("p (h d) -> p h d", h=4)),
            reads=[R(psv, 0, 256)], writes=[R(vtok[tt])])
    sqt = [c.sb(c.name("sqt"), [128, 512], BF16) for _ in range(2)]
    negc = negc_from_norms(c, C, [ps[6]], rows, qeff, keff, 1.0 / 8.0, sqt)
    if "DBGT" in c.dram:
        tk.emit("gq", lambda: nc.gpsimd.dma_start(
            out=c.dram["DBGT"][0:128, 0:2048], in_=qeff[0][:, :]), reads=[R(qeff[0])])
        tk.emit("gq", lambda: nc.gpsimd.dma_start(
            out=c.dram["DBGT"][128:256, 0:2048], in_=keff[0][:, :]), reads=[R(keff[0])])
        tk.emit("gq", lambda: nc.gpsimd.dma_start(
            out=c.dram["DBGT"][256:384, 0:2048], in_=qeff[7][:, :]), reads=[R(qeff[7])])
        tk.emit("sq", lambda: nc.sync.dma_start(
            out=c.dram["DBGT"][384:512, 0:1], in_=negc[:, :],
            allow_slow_non_contiguous=True), reads=[R(negc)])
        tk.emit("gq", lambda: nc.gpsimd.dma_start(
            out=c.dram["DBGT"][512:576, 0:2048], in_=t1T[:, :]), reads=[R(t1T)])
        tk.emit("gq", lambda: nc.gpsimd.dma_start(
            out=c.dram["DBGT"][576:704, 0:260], in_=vtok[0][:, :, :]), reads=[R(vtok[0])])
    octx = mk_chunks(c, "octx", BF16, n=8, F=512)
    ebuf = [c.sb(c.name("eb"), [128, 512], BF16) for _ in range(3)]
    aBs = [c.sb(c.name("aB"), [128, 512], F32) for _ in range(2)]
    wout = load_pack_resident(c, "l1a_wout", 8, 128, 1024)
    boutc = load_f32(c, "l1a_bout", [128, 8])
    a1t = [c.sb(c.name("a1t"), [128, 512], F32) for _ in range(2)]

    def _out(h, qb, psO, a):
        dst = octx[h // 2]
        b = 64 * (h % 2)
        tk.emit("dve", lambda dst=dst, b=b, psO=psO, a=a: nc.vector.tensor_mul(
            dst[b:b + 64, :], psO[0:64, 0:512], a[0:64, :]),
            reads=[R(psO, 0, 512), R(a)], writes=[R(dst)])

    cfg = dict(nh=16,
               qmap=lambda h: (qeff[h // 2], 64 * (h % 2)),
               kmap=lambda h: (keff[h // 4], 64 * (h % 2)),
               vsl=lambda kt, h: (vtok[kt][:, h // 4, :], R(vtok[kt])),
               scale=1.0 / 8.0, negc=negc,
               psS=ps[0:2], psO=ps[2:5], psA=ps[5], rows=rows,
               arows=Rows(c, c.name("arows"), 1024),
               ebuf=ebuf, aB=aBs, out=_out)
    for qb in range(4):
        cfg["ktlist"] = lambda q, qb=qb: [
            (kt, (C["maskc4"], (kt - 4 * qb) * 512, (kt - 4 * qb + 1) * 512)
             if kt >= 4 * qb else None)
            for kt in range(4 * qb + 4)]
        _attn64_one_qb(c, C, cfg, qb)
        for oc in range(8):
            psW = ps[6 + oc % 2]
            for dc in range(8):
                tk.emit("pe", lambda oc=oc, dc=dc, psW=psW: nc.tensor.matmul(
                    psW[:, 0:512], wout[oc][:, dc * 128:(dc + 1) * 128],
                    octx[dc][:, :], start=(dc == 0), stop=(dc == 7)),
                    reads=[(id(wout[oc]), dc * 128, (dc + 1) * 128), R(octx[dc])],
                    writes=[R(psW, 0, 512)])
            a1 = a1t[oc % 2]
            tk.emit("act", lambda oc=oc, psW=psW, a1=a1: nc.scalar.add(
                a1[:, :], psW[:, 0:512], boutc[:, oc:oc + 1]),
                reads=[R(psW, 0, 512), R(boutc, oc, oc + 1)], writes=[R(a1)])
            tk.emit("sq", lambda oc=oc, qb=qb, a1=a1: nc.sync.dma_start(
                out=c.dram["A1T"][oc * 128:(oc + 1) * 128, qb * 512:(qb + 1) * 512],
                in_=a1[:, :]),
                reads=[R(a1)])


def stage_l1_ffn(c, C):
    tk, nc = c.tk, c.nc
    ps = [c.ps(c.name("ps")) for _ in range(8)]
    rows = Rows(c, "l1f_rows", 1024)
    aft = c.nc.dram_tensor("AFt", [1024, 2048], F32, kind="Internal").ap()
    with scope(c):
        CS = load_f32(c, "l1_CS", [128, 2048])
        SS = load_f32(c, "l1_SS", [128, 2048])
        xb = mk_chunks(c, "xb", BF16, n=8, F=512)
        qin = mk_chunks(c, "qin", BF16, n=8, F=512)
        qe = mk_chunks(c, "qe", BF16, n=8, F=512)
        ke = mk_chunks(c, "ke", BF16, n=8)
        vtok = [c.sb(c.name("vtok"), [128, 16, 65], BF16) for _ in range(16)]
        qraw = [c.sb(c.name("qraw"), [128, 512], BF16) for _ in range(2)]
        tmpR = [c.sb(c.name("tmpR"), [128, 512], F32) for _ in range(2)]
        sqt = [c.sb(c.name("sqt"), [128, 512], BF16) for _ in range(2)]
        octx = mk_chunks(c, "octx", BF16, n=8, F=512)
        ebuf = [c.sb(c.name("eb"), [128, 512], BF16) for _ in range(3)]
        aBs = [c.sb(c.name("aB"), [128, 512], F32) for _ in range(2)]
        negc = c.sb("negcl1f", [128, 1], F32)
        arows_l1f = Rows(c, c.name("arows"), 1024)
        wqs = WStream(c, "l1f_wq", 8, 128, 1024, nbuf=2)
        wks = WStream(c, "l1f_wk", 8, 128, 1024, nbuf=2)
        wos = WStream(c, "l1f_wo", 8, 128, 1024, nbuf=2)
        bqc = load_f32(c, "l1f_bq", [128, 8])
        bkc = load_f32(c, "l1f_bk", [128, 8])
        boc = load_f32(c, "l1f_bo", [128, 8])
        wvr = load_pack_resident(c, "l1f_wv", 8, 128, 1024)
        bvrow = c.sb("bvrow", [128, 1024], BF16)
        tk.emit("gq", lambda: nc.gpsimd.dma_start(out=bvrow[:, :], in_=c.dram["l1f_bv"][:, :]),
                writes=[R(bvrow)])
        kmax, rkmax = rows.slot(0, 520, 1)
        qmax, rqmax = rows.slot(0, 521, 1)
        cc, rcc = rows.slot(0, 522, 1)
        tmpm, rtmpm = rows.slot(0, 523, 1)

        def _norm_max(tiles, cols, dst, rdst, first):
            psN = ps[6]
            for i, qt in enumerate(tiles):
                sq = sqt[i % 2]
                tk.emit("act", lambda qt=qt, sq=sq: nc.scalar.activation(
                    sq[:, :], qt[:, cols[0]:cols[1]], AF.Square),
                    reads=[R(qt, cols[0], cols[1])], writes=[R(sq)])
                tk.emit("pe", lambda sq=sq, i=i, n=len(tiles): nc.tensor.matmul(
                    psN[0:1, 0:512], C["ones_col_b"], sq[:, :],
                    start=(i == 0), stop=(i == n - 1)),
                    reads=[R(sq)], writes=[R(psN, 0, 512)])
            tk.emit("dve", lambda: nc.vector.reduce_max(tmpm, psN[0:1, 0:512], axis=AX.X),
                    reads=[R(psN, 0, 512)], writes=[rtmpm])
            if first:
                tk.emit("dve", lambda: nc.vector.tensor_copy(dst, tmpm),
                        reads=[rtmpm], writes=[rdst])
            else:
                tk.emit("dve", lambda: nc.vector.tensor_max(dst, dst, tmpm),
                        reads=[rdst, rtmpm], writes=[rdst])

        for itb, tb in enumerate(TBS):
            qb = itb
            for dc in range(8):
                tk.emit("gq", lambda dc=dc, tb=tb: nc.gpsimd.dma_start(
                    out=xb[dc][:, :],
                    in_=c.dram["xn1T"][dc * 128:(dc + 1) * 128, tb:tb + 512]),
                    writes=[R(xb[dc])])
            for dc in range(8):
                ii = dc % 2
                psq = ps[ii]
                tk.emit("pe", lambda dc=dc, psq=psq: nc.tensor.matmul(
                    psq[:, 0:512], C["deintb"][:, :], xb[dc][:, :],
                    start=True, stop=True),
                    reads=[R(C["deintb"]), R(xb[dc])], writes=[R(psq, 0, 512)])
                qr = qraw[ii]
                tk.emit("dve", lambda psq=psq, qr=qr: nc.vector.tensor_copy(qr[:, :], psq[:, 0:512]),
                        reads=[R(psq, 0, 512)], writes=[R(qr)])
                rope_tile(c, "pool" if ii else "dve", qr, 0, qin[dc],
                          (0, 512), CS, SS, tb, tmpR[ii])
            for (W, dst, bc, full) in ((wqs, qe, bqc, False), (wks, ke, bkc, True)):
                W.fetched = -1
                W.fetch(0); W.fetch(1)
                for oc in range(8):
                    wb, wreg = W.get(oc)
                    psq = ps[oc % 2]
                    for dc in range(8):
                        tk.emit("pe", lambda wb=wb, dc=dc, psq=psq: nc.tensor.matmul(
                            psq[:, 0:512], wb[:, dc * 128:(dc + 1) * 128],
                            qin[dc][:, :], start=(dc == 0), stop=(dc == 7)),
                            reads=[(id(wb), dc * 128, (dc + 1) * 128), R(qin[dc])],
                            writes=[R(psq, 0, 512)])
                    c0, c1 = (tb, tb + 512) if full else (0, 512)
                    tk.emit("act", lambda psq=psq, dst=dst, oc=oc, bc=bc, c0=c0, c1=c1: nc.scalar.add(
                        dst[oc][:, c0:c1], psq[:, 0:512], bc[:, oc:oc + 1]),
                        reads=[R(psq, 0, 512), R(bc, oc, oc + 1)],
                        writes=[R(dst[oc], c0, c1)])
                    W.fetch(oc + 2)
            for tt in range(4 * itb, 4 * itb + 4):
                tk.emit("dve", lambda tt=tt: nc.vector.memset(vtok[tt][:, :, :], 1.0),
                        writes=[R(vtok[tt])])
                lo = (tt - 4 * itb) * 128
                for vh in range(2):
                    psv = ps[4 + (tt * 2 + vh) % 2]
                    for dc in range(8):
                        tk.emit("pe", lambda tt=tt, vh=vh, dc=dc, psv=psv, lo=lo: nc.tensor.matmul(
                            psv[:, 0:512], xb[dc][:, lo:lo + 128],
                            wvr[dc][:, vh * 512:(vh + 1) * 512],
                            start=(dc == 0), stop=False),
                            reads=[R(xb[dc], lo, lo + 128),
                                   (id(wvr[dc]), vh * 512, (vh + 1) * 512)],
                            writes=[R(psv, 0, 512)])
                    tk.emit("pe", lambda vh=vh, psv=psv: nc.tensor.matmul(
                        psv[:, 0:512], C["ones_row_b"], bvrow[0:1, vh * 512:(vh + 1) * 512],
                        start=False, stop=True),
                        reads=[R(bvrow, vh * 512, (vh + 1) * 512)],
                        writes=[R(psv, 0, 512)])
                    tk.emit("dve", lambda tt=tt, vh=vh, psv=psv: nc.vector.tensor_copy(
                        vtok[tt][:, vh * 8:(vh + 1) * 8, 0:64],
                        psv[:, 0:512].rearrange("p (h d) -> p h d", h=8)),
                        reads=[R(psv, 0, 512)], writes=[R(vtok[tt])])
            _norm_max(ke, (tb, tb + 512), kmax, rkmax, first=(itb == 0))
            _norm_max(qe, (0, 512), qmax, rqmax, first=True)
            tk.emit("dve", lambda: nc.vector.tensor_mul(cc, qmax, kmax),
                    reads=[rqmax, rkmax], writes=[rcc])
            tk.emit("act", lambda: nc.scalar.activation(cc, cc, AF.Ln, bias=1e-24),
                    reads=[rcc], writes=[rcc])
            tk.emit("act", lambda: nc.scalar.activation(cc, cc, AF.Exp, scale=0.5),
                    reads=[rcc], writes=[rcc])
            tk.emit("pe", lambda: nc.tensor.matmul(ps[6][0:128, 0:1], C["ones_row_f"],
                                                   cc, start=True, stop=True),
                    reads=[rcc], writes=[R(ps[6], 0, 1)])
            tk.emit("act", lambda: nc.scalar.mul(negc[:, :], ps[6][0:128, 0:1], -1.0 / 8.0),
                    reads=[R(ps[6], 0, 1)], writes=[R(negc)])
            cfg = dict(nh=16,
                       qmap=lambda h: (qe[h // 2], 64 * (h % 2)),
                       kmap=lambda h: (ke[h // 2], 64 * (h % 2)),
                       vsl=lambda kt, h: (vtok[kt][:, h, :], R(vtok[kt])),
                       scale=1.0 / 8.0, negc=negc,
                       psS=ps[0:2], psO=ps[2:5], psA=ps[5], rows=rows,
                       arows=arows_l1f,
                       ebuf=ebuf, aB=aBs, out=None,
                       ktlist=lambda q: [
                           (kt, (C["maskc4"], (kt - 4 * qb) * 512, (kt - 4 * qb + 1) * 512)
                            if kt >= 4 * qb else None)
                           for kt in range(4 * qb + 4)],
                       qcols=(0, 512))

            def _out(h, q_, psO, a):
                dst = octx[h // 2]
                b = 64 * (h % 2)
                tk.emit("dve", lambda dst=dst, b=b, psO=psO, a=a: nc.vector.tensor_mul(
                    dst[b:b + 64, :], psO[0:64, 0:512], a[0:64, :]),
                    reads=[R(psO, 0, 512), R(a)], writes=[R(dst)])
            cfg["out"] = _out
            _attn64_one_qb(c, C, cfg, qb)
            wos.fetched = -1
            wos.fetch(0); wos.fetch(1)
            aoutt = [c.sb(c.name("aoutt"), [128, 512], F32) for _ in range(2)] \
                if itb == 0 else aoutt_hold
            aoutt_hold = aoutt
            for oc in range(8):
                wb, wreg = wos.get(oc)
                psW = ps[6 + oc % 2]
                for dc in range(8):
                    tk.emit("pe", lambda wb=wb, dc=dc, psW=psW: nc.tensor.matmul(
                        psW[:, 0:512], wb[:, dc * 128:(dc + 1) * 128],
                        octx[dc][:, :], start=(dc == 0), stop=(dc == 7)),
                        reads=[(id(wb), dc * 128, (dc + 1) * 128), R(octx[dc])],
                        writes=[R(psW, 0, 512)])
                ao = aoutt[oc % 2]
                tk.emit("act", lambda psW=psW, ao=ao, oc=oc: nc.scalar.add(
                    ao[:, :], psW[:, 0:512], boc[:, oc:oc + 1]),
                    reads=[R(psW, 0, 512), R(boc, oc, oc + 1)], writes=[R(ao)])
                tk.emit("sq", lambda ao=ao, oc=oc, tb=tb: nc.sync.dma_start(
                    out=aft[oc * 128:(oc + 1) * 128, tb:tb + 512], in_=ao[:, :]),
                    reads=[R(ao)])
                wos.fetch(oc + 2)
    x1fb = mk_chunks(c, "x1fb", BF16)
    with scope(c):
        xn1 = mk_chunks(c, "z", F32)
        load_chunks_f32(c, "xn1T", dst=xn1)
        tmp = [c.sb(c.name("ztmp"), [128, 2048], F32) for _ in range(2)]
        for ci in range(8):
            t = tmp[ci % 2]
            tk.emit("sq", lambda ci=ci, t=t: nc.sync.dma_start(
                out=t[:, :], in_=aft[ci * 128:(ci + 1) * 128, :]),
                writes=[R(t)])
            e = "dve" if ci % 2 == 0 else "pool"
            E = nc.vector if ci % 2 == 0 else nc.gpsimd
            tk.emit(e, lambda ci=ci, t=t, E=E: E.tensor_add(
                xn1[ci][:, :], xn1[ci][:, :], t[:, :]),
                reads=[R(xn1[ci]), R(t)], writes=[R(xn1[ci])])
        ln1w = load_f32(c, "l1f_ln1w", [128, 8])
        ln1b = load_f32(c, "l1f_ln1b", [128, 8])
        L = LNState(c, ps[0:3])
        for tb in TBS:
            ln_tb(c, C, L, xn1, tb, [xn1, x1fb], gamma=ln1w, beta=ln1b)
        for ci in range(8):
            tk.emit("sq", lambda ci=ci: nc.sync.dma_start(
                out=aft[ci * 128:(ci + 1) * 128, :], in_=xn1[ci][:, :]),
                reads=[R(xn1[ci])])
    glog = c.sb("glog", [8, 2048], F32)
    w8all = c.sb("w8all", [128, 128], F32)
    wrowb = c.sb("wrowb", [8, 2048], BF16)
    with scope(c):
        gw = load_pack_resident(c, "l1f_gw", 8, 128, 8)
        gb = load_f32(c, "l1f_gb", [128, 1])
        identf = load_f32(c, "ident", [128, 128])
        for itb, tb in enumerate(TBS):
            psG = ps[itb % 2]
            for dc in range(8):
                tk.emit("pe", lambda dc=dc, tb=tb, psG=psG: nc.tensor.matmul(
                    psG[0:8, 0:512], gw[dc][:, :], x1fb[dc][:, tb:tb + 512],
                    start=(dc == 0), stop=(dc == 7)),
                    reads=[(id(gw[dc]), 0, 8), R(x1fb[dc], tb, tb + 512)],
                    writes=[R(psG, 0, 512)])
            tk.emit("act", lambda tb=tb, psG=psG: nc.scalar.add(
                glog[0:8, tb:tb + 512], psG[0:8, 0:512], gb[0:8, 0:1]),
                reads=[R(psG, 0, 512), R(gb)], writes=[R(glog, tb, tb + 512)])
        g8 = c.sb("g8", [128, 8], F32)
        eqt = c.sb("eqt", [128, 8], F32)
        gtmp = c.sb("gtmp", [128, 4], F32)
        for tt in range(16):
            psT = ps[2 + tt % 2]
            tk.emit("pe", lambda tt=tt, psT=psT: nc.tensor.transpose(
                psT[0:128, 0:8], glog[0:8, tt * 128:(tt + 1) * 128],
                identf[0:8, 0:8]),
                reads=[R(glog, tt * 128, (tt + 1) * 128), R(identf)],
                writes=[R(psT, 0, 8)])
            v = nc.vector
            tk.emit("dve", lambda psT=psT: v.tensor_copy(g8[:, :], psT[0:128, 0:8]),
                    reads=[R(psT, 0, 8)], writes=[R(g8)])
            tk.emit("dve", lambda: v.reduce_max(gtmp[:, 0:1], g8[:, :], axis=AX.X),
                    reads=[R(g8)], writes=[R(gtmp, 0, 1)])
            tk.emit("dve", lambda: v.tensor_scalar_mul(gtmp[:, 1:2], gtmp[:, 0:1], -1.0),
                    reads=[R(gtmp, 0, 1)], writes=[R(gtmp, 1, 2)])
            tk.emit("act", lambda: nc.scalar.activation(
                g8[:, :], g8[:, :], AF.Exp, bias=gtmp[:, 1:2]),
                reads=[R(g8), R(gtmp, 1, 2)], writes=[R(g8)])
            tk.emit("dve", lambda: v.reduce_sum(gtmp[:, 2:3], g8[:, :], axis=AX.X),
                    reads=[R(g8)], writes=[R(gtmp, 2, 3)])
            tk.emit("dve", lambda: v.reciprocal(gtmp[:, 3:4], gtmp[:, 2:3]),
                    reads=[R(gtmp, 2, 3)], writes=[R(gtmp, 3, 4)])
            tk.emit("dve", lambda: v.tensor_scalar_mul(g8[:, :], g8[:, :], gtmp[:, 3:4]),
                    reads=[R(g8), R(gtmp, 3, 4)], writes=[R(g8)])
            tk.emit("dve", lambda: v.reduce_max(gtmp[:, 0:1], g8[:, :], axis=AX.X),
                    reads=[R(g8)], writes=[R(gtmp, 0, 1)])
            tk.emit("dve", lambda: v.tensor_single_scalar(
                eqt[:, :], g8[:, :], gtmp[:, 0:1], ALU.is_ge),
                reads=[R(g8), R(gtmp, 0, 1)], writes=[R(eqt)])
            tk.emit("dve", lambda: v.scalar_tensor_tensor(
                eqt[:, :], eqt[:, :], -2.0, g8[:, :], ALU.mult, ALU.add),
                reads=[R(eqt), R(g8)], writes=[R(eqt)])
            tk.emit("dve", lambda: v.reduce_max(gtmp[:, 1:2], eqt[:, :], axis=AX.X),
                    reads=[R(eqt)], writes=[R(gtmp, 1, 2)])
            tk.emit("dve", lambda: v.tensor_single_scalar(
                eqt[:, :], g8[:, :], gtmp[:, 1:2], ALU.is_ge),
                reads=[R(g8), R(gtmp, 1, 2)], writes=[R(eqt)])
            tk.emit("dve", lambda tt=tt: v.tensor_mul(
                w8all[:, tt * 8:(tt + 1) * 8], g8[:, :], eqt[:, :]),
                reads=[R(g8), R(eqt)], writes=[R(w8all, tt * 8, (tt + 1) * 8)])
            psT2 = ps[4 + tt % 2]
            tk.emit("pe", lambda tt=tt, psT2=psT2: nc.tensor.transpose(
                psT2[0:8, 0:128], w8all[:, tt * 8:(tt + 1) * 8], identf[:, :]),
                reads=[R(w8all, tt * 8, (tt + 1) * 8), R(identf)],
                writes=[R(psT2, 0, 128)])
            tk.emit("act", lambda tt=tt, psT2=psT2: nc.scalar.copy(
                wrowb[0:8, tt * 128:(tt + 1) * 128], psT2[0:8, 0:128]),
                reads=[R(psT2, 0, 128)], writes=[R(wrowb, tt * 128, (tt + 1) * 128)])
    macc = mk_chunks(c, "macc", F32)
    with scope(c):
        sel8 = c.sb("sel8_s", [128, 1024], BF16)
        tk.emit("gq", lambda: nc.gpsimd.dma_start(out=sel8[:, :], in_=c.dram["sel8"][:, :]),
                writes=[R(sel8)])
        ebc = load_f32(c, "l1f_eb", [128, 64])
        ewS = WStream(c, "l1f_ew", 64, 128, 1024, nbuf=4)
        wBfull = c.sb("wBfull", [128, 2048], BF16)
        tmpE = [c.sb(c.name("tmpE"), [128, 512], BF16) for _ in range(2)]
        ewS.fetch(0); ewS.fetch(1); ewS.fetch(2); ewS.fetch(3)
        for e in range(8):
            for itb, tb in enumerate(TBS):
                psB = ps[6]
                tk.emit("pe", lambda e=e, tb=tb, psB=psB: nc.tensor.matmul(
                    psB[:, 0:512], sel8[0:8, e * 128:(e + 1) * 128],
                    wrowb[0:8, tb:tb + 512], start=True, stop=True),
                    reads=[R(sel8, e * 128, (e + 1) * 128), R(wrowb, tb, tb + 512)],
                    writes=[R(psB, 0, 512)])
                tk.emit("dve", lambda tb=tb, psB=psB: nc.vector.tensor_copy(
                    wBfull[:, tb:tb + 512], psB[:, 0:512]),
                    reads=[R(psB, 0, 512)], writes=[R(wBfull, tb, tb + 512)])
            for oc in range(8):
                wb, wreg = ewS.get(e * 8 + oc)
                for itb, tb in enumerate(TBS):
                    psE = ps[(oc * 4 + itb) % 2]
                    for dc in range(8):
                        tk.emit("pe", lambda wb=wb, dc=dc, tb=tb, psE=psE: nc.tensor.matmul(
                            psE[:, 0:512], wb[:, dc * 128:(dc + 1) * 128],
                            x1fb[dc][:, tb:tb + 512], start=(dc == 0), stop=(dc == 7)),
                            reads=[(id(wb), dc * 128, (dc + 1) * 128),
                                   R(x1fb[dc], tb, tb + 512)],
                            writes=[R(psE, 0, 512)])
                    col = e * 8 + oc
                    if e == 0:
                        tk.emit("dve", lambda oc=oc, tb=tb, psE=psE, col=col: nc.vector.scalar_tensor_tensor(
                            macc[oc][:, tb:tb + 512], psE[:, 0:512], ebc[:, col:col + 1],
                            wBfull[:, tb:tb + 512], ALU.add, ALU.mult),
                            reads=[R(psE, 0, 512), R(ebc, col, col + 1),
                                   R(wBfull, tb, tb + 512)],
                            writes=[R(macc[oc], tb, tb + 512)])
                    else:
                        tE = tmpE[itb % 2]
                        tk.emit("dve", lambda tE=tE, psE=psE, col=col, tb=tb: nc.vector.scalar_tensor_tensor(
                            tE[:, :], psE[:, 0:512], ebc[:, col:col + 1],
                            wBfull[:, tb:tb + 512], ALU.add, ALU.mult),
                            reads=[R(psE, 0, 512), R(ebc, col, col + 1),
                                   R(wBfull, tb, tb + 512)],
                            writes=[R(tE)])
                        tk.emit("pool", lambda oc=oc, tb=tb, tE=tE: nc.gpsimd.tensor_add(
                            macc[oc][:, tb:tb + 512], macc[oc][:, tb:tb + 512], tE[:, :]),
                            reads=[R(macc[oc], tb, tb + 512), R(tE)],
                            writes=[R(macc[oc], tb, tb + 512)])
                ewS.fetch(e * 8 + oc + 4)
    with scope(c):
        tmp = [c.sb(c.name("z2tmp"), [128, 2048], F32) for _ in range(2)]
        for ci in range(8):
            t = tmp[ci % 2]
            tk.emit("sq", lambda ci=ci, t=t: nc.sync.dma_start(
                out=t[:, :], in_=aft[ci * 128:(ci + 1) * 128, :]),
                writes=[R(t)])
            e = "dve" if ci % 2 == 0 else "pool"
            E = nc.vector if ci % 2 == 0 else nc.gpsimd
            tk.emit(e, lambda ci=ci, t=t, E=E: E.tensor_add(
                macc[ci][:, :], macc[ci][:, :], t[:, :]),
                reads=[R(macc[ci]), R(t)], writes=[R(macc[ci])])
    with scope(c):
        ln2w = load_f32(c, "l1f_ln2w", [128, 8])
        ln2b = load_f32(c, "l1f_ln2b", [128, 8])
        L = LNState(c, ps[0:3])
        for tb in TBS:
            ln_tb(c, C, L, macc, tb, [macc], gamma=ln2w, beta=ln2b)
    store_chunks_f32(c, "F1T", macc)



_L0F_W = []
for _l in range(2):
    _L0F_W += [f"l0f{_l}_{s}" for s in
               ("wq", "wk", "wv", "wo", "qs", "ks", "qsr", "ksr",
                "lng", "lnb", "w1", "b1", "w2", "b2")]

STAGES = {
    "ln0": (stage_ln0, ["xT"], (), ["xnT"]),
    "l0attn": (stage_l0_attn, ["xnT", "l0a_qw", "l0a_kw", "l0a_vw", "maskc4"],
               ("maskc4",), ["A0T"]),
    "l0ffn": (stage_l0_ffn,
              ["xnT", "mask8", "sel2", "selB",
               "l0_CSq", "l0_SSq", "l0_CSk", "l0_SSk"] + _L0F_W,
              ("mask8", "sel2"), ["F0T"]),
    "mid": (stage_mid, ["xT", "A0T", "F0T"], (), ["x1T", "xn1T"]),
    "l1attn": (stage_l1_attn,
               ["xn1T", "l1a_wqa", "l1a_wqb", "l1a_wkva", "l1a_wk", "l1a_wv",
                "l1a_wout", "l1a_bout", "l1_CS", "l1_SS", "maskc4"],
               ("maskc4",), ["A1T"]),
    "l1ffn": (stage_l1_ffn,
              ["xn1T", "l1_CS", "l1_SS", "maskc4", "deint", "ident", "sel8",
               "l1f_wq", "l1f_wk", "l1f_wv", "l1f_bq", "l1f_bk", "l1f_bv",
               "l1f_wo", "l1f_bo", "l1f_gw", "l1f_gb", "l1f_ew", "l1f_eb",
               "l1f_ln1w", "l1f_ln1b", "l1f_ln2w", "l1f_ln2b"],
              ("maskc4", "deintb"), ["F1T"]),
    "fin": (stage_fin, ["x1T", "A1T", "F1T"], (), ["outT"]),
}

STAGE_ORDER = ["ln0", "l0attn", "l0ffn", "mid", "l1attn", "l1ffn", "fin"]

_INTERMEDIATE = {"xnT", "A0T", "F0T", "x1T", "xn1T", "A1T", "F1T", "outT"}

_NC_CACHE = {}


def _np_dtype_to_mybir(dt):
    if dt == np.dtype("float32"):
        return F32
    if dt == BFNP:
        return BF16
    raise ValueError(dt)


def build_stage_nc(stage, shapes):
    key = stage
    if key in _NC_CACHE:
        return _NC_CACHE[key]
    builder, ins, cwhich, outs = STAGES[stage]
    nc = bass.Bass("TRN2", target_bir_lowering=False)
    stack = ExitStack()
    c = Ctx(nc, stack)
    for name in ins:
        shp, dt = shapes[name]
        c.dram[name] = nc.dram_tensor(name, list(shp), dt, kind="ExternalInput").ap()
    for name in outs:
        c.dram[name] = nc.dram_tensor(name, [1024, 2048], F32, kind="ExternalOutput").ap()
    C = load_consts(c, cwhich)
    c.tk.barrier()
    builder(c, C)
    c.tk.fin()
    stack.close()
    _NC_CACHE[key] = nc
    return nc


def kernel(**inputs):
    consts = host_consts()
    pms = [prep_core_inputs(b, inputs) for b in range(8)]
    inter = [dict() for _ in range(8)]
    shapes = {}
    for k, v in list(pms[0].items()) + list(consts.items()):
        shapes[k] = (v.shape, _np_dtype_to_mybir(v.dtype))
    for k in _INTERMEDIATE:
        shapes[k] = ((1024, 2048), F32)
    for stage in STAGE_ORDER:
        builder, ins, cwhich, outs = STAGES[stage]
        nc = build_stage_nc(stage, shapes)
        in_maps = []
        for b in range(8):
            im = {}
            for name in ins:
                if name in consts:
                    im[name] = consts[name]
                elif name in _INTERMEDIATE:
                    im[name] = inter[b][name]
                else:
                    im[name] = pms[b][name]
            in_maps.append(im)
        res = run_bass_kernel_spmd(nc, in_maps, core_ids=list(range(8)))
        for b in range(8):
            for name in outs:
                inter[b][name] = res.results[b][name]
    out = np.stack([inter[b]["outT"].T for b in range(8)])
    return np.ascontiguousarray(out.astype(np.float32))



FUSED = True

_FUSED_CONSTS = ("maskc4", "mask8", "sel2", "deintb")


def build_fused_nc(shapes):
    if "fused" in _NC_CACHE:
        return _NC_CACHE["fused"]
    nc = bass.Bass("TRN2", target_bir_lowering=False)
    stack = ExitStack()
    c = Ctx(nc, stack)
    seen = set()
    for stage in STAGE_ORDER:
        builder, ins, cwhich, outs = STAGES[stage]
        for name in ins:
            if name in seen or name in _INTERMEDIATE:
                continue
            seen.add(name)
            shp, dt = shapes[name]
            c.dram[name] = nc.dram_tensor(name, list(shp), dt,
                                          kind="ExternalInput").ap()
    for name in _INTERMEDIATE:
        kind = "ExternalOutput" if name == "outT" else "Internal"
        c.dram[name] = nc.dram_tensor(name, [1024, 2048], F32, kind=kind).ap()
    C = load_consts(c, _FUSED_CONSTS)
    c.tk.barrier()
    for stage in STAGE_ORDER:
        builder, ins, cwhich, outs = STAGES[stage]
        with scope(c):
            builder(c, C)
    c.tk.fin()
    stack.close()
    _NC_CACHE["fused"] = nc
    return nc


def _kernel_fused(inputs):
    consts = host_consts()
    pms = [prep_core_inputs(b, inputs) for b in range(8)]
    shapes = {}
    for k, v in list(pms[0].items()) + list(consts.items()):
        shapes[k] = (v.shape, _np_dtype_to_mybir(v.dtype))
    nc = build_fused_nc(shapes)
    names = set()
    for stage in STAGE_ORDER:
        for name in STAGES[stage][1]:
            if name not in _INTERMEDIATE:
                names.add(name)
    in_maps = []
    for b in range(8):
        im = {}
        for name in names:
            im[name] = consts[name] if name in consts else pms[b][name]
        in_maps.append(im)
    res = run_bass_kernel_spmd(nc, in_maps, core_ids=list(range(8)))
    out = np.stack([res.results[b]["outT"].T for b in range(8)])
    return np.ascontiguousarray(out.astype(np.float32))


_kernel_multilaunch = kernel


def kernel(**inputs):
    if FUSED:
        return _kernel_fused(inputs)
    return _kernel_multilaunch(**inputs)


# version 10 (final)
# speedup vs baseline: 1.3358x; 1.0203x over previous
import numpy as np
import ml_dtypes
from contextlib import ExitStack

import concourse.bass as bass
import concourse.mybir as mybir
from concourse.bass_utils import run_bass_kernel_spmd

F32 = mybir.dt.float32
BF16 = mybir.dt.bfloat16
AF = mybir.ActivationFunctionType
ALU = mybir.AluOpType
AX = mybir.AxisListType
BFNP = ml_dtypes.bfloat16

T = 2048
D = 1024
NCHUNK = 8
TBS = [0, 512, 1024, 1536]
NTB = 4


def pack_lhsT(W):
    K, M = W.shape
    kc = K // 128 if K >= 128 else 1
    kp = min(K, 128)
    mc = (M + 127) // 128
    mp = min(M, 128)
    assert K % kp == 0 and M % mp == 0
    arr = W.reshape(kc, kp, mc, mp).transpose(2, 1, 0, 3)
    arr = arr.reshape(mc, kp, kc * mp)
    if kp < 128:
        arr = np.concatenate(
            [arr, np.zeros((mc, 128 - kp, kc * mp), dtype=arr.dtype)], axis=1)
    return np.ascontiguousarray(arr)


def pack_rhs(W):
    K, M = W.shape
    kp = min(K, 128)
    return np.ascontiguousarray(W.reshape(K // kp, kp, M))


def pack_cols(v):
    n = v.shape[0] // 128
    return np.ascontiguousarray(v.reshape(n, 128).T)


def deinterleave_perm(n, head=64):
    p = np.zeros(n, dtype=np.int64)
    for h in range(n // head):
        for j in range(head):
            p[h * head + j] = h * head + (2 * j if j < head // 2 else 2 * (j - head // 2) + 1)
    return p


def host_masks():
    kp = np.arange(128)[:, None]
    qf = np.arange(512)[None, :]
    mc4 = np.stack([(qf - kp >= 128 * j).astype(np.float32) for j in range(4)])
    m8 = []
    for j in range(-4, 4):
        d = qf - kp - 128 * j
        m8.append(((d >= 0) & (d <= 512)).astype(np.float32))
    return mc4.astype(BFNP), np.stack(m8).astype(BFNP)


def host_l0_xpos_tables():
    DH = 64
    WINDOW = 512
    half = np.arange(0, DH, 2, dtype=np.float64)
    inv_freq = 1.0 / 10000.0 ** (half / DH)
    t = np.arange(T, dtype=np.float64)
    freqs = t[:, None] * inv_freq[None, :]
    freqs = np.concatenate([freqs, freqs], 1)
    base = (half + 0.4 * DH) / (1.4 * DH)
    power = (t - T // 2) / float(WINDOW // 2)
    scale = base[None, :] ** power[:, None]
    scale = np.concatenate([scale, scale], 1)
    cos, sin = np.cos(freqs), np.sin(freqs)
    sgn = np.where(np.arange(DH)[None, :] < 32, 1.0, -1.0)
    CSq = (cos * scale).T
    SSq = (sin * scale).T * sgn.T
    CSk = (cos / scale).T
    SSk = (sin / scale).T * sgn.T
    out = []
    for A in (CSq, SSq, CSk, SSk):
        out.append(np.ascontiguousarray(np.vstack([A, A]).astype(np.float32)))
    return out


def host_l1_rope_tables():
    DH = 64
    inv_freq = 1.0 / 10000.0 ** (np.arange(0, DH, 2, dtype=np.float64) / DH)
    t = np.arange(T, dtype=np.float64)
    fr = t[:, None] * inv_freq[None, :]
    c = np.cos(fr).T
    s = np.sin(fr).T
    CS = np.vstack([c, c])
    SS = np.vstack([s, -s])
    CS = np.vstack([CS, CS]).astype(np.float32)
    SS = np.vstack([SS, SS]).astype(np.float32)
    return np.ascontiguousarray(CS), np.ascontiguousarray(SS)


def host_consts():
    d = {}
    mc4, m8 = host_masks()
    d["maskc4"] = mc4
    d["mask8"] = m8
    CSq, SSq, CSk, SSk = host_l0_xpos_tables()
    d["l0_CSq"], d["l0_SSq"], d["l0_CSk"], d["l0_SSk"] = CSq, SSq, CSk, SSk
    CS, SS = host_l1_rope_tables()
    d["l1_CS"], d["l1_SS"] = CS, SS
    ident = np.eye(128, dtype=np.float32)
    d["ident"] = ident
    pi = deinterleave_perm(128)
    dm = np.zeros((128, 128), dtype=np.float32)
    for j in range(128):
        dm[pi[j], j] = 1.0
    d["deint"] = dm
    sel2 = np.zeros((128, 2), dtype=np.float32)
    sel2[0:64, 0] = 1.0
    sel2[64:128, 1] = 1.0
    d["sel2"] = sel2
    selB = np.zeros((128, 128), dtype=np.float32)
    selB[0, 0:64] = 1.0
    selB[1, 64:128] = 1.0
    d["selB"] = selB
    sel8 = np.zeros((128, 1024), dtype=np.float32)
    for e in range(8):
        sel8[e, e * 128:(e + 1) * 128] = 1.0
    d["sel8"] = sel8
    return d


def prep_core_inputs(b, I):
    m = {}
    m["xT"] = np.ascontiguousarray(I["L0_attn_x"][b].T)
    m["l0a_qw"] = pack_lhsT(I["L0_attn_Qw"])
    m["l0a_kw"] = pack_lhsT(I["L0_attn_Kw"])
    m["l0a_vw"] = pack_rhs(I["L0_attn_Vw"])
    for l in range(2):
        wqkv = I["L0_ffn_wqkv"][l]
        m[f"l0f{l}_wq"] = pack_lhsT(wqkv[:, 0:512])
        m[f"l0f{l}_wk"] = pack_lhsT(wqkv[:, 512:1024])
        m[f"l0f{l}_wv"] = pack_rhs(wqkv[:, 1024:1536])
        m[f"l0f{l}_wo"] = pack_lhsT(I["L0_ffn_wo"][l])
        m[f"l0f{l}_qs"] = np.ascontiguousarray(
            np.tile(I["L0_ffn_q_scale"][l], 2).reshape(128, 1))
        m[f"l0f{l}_ks"] = np.ascontiguousarray(
            np.tile(I["L0_ffn_k_scale"][l], 2).reshape(128, 1))
        m[f"l0f{l}_qsr"] = np.ascontiguousarray(np.pad(
            I["L0_ffn_q_scale"][l].reshape(1, 64), ((0, 127), (0, 0))))
        m[f"l0f{l}_ksr"] = np.ascontiguousarray(np.pad(
            I["L0_ffn_k_scale"][l].reshape(1, 64), ((0, 127), (0, 0))))
        m[f"l0f{l}_lng"] = pack_cols(I["L0_ffn_ln_g"][l])
        m[f"l0f{l}_lnb"] = pack_cols(I["L0_ffn_ln_b"][l])
        w1 = I["L0_ffn_w1"][l]
        b1 = I["L0_ffn_b1"][l]
        w1p = np.zeros((1024, 5632), dtype=np.float32)
        b1p = np.zeros((5632,), dtype=np.float32)
        w1p[:, 0:2730] = w1[:, 0:2730]
        w1p[:, 2816:5546] = w1[:, 2730:5460]
        b1p[0:2730] = b1[0:2730]
        b1p[2816:5546] = b1[2730:5460]
        m[f"l0f{l}_w1"] = pack_lhsT(w1p)
        m[f"l0f{l}_b1"] = pack_cols(b1p)
        w2p = np.zeros((2816, 1024), dtype=np.float32)
        w2p[0:2730] = I["L0_ffn_w2"][l]
        m[f"l0f{l}_w2"] = pack_lhsT(w2p)
        m[f"l0f{l}_b2"] = pack_cols(I["L0_ffn_b2"][l])
    pi = deinterleave_perm(1024)
    m["l1a_wqa"] = pack_lhsT(I["L1_attn_Wqa"])
    m["l1a_wqb"] = pack_lhsT(I["L1_attn_Wqb"][:, pi])
    m["l1a_wkva"] = pack_lhsT(I["L1_attn_Wkva"])
    kvb = I["L1_attn_Wkvb"]
    kperm = np.zeros(256, dtype=np.int64)
    vperm = np.zeros(256, dtype=np.int64)
    for h in range(4):
        for j in range(64):
            dd = 2 * j if j < 32 else 2 * (j - 32) + 1
            kperm[64 * h + j] = 128 * h + 2 * dd
            vperm[64 * h + j] = 128 * h + 2 * j + 1
    dup = np.concatenate([np.tile(np.arange(64 * g, 64 * (g + 1)), 2)
                          for g in range(4)])
    m["l1a_wk"] = np.ascontiguousarray(np.pad(kvb[:, kperm][:, dup],
                                              ((0, 96), (0, 0))))
    m["l1a_wv"] = np.ascontiguousarray(np.pad(kvb[:, vperm],
                                              ((0, 96), (0, 0))))
    m["l1a_wout"] = pack_lhsT(I["L1_attn_Wout"])
    m["l1a_bout"] = pack_cols(I["L1_attn_bout"])
    ipw = I["L1_ffn_in_proj_w"]
    ipb = I["L1_ffn_in_proj_b"]
    Wq, Wk, Wv = ipw[0:1024], ipw[1024:2048], ipw[2048:3072]
    m["l1f_wq"] = pack_lhsT(np.ascontiguousarray(Wq.T[pi]))
    m["l1f_wk"] = pack_lhsT(np.ascontiguousarray(Wk.T[pi]))
    m["l1f_wv"] = pack_rhs(np.ascontiguousarray(Wv.T))
    m["l1f_bq"] = pack_cols(ipb[0:1024])
    m["l1f_bk"] = pack_cols(ipb[1024:2048])
    m["l1f_bv"] = np.ascontiguousarray(np.pad(ipb[2048:3072].reshape(1, 1024),
                                              ((0, 127), (0, 0))))
    m["l1f_wo"] = pack_lhsT(np.ascontiguousarray(I["L1_ffn_out_proj_w"].T))
    m["l1f_bo"] = pack_cols(I["L1_ffn_out_proj_b"])
    m["l1f_gw"] = np.ascontiguousarray(
        I["L1_ffn_gate_w"].T.reshape(8, 128, 8))
    m["l1f_gb"] = np.ascontiguousarray(
        np.pad(I["L1_ffn_gate_b"], (0, 120)).reshape(128, 1))
    ew = I["L1_ffn_expert_w"]
    m["l1f_ew"] = np.stack([pack_lhsT(np.ascontiguousarray(ew[e].T))
                            for e in range(8)]).reshape(64, 128, 1024)
    m["l1f_eb"] = np.concatenate([pack_cols(I["L1_ffn_expert_b"][e])
                                  for e in range(8)], axis=1)
    m["l1f_ln1w"] = pack_cols(I["L1_ffn_ln1_w"])
    m["l1f_ln1b"] = pack_cols(I["L1_ffn_ln1_b"])
    m["l1f_ln2w"] = pack_cols(I["L1_ffn_ln2_w"])
    m["l1f_ln2b"] = pack_cols(I["L1_ffn_ln2_b"])
    return m



class Tracker:

    def __init__(self, nc, stack):
        self.nc = nc
        self.eng = {
            "pe": nc.tensor, "act": nc.scalar, "dve": nc.vector,
            "pool": nc.gpsimd, "sq": nc.sync, "gq": nc.gpsimd,
        }
        self.step = {"pe": 1, "act": 1, "dve": 1, "pool": 1, "sq": 16, "gq": 16}
        self.sem = {}
        self.cum = {}
        self.instrs = {}
        self.incs = {}
        self.waited = {}
        for e in self.eng:
            self.sem[e] = stack.enter_context(nc.semaphore("s_" + e))
            self.cum[e] = 0
            self.instrs[e] = []
            self.incs[e] = []
        self.writes = {}
        self.reads = {}

    def _need(self, e, idx):
        if e in ("sq", "gq"):
            return self.cum[e]
        arr = self.incs[e]
        import bisect
        pos = bisect.bisect_left(arr, (idx, -1))
        if pos < len(arr):
            return arr[pos][1]
        self.cum[e] += self.step[e]
        v = self.cum[e]
        self.instrs[e][idx].then_inc(self.sem[e], self.step[e])
        arr.append((idx, v))
        return v

    def _edges(self, regions, kinds):
        deps = []
        for (key, lo, hi) in regions:
            for store in kinds:
                recs = store.get(key)
                if not recs:
                    continue
                for (l2, h2, e2, i2) in recs:
                    if l2 < hi and lo < h2:
                        deps.append((e2, i2))
        return deps

    def emit(self, e, thunk, reads=(), writes=()):
        deps = []
        deps += self._edges(reads, (self.writes,))
        deps += self._edges(writes, (self.writes, self.reads))
        byeng = {}
        for (e2, i2) in deps:
            if e2 == e and e in ("pe", "sq", "gq"):
                continue
            if e2 == e and i2 == len(self.instrs[e]) - 0:
                continue
            if byeng.get(e2, -1) < i2:
                byeng[e2] = i2
        for e2, i2 in sorted(byeng.items()):
            v = self._need(e2, i2)
            k = (e, e2)
            if self.waited.get(k, -1) >= v:
                continue
            self.eng[e].wait_ge(self.sem[e2], v)
            self.waited[k] = v
        instr = thunk()
        idx = len(self.instrs[e])
        self.instrs[e].append(instr)
        if e in ("sq", "gq"):
            self.cum[e] += self.step[e]
            instr.then_inc(self.sem[e], self.step[e])
            self.incs[e].append((idx, self.cum[e]))
        for (key, lo, hi) in writes:
            wl = self.writes.setdefault(key, [])
            out = []
            for r in wl:
                if r[0] < hi and lo < r[1]:
                    if r[0] < lo:
                        out.append((r[0], lo, r[2], r[3]))
                    if r[1] > hi:
                        out.append((hi, r[1], r[2], r[3]))
                else:
                    out.append(r)
            out.append((lo, hi, e, idx))
            self.writes[key] = out
            rl = self.reads.get(key)
            if rl:
                self.reads[key] = [r for r in rl if not (r[0] < hi and lo < r[1])]
        for (key, lo, hi) in reads:
            self.reads.setdefault(key, []).append((lo, hi, e, idx))
        return instr

    def fin(self):
        for e in ("sq", "gq"):
            if self.cum[e] < self.step[e] * len(self.instrs[e]):
                if self.instrs[e]:
                    self._need(e, len(self.instrs[e]) - 1)
        for e in ("sq", "gq"):
            if self.cum[e] > 0:
                self.eng[e].wait_ge(self.sem[e], self.cum[e])


class Ctx:
    def __init__(self, nc, stack):
        self.nc = nc
        self.stack = stack
        self.tk = Tracker(nc, stack)
        self.dram = {}
        self.uid = 0

    def sb(self, name, shape, dtype):
        t = self.stack.enter_context(self.nc.sbuf_tensor(name, shape, dtype))
        return t

    def ps(self, name, shape=(128, 512), dtype=F32):
        return self.stack.enter_context(self.nc.psum_tensor(name, list(shape), dtype))

    def name(self, base):
        self.uid += 1
        return f"{base}_{self.uid}"


def R(buf, lo=0, hi=1 << 30):
    return (id(buf), lo, hi)



def load_consts(c: Ctx, which=()):
    tk, nc = c.tk, c.nc
    d = {}
    for v in (1e-5, 1e-24):
        t = c.sb(c.name("constb"), [128, 1], F32)
        tk.emit("dve", lambda t=t, v=v: nc.vector.memset(t[:, :], float(v)),
                writes=[R(t)])
        nc.const_aps.register(F32, float(v), t[:, :])
    ones = c.sb("onesbuf", [128, 8], F32)
    tk.emit("dve", lambda: nc.vector.memset(ones[:, 0:4], 1.0), writes=[R(ones, 0, 4)])
    d["ones_col_f"] = ones[:, 0:1]
    onesb = c.sb("onesbufb", [128, 8], BF16)
    tk.emit("dve", lambda: nc.vector.memset(onesb[:, 0:4], 1.0), writes=[R(onesb, 0, 4)])
    d["ones_col_b"] = onesb[:, 0:1]
    onesr = c.sb("onesrow", [1, 128], F32)
    tk.emit("dve", lambda: nc.vector.memset(onesr[:, :], 1.0), writes=[R(onesr)])
    d["ones_row_f"] = onesr[0:1, :]
    onesrb = c.sb("onesrowb", [1, 128], BF16)
    tk.emit("dve", lambda: nc.vector.memset(onesrb[:, :], 1.0), writes=[R(onesrb)])
    d["ones_row_b"] = onesrb[0:1, :]
    if "maskc4" in which:
        mc4 = c.sb("maskc4_s", [128, 4 * 512], BF16)
        for j in range(4):
            tk.emit("sq", lambda j=j: nc.sync.dma_start(
                out=mc4[:, j * 512:(j + 1) * 512], in_=c.dram["maskc4"][j]),
                writes=[R(mc4, j * 512, (j + 1) * 512)])
        d["maskc4"] = mc4
    if "mask8" in which:
        m8 = c.sb("mask8_s", [128, 8 * 512], BF16)
        for j in range(8):
            tk.emit("sq", lambda j=j: nc.sync.dma_start(
                out=m8[:, j * 512:(j + 1) * 512], in_=c.dram["mask8"][j]),
                writes=[R(m8, j * 512, (j + 1) * 512)])
        d["mask8"] = m8
    if "deintb" in which:
        deintb = c.sb("deintb_s", [128, 128], BF16)
        tk.emit("gq", lambda: nc.gpsimd.dma_start(
            out=deintb[:, :], in_=c.dram["deint"][:, :]), writes=[R(deintb)])
        d["deintb"] = deintb
    if "sel2" in which:
        sel2 = c.sb("sel2_s", [128, 2], BF16)
        tk.emit("gq", lambda: nc.gpsimd.dma_start(
            out=sel2[:, :], in_=c.dram["sel2"][:, :]), writes=[R(sel2)])
        d["sel2"] = sel2
        selB = c.sb("selB_s", [128, 128], BF16)
        tk.emit("gq", lambda: nc.gpsimd.dma_start(
            out=selB[:, :], in_=c.dram["selB"][:, :]), writes=[R(selB)])
        d["selB"] = selB
    return d


class Rows:
    def __init__(self, c, name, ncols):
        self.t = c.sb(name, [128, ncols], F32)
        self.ncols = ncols

    def slot(self, base, col, w=512):
        return self.t[base:base + 1, col:col + w], R(self.t, col, col + w)


class LNState:
    def __init__(self, c, psum_pool):
        self.rows = Rows(c, c.name("lnrows"), 1024)
        self.sq = [c.sb(c.name("lnsq"), [128, 512], BF16) for _ in range(2)]
        self.tmp = [c.sb(c.name("lntmp"), [128, 512], F32) for _ in range(4)]
        self.muB = c.sb(c.name("lnmuB"), [128, 512], F32)
        self.rB = c.sb(c.name("lnrB"), [128, 512], F32)
        self.ps = psum_pool


def ln_tb(c, C, L, src, tb, dsts, gamma=None, beta=None, eps=1e-5, src_tb=None):
    tk, nc = c.tk, c.nc
    st = tb if src_tb is None else src_tb
    srcf32 = src[0].dtype == F32
    psA, psB, psC = L.ps[0], L.ps[1], L.ps[2]
    onescol = C["ones_col_f"] if srcf32 else C["ones_col_b"]
    onesb = C["ones_col_b"]
    onesrow = C["ones_row_f"]
    for ci in range(8):
        s = src[ci]
        tk.emit("pe", lambda s=s, ci=ci: nc.tensor.matmul(
            psA[0:1, 0:512], onescol, s[:, st:st + 512],
            start=(ci == 0), stop=(ci == 7)),
            reads=[R(s, st, st + 512)], writes=[R(psA, 0, 512)])
        sq = L.sq[ci % 2]
        tk.emit("act", lambda s=s, sq=sq: nc.scalar.activation(
            sq[:, :], s[:, st:st + 512], AF.Square),
            reads=[R(s, st, st + 512)], writes=[R(sq)])
        tk.emit("pe", lambda sq=sq, ci=ci: nc.tensor.matmul(
            psB[0:1, 0:512], onesb, sq[:, :],
            start=(ci == 0), stop=(ci == 7)),
            reads=[R(sq)], writes=[R(psB, 0, 512)])
    mu, rmu = L.rows.slot(0, 0)
    msq, rmsq = L.rows.slot(32, 0)
    var, rvar = L.rows.slot(0, 512)
    tmq, rtmq = L.rows.slot(32, 512)
    tk.emit("act", lambda: nc.scalar.mul(mu, psA[0:1, 0:512], 1.0 / 1024.0),
            reads=[R(psA, 0, 512)], writes=[rmu])
    tk.emit("act", lambda: nc.scalar.mul(msq, psB[0:1, 0:512], 1.0 / 1024.0),
            reads=[R(psB, 0, 512)], writes=[rmsq])
    tk.emit("dve", lambda: nc.vector.tensor_mul(tmq, mu, mu),
            reads=[rmu], writes=[rtmq])
    tk.emit("dve", lambda: nc.vector.tensor_sub(var, msq, tmq),
            reads=[rmsq, rtmq], writes=[rvar])
    tk.emit("act", lambda: nc.scalar.activation(var, var, AF.Ln, bias=float(eps)),
            reads=[rvar], writes=[rvar])
    tk.emit("act", lambda: nc.scalar.activation(var, var, AF.Exp, scale=-0.5),
            reads=[rvar], writes=[rvar])
    tk.emit("pe", lambda: nc.tensor.matmul(psC[:, 0:512], onesrow, mu,
                                           start=True, stop=True),
            reads=[rmu], writes=[R(psC, 0, 512)])
    tk.emit("dve", lambda: nc.vector.tensor_copy(L.muB[:, :], psC[:, 0:512]),
            reads=[R(psC, 0, 512)], writes=[R(L.muB)])
    tk.emit("pe", lambda: nc.tensor.matmul(psC[:, 0:512], onesrow, var,
                                           start=True, stop=True),
            reads=[rvar], writes=[R(psC, 0, 512)])
    tk.emit("dve", lambda: nc.vector.tensor_copy(L.rB[:, :], psC[:, 0:512]),
            reads=[R(psC, 0, 512)], writes=[R(L.rB)])
    for ci in range(8):
        s = src[ci]
        e = "dve" if (ci % 2 == 0) else "pool"
        eng = nc.vector if (ci % 2 == 0) else nc.gpsimd
        t1 = L.tmp[ci % 2]
        tk.emit(e, lambda s=s, t1=t1, eng=eng: eng.tensor_sub(
            t1[:, :], s[:, st:st + 512], L.muB[:, :]),
            reads=[R(s, st, st + 512), R(L.muB)], writes=[R(t1)])
        if gamma is None:
            for dst in dsts:
                dd = dst[ci]
                tk.emit(e, lambda t1=t1, dd=dd, eng=eng: eng.tensor_mul(
                    dd[:, tb:tb + 512], t1[:, :], L.rB[:, :]),
                    reads=[R(t1), R(L.rB)], writes=[R(dd, tb, tb + 512)])
        else:
            t2 = L.tmp[2 + ci % 2]
            tk.emit(e, lambda t1=t1, t2=t2, eng=eng: eng.tensor_mul(
                t2[:, :], t1[:, :], L.rB[:, :]),
                reads=[R(t1), R(L.rB)], writes=[R(t2)])
            for dst in dsts:
                dd = dst[ci]
                tk.emit(e, lambda t2=t2, dd=dd, ci=ci, eng=eng: eng.tensor_scalar(
                    dd[:, tb:tb + 512], t2[:, :], gamma[:, ci:ci + 1],
                    beta[:, ci:ci + 1], ALU.mult, ALU.add),
                    reads=[R(t2), R(gamma, ci, ci + 1), R(beta, ci, ci + 1)],
                    writes=[R(dd, tb, tb + 512)])



class WStream:
    def __init__(self, c, dram_name, n_oc, P, K, nbuf=2, base=0):
        self.c = c
        self.src = c.dram[dram_name]
        self.n_oc = n_oc
        self.P, self.K = P, K
        self.nbuf = nbuf
        self.base = base
        self.bufs = [c.sb(c.name(dram_name + "_wb"), [P, K], BF16)
                     for _ in range(nbuf)]
        self.fetched = -1

    def fetch(self, oc):
        if oc >= self.n_oc or oc <= self.fetched:
            return
        self.fetched = oc
        b = self.bufs[oc % self.nbuf]
        tk, nc = self.c.tk, self.c.nc
        tk.emit("gq", lambda: nc.gpsimd.dma_start(out=b[:, :],
                in_=self.src[self.base + oc]),
                writes=[R(b)])

    def get(self, oc):
        return self.bufs[oc % self.nbuf], R(self.bufs[oc % self.nbuf])


def load_pack_resident(c, dram_name, n_oc, P, K):
    tk, nc = c.tk, c.nc
    out = []
    for oc in range(n_oc):
        t = c.sb(c.name(dram_name + "_r"), [P, K], BF16)
        tk.emit("gq", lambda t=t, oc=oc: nc.gpsimd.dma_start(
            out=t[:, :], in_=c.dram[dram_name][oc]), writes=[R(t)])
        out.append(t)
    return out


def load_f32(c, dram_name, shape, qn="sq"):
    t = c.sb(c.name(dram_name + "_f"), list(shape), F32)
    tk, nc = c.tk, c.nc
    src = c.dram[dram_name]
    tk.emit(qn, lambda: tk.eng[qn].dma_start(out=t[:, :], in_=src[:, :]),
            writes=[R(t)])
    return t


def load_chunks_f32(c, dram_name, dst=None, cast_bf16=None):
    tk, nc = c.tk, c.nc
    src = c.dram[dram_name]
    if dst is not None:
        for ci in range(8):
            tk.emit("sq", lambda ci=ci: nc.sync.dma_start(
                out=dst[ci][:, :], in_=src[ci * 128:(ci + 1) * 128, :]),
                writes=[R(dst[ci])])
    if cast_bf16 is not None:
        for ci in range(8):
            tk.emit("gq", lambda ci=ci: nc.gpsimd.dma_start(
                out=cast_bf16[ci][:, :], in_=src[ci * 128:(ci + 1) * 128, :]),
                writes=[R(cast_bf16[ci])])


def store_chunks_f32(c, dram_name, src, tbs=None):
    tk, nc = c.tk, c.nc
    dst = c.dram[dram_name]
    for ci in range(8):
        tk.emit("sq", lambda ci=ci: nc.sync.dma_start(
            out=dst[ci * 128:(ci + 1) * 128, :], in_=src[ci][:, :]),
            reads=[R(src[ci])])


def mk_chunks(c, base, dtype, n=8, F=T):
    return [c.sb(c.name(base), [128, F], dtype) for _ in range(n)]



def negc_from_norms(c, C, ps, rows, q_tiles, k_tiles, scale, sqtmp):
    tk, nc = c.tk, c.nc
    psN = ps[0]
    acc, racc = rows.slot(0, 512, 8)
    for (j, tiles) in ((0, q_tiles), (1, k_tiles)):
        for itb, tb in enumerate(TBS):
            for i, qt in enumerate(tiles):
                sq = sqtmp[i % 2]
                tk.emit("act", lambda qt=qt, sq=sq, tb=tb: nc.scalar.activation(
                    sq[:, :], qt[:, tb:tb + 512], AF.Square),
                    reads=[R(qt, tb, tb + 512)], writes=[R(sq)])
                tk.emit("pe", lambda sq=sq, i=i, n=len(tiles): nc.tensor.matmul(
                    psN[0:1, 0:512], C["ones_col_b"], sq[:, :],
                    start=(i == 0), stop=(i == n - 1)),
                    reads=[R(sq)], writes=[R(psN, 0, 512)])
            col = j * 4 + itb
            tk.emit("dve", lambda col=col: nc.vector.reduce_max(
                acc[0:1, col - j * 4 + j * 4:col + 1] if False else acc[0:1, col:col + 1],
                psN[0:1, 0:512], axis=AX.X),
                reads=[R(psN, 0, 512)], writes=[(racc[0], racc[1] + col, racc[1] + col + 1)])
    mq, rmq = rows.slot(0, 520, 1)
    mk, rmk = rows.slot(0, 521, 1)
    cc, rcc = rows.slot(0, 522, 1)
    tk.emit("dve", lambda: nc.vector.reduce_max(mq, acc[0:1, 0:4], axis=AX.X),
            reads=[(racc[0], racc[1], racc[1] + 4)], writes=[rmq])
    tk.emit("dve", lambda: nc.vector.reduce_max(mk, acc[0:1, 4:8], axis=AX.X),
            reads=[(racc[0], racc[1] + 4, racc[1] + 8)], writes=[rmk])
    tk.emit("dve", lambda: nc.vector.tensor_mul(cc, mq, mk),
            reads=[rmq, rmk], writes=[rcc])
    tk.emit("act", lambda: nc.scalar.activation(cc, cc, AF.Ln),
            reads=[rcc], writes=[rcc])
    tk.emit("act", lambda: nc.scalar.activation(cc, cc, AF.Exp, scale=0.5),
            reads=[rcc], writes=[rcc])
    negc = c.sb(c.name("negc"), [128, 1], F32)
    tk.emit("pe", lambda: nc.tensor.matmul(psN[0:128, 0:1], C["ones_row_f"],
                                           cc, start=True, stop=True),
            reads=[rcc], writes=[R(psN, 0, 1)])
    tk.emit("act", lambda: nc.scalar.mul(negc[:, :], psN[0:128, 0:1], -float(scale)),
            reads=[R(psN, 0, 1)], writes=[R(negc)])
    return negc



def attn64(c, C, cfg):
    tk, nc = c.tk, c.nc
    negc = cfg["negc"]
    for qb in range(4):
        ktl = cfg["ktlist"](qb)
        for h in range(cfg["nh"]):
            qt, q_base = cfg["qmap"](h)
            ktn, k_base = cfg["kmap"](h)
            psO = cfg["psO"][h % len(cfg["psO"])]
            nkt = len(ktl)
            for i, (kt, mask) in enumerate(ktl):
                psS = cfg["psS"][i % len(cfg["psS"])]
                tk.emit("pe", lambda psS=psS, kt=kt, ktn=ktn, k_base=k_base,
                        qt=qt, q_base=q_base, qb=qb: nc.tensor.matmul(
                    psS[:, 0:512],
                    ktn[k_base:k_base + 64, kt * 128:(kt + 1) * 128],
                    qt[q_base:q_base + 64, qb * 512:(qb + 1) * 512],
                    start=True, stop=True),
                    reads=[R(ktn, kt * 128, (kt + 1) * 128),
                           R(qt, qb * 512, (qb + 1) * 512)],
                    writes=[R(psS, 0, 512)])
                E = cfg["ebuf"][i % 2]
                tk.emit("act", lambda psS=psS, E=E: nc.scalar.activation(
                    E[:, :], psS[:, 0:512], AF.Exp, bias=negc[:, :],
                    scale=float(cfg["scale"])),
                    reads=[R(psS, 0, 512), R(negc)], writes=[R(E)])
                if mask is not None:
                    mt, mlo, mhi = mask
                    tk.emit("dve", lambda E=E, mt=mt, mlo=mlo: nc.vector.tensor_mul(
                        E[:, :], E[:, :], mt[:, mlo:mlo + 512]),
                        reads=[R(E), (id(mt), mlo, mhi)], writes=[R(E)])
                vap, vreg = cfg["vsl"](kt, h)
                tk.emit("pe", lambda vap=vap, E=E, i=i, psO=psO, nkt=nkt: nc.tensor.matmul(
                    psO[0:65, 0:512], vap, E[:, :],
                    start=(i == 0), stop=(i == nkt - 1)),
                    reads=[vreg, R(E)], writes=[R(psO, 0, 512)])
            arow, rarow = cfg["rows"].slot(0, 0)
            tk.emit("dve", lambda psO=psO, arow=arow: nc.vector.reciprocal(
                arow, psO[64:65, 0:512]),
                reads=[R(psO, 0, 512)], writes=[rarow])
            psA = cfg["psA"]
            tk.emit("pe", lambda arow=arow: nc.tensor.matmul(
                psA[0:64, 0:512], C["ones_row_f"][0:1, 0:64], arow,
                start=True, stop=True),
                reads=[rarow], writes=[R(psA, 0, 512)])
            aB = cfg["aB"][h % 2]
            tk.emit("act", lambda aB=aB: nc.scalar.copy(aB[0:64, :], psA[0:64, 0:512]),
                    reads=[R(psA, 0, 512)], writes=[R(aB)])
            cfg["out"](h, qb, psO, aB)



def stage_ln0(c, C):
    tk, nc = c.tk, c.nc
    x = mk_chunks(c, "x", F32)
    load_chunks_f32(c, "xT", dst=x)
    xn = mk_chunks(c, "xn", F32)
    ps = [c.ps(c.name("psln")) for _ in range(3)]
    L = LNState(c, ps)
    for tb in TBS:
        ln_tb(c, C, L, x, tb, [xn])
    store_chunks_f32(c, "xnT", xn)


def stage_l0_attn(c, C):
    tk, nc = c.tk, c.nc
    xnb = mk_chunks(c, "xnb", BF16)
    load_chunks_f32(c, "xnT", cast_bf16=xnb)
    qw = WStream(c, "l0a_qw", 8, 128, 1024, nbuf=2)
    kw = WStream(c, "l0a_kw", 8, 128, 1024, nbuf=2)
    vw = load_pack_resident(c, "l0a_vw", 8, 128, 1024)
    qT = mk_chunks(c, "l0a_qT", BF16)
    kT = mk_chunks(c, "l0a_kT", BF16)
    vtok = [c.sb(c.name("l0a_vtok"), [128, 1024], BF16) for _ in range(16)]
    ps = [c.ps(c.name("ps")) for _ in range(7)]
    rows = Rows(c, "l0a_rows", 1024)
    sqt = [c.sb(c.name("sqt"), [128, 512], BF16) for _ in range(2)]
    for (W, dst) in ((qw, qT), (kw, kT)):
        W.fetch(0); W.fetch(1)
        for oc in range(8):
            wb, wreg = W.get(oc)
            for itb, tb in enumerate(TBS):
                psq = ps[itb % 2]
                for dc in range(8):
                    tk.emit("pe", lambda wb=wb, dc=dc, tb=tb, psq=psq: nc.tensor.matmul(
                        psq[:, 0:512], wb[:, dc * 128:(dc + 1) * 128],
                        xnb[dc][:, tb:tb + 512], start=(dc == 0), stop=(dc == 7)),
                        reads=[(id(wb), dc * 128, (dc + 1) * 128), R(xnb[dc], tb, tb + 512)],
                        writes=[R(psq, 0, 512)])
                tk.emit("act", lambda psq=psq, dst=dst, oc=oc, tb=tb: nc.scalar.copy(
                    dst[oc][:, tb:tb + 512], psq[:, 0:512]),
                    reads=[R(psq, 0, 512)], writes=[R(dst[oc], tb, tb + 512)])
            W.fetch(oc + 2)
    for tt in range(16):
        for vh in range(2):
            psv = ps[2 + (tt * 2 + vh) % 2]
            for dc in range(8):
                tk.emit("pe", lambda tt=tt, vh=vh, dc=dc, psv=psv: nc.tensor.matmul(
                    psv[:, 0:512], xnb[dc][:, tt * 128:(tt + 1) * 128],
                    vw[dc][:, vh * 512:(vh + 1) * 512],
                    start=(dc == 0), stop=(dc == 7)),
                    reads=[R(xnb[dc], tt * 128, (tt + 1) * 128),
                           (id(vw[dc]), vh * 512, (vh + 1) * 512)],
                    writes=[R(psv, 0, 512)])
            tk.emit("act", lambda psv=psv, tt=tt, vh=vh: nc.scalar.copy(
                vtok[tt][:, vh * 512:(vh + 1) * 512], psv[:, 0:512]),
                reads=[R(psv, 0, 512)], writes=[R(vtok[tt], vh * 512, (vh + 1) * 512)])
    negc = negc_from_norms(c, C, [ps[4]], rows, qT, kT, 1.0 / 32.0, sqt)
    if "DBGT" in c.dram:
        tk.emit("sq", lambda: nc.sync.dma_start(
            out=c.dram["DBGT"][0:128, 0:1], in_=negc[:, :],
            allow_slow_non_contiguous=True), reads=[R(negc)])
        tk.emit("gq", lambda: nc.gpsimd.dma_start(
            out=c.dram["DBGT"][128:256, 0:2048], in_=qT[0][:, :]), reads=[R(qT[0])])
        tk.emit("gq", lambda: nc.gpsimd.dma_start(
            out=c.dram["DBGT"][256:384, 0:2048], in_=kT[0][:, :]), reads=[R(kT[0])])
        tk.emit("gq", lambda: nc.gpsimd.dma_start(
            out=c.dram["DBGT"][384:512, 0:1024], in_=vtok[0][:, :]), reads=[R(vtok[0])])
        tk.emit("sq", lambda: nc.sync.dma_start(
            out=c.dram["DBGT"][768:896, 0:1024], in_=rows.t[:, :]),
            reads=[R(rows.t)])
    ebuf = c.sb("l0a_ebuf", [128, 16 * 512], BF16)
    aB = c.sb("l0a_aB", [128, 512], F32)
    a0t = [c.sb(c.name("a0t"), [128, 512], F32) for _ in range(2)]
    for qb in range(4):
        nkt = 4 * qb + 4

        def emit_l(kt, nkt=None):
            tk.emit("pe", lambda kt=kt: nc.tensor.matmul(
                ps[4][0:1, 0:512], C["ones_col_b"],
                ebuf[:, kt * 512:(kt + 1) * 512],
                start=(kt == 0), stop=(kt == nkt - 1)),
                reads=[R(ebuf, kt * 512, (kt + 1) * 512)],
                writes=[R(ps[4], 0, 512)])
        for kt in range(nkt):
            psS = ps[kt % 2]
            for dc in range(8):
                tk.emit("pe", lambda kt=kt, dc=dc, qb=qb, psS=psS: nc.tensor.matmul(
                    psS[:, 0:512], kT[dc][:, kt * 128:(kt + 1) * 128],
                    qT[dc][:, qb * 512:(qb + 1) * 512],
                    start=(dc == 0), stop=(dc == 7)),
                    reads=[R(kT[dc], kt * 128, (kt + 1) * 128),
                           R(qT[dc], qb * 512, (qb + 1) * 512)],
                    writes=[R(psS, 0, 512)])
            if kt > 0:
                emit_l(kt - 1, nkt)
            tk.emit("act", lambda psS=psS, kt=kt: nc.scalar.activation(
                ebuf[:, kt * 512:(kt + 1) * 512], psS[:, 0:512], AF.Exp,
                bias=negc[:, :], scale=1.0 / 32.0),
                reads=[R(psS, 0, 512), R(negc)],
                writes=[R(ebuf, kt * 512, (kt + 1) * 512)])
            j = kt - 4 * qb
            if j >= 0:
                tk.emit("dve", lambda kt=kt, j=j: nc.vector.tensor_mul(
                    ebuf[:, kt * 512:(kt + 1) * 512],
                    ebuf[:, kt * 512:(kt + 1) * 512],
                    C["maskc4"][:, j * 512:(j + 1) * 512]),
                    reads=[R(ebuf, kt * 512, (kt + 1) * 512),
                           R(C["maskc4"], j * 512, (j + 1) * 512)],
                    writes=[R(ebuf, kt * 512, (kt + 1) * 512)])
        emit_l(nkt - 1, nkt)
        arow, rarow = rows.slot(0, 0)
        if "DBGT" in c.dram and qb == 0:
            tk.emit("gq", lambda: nc.gpsimd.dma_start(
                out=c.dram["DBGT"][512:640, 0:2048], in_=ebuf[:, 0:2048]),
                reads=[R(ebuf, 0, 2048)])
            lrow = c.sb("dbglrow", [1, 512], F32)
            tk.emit("dve", lambda: nc.vector.tensor_copy(lrow[:, :], ps[4][0:1, 0:512]),
                    reads=[R(ps[4], 0, 512)], writes=[R(lrow)])
            tk.emit("sq", lambda: nc.sync.dma_start(
                out=c.dram["DBGT"][640:641, 0:512], in_=lrow[:, :]), reads=[R(lrow)])
        tk.emit("dve", lambda: nc.vector.reciprocal(arow, ps[4][0:1, 0:512]),
                reads=[R(ps[4], 0, 512)], writes=[rarow])
        tk.emit("pe", lambda: nc.tensor.matmul(
            ps[5][:, 0:512], C["ones_row_f"], arow, start=True, stop=True),
            reads=[rarow], writes=[R(ps[5], 0, 512)])
        tk.emit("act", lambda: nc.scalar.copy(aB[:, :], ps[5][:, 0:512]),
                reads=[R(ps[5], 0, 512)], writes=[R(aB)])
        for dvc in range(8):
            psO = ps[2 + dvc % 2]
            for kt in range(nkt):
                tk.emit("pe", lambda dvc=dvc, kt=kt, psO=psO: nc.tensor.matmul(
                    psO[:, 0:512], vtok[kt][:, dvc * 128:(dvc + 1) * 128],
                    ebuf[:, kt * 512:(kt + 1) * 512],
                    start=(kt == 0), stop=(kt == nkt - 1)),
                    reads=[R(vtok[kt], dvc * 128, (dvc + 1) * 128),
                           R(ebuf, kt * 512, (kt + 1) * 512)],
                    writes=[R(psO, 0, 512)])
            a0 = a0t[dvc % 2]
            tk.emit("dve", lambda psO=psO, a0=a0: nc.vector.tensor_mul(
                a0[:, :], psO[:, 0:512], aB[:, :]),
                reads=[R(psO, 0, 512), R(aB)], writes=[R(a0)])
            tk.emit("sq", lambda a0=a0, dvc=dvc, qb=qb: nc.sync.dma_start(
                out=c.dram["A0T"][dvc * 128:(dvc + 1) * 128, qb * 512:(qb + 1) * 512],
                in_=a0[:, :]),
                reads=[R(a0)])



def _tracker_barrier(self):
    vals = {}
    for e in self.eng:
        if self.instrs[e]:
            vals[e] = self._need(e, len(self.instrs[e]) - 1)
    for f in self.eng:
        for e, v in vals.items():
            if e == f:
                continue
            k = (f, e)
            if self.waited.get(k, -1) >= v:
                continue
            self.eng[f].wait_ge(self.sem[e], v)
            self.waited[k] = v

Tracker.barrier = _tracker_barrier

from contextlib import contextmanager

@contextmanager
def scope(c):
    st = ExitStack()
    old = c.stack
    c.stack = st
    try:
        yield
    finally:
        st.close()
        c.stack = old
        c.tk.barrier()


def reload_pack(c, bufs, dram_name, base=0):
    tk, nc = c.tk, c.nc
    for i, t in enumerate(bufs):
        tk.emit("gq", lambda t=t, i=i: nc.gpsimd.dma_start(
            out=t[:, :], in_=c.dram[dram_name][base + i]), writes=[R(t)])


def rope_tile(c, eng, src, s_lo, dst, dst_ap_cols, CS, SS, tb, tmpR):
    tk, nc = c.tk, c.nc
    E = nc.vector if eng == "dve" else nc.gpsimd
    c0, c1 = dst_ap_cols
    for (o, i) in ((0, 32), (32, 0), (64, 96), (96, 64)):
        tk.emit(eng, lambda o=o, i=i: E.tensor_mul(
            tmpR[o:o + 32, 0:512], src[i:i + 32, s_lo:s_lo + 512],
            SS[i:i + 32, tb:tb + 512]),
            reads=[R(src, s_lo, s_lo + 512), R(SS, tb, tb + 512)],
            writes=[R(tmpR, 0, 512)])
    tk.emit(eng, lambda: E.tensor_mul(
        dst[:, c0:c1], src[:, s_lo:s_lo + 512], CS[:, tb:tb + 512]),
        reads=[R(src, s_lo, s_lo + 512), R(CS, tb, tb + 512)],
        writes=[R(dst, c0, c1)])
    tk.emit(eng, lambda: E.tensor_add(dst[:, c0:c1], dst[:, c0:c1], tmpR[:, 0:512]),
            reads=[R(dst, c0, c1), R(tmpR, 0, 512)], writes=[R(dst, c0, c1)])


def shift_cast(c, y, hb, shift_from=4):
    tk, nc = c.tk, c.nc
    for ci in range(8):
        e = "dve" if ci % 2 == 0 else "pool"
        E = nc.vector if ci % 2 == 0 else nc.gpsimd
        if ci < shift_from:
            tk.emit(e, lambda ci=ci, E=E: E.tensor_copy(hb[ci][:, :], y[ci][:, :]),
                    reads=[R(y[ci])], writes=[R(hb[ci])])
        else:
            tk.emit(e, lambda ci=ci, E=E: E.tensor_copy(
                hb[ci][:, 1:2048], y[ci][:, 0:2047]),
                reads=[R(y[ci])], writes=[R(hb[ci])])
            tk.emit(e, lambda ci=ci, E=E: E.memset(hb[ci][:, 0:1], 0.0),
                    reads=[], writes=[R(hb[ci], 0, 1)])


def stage_l0_ffn(c, C):
    tk, nc = c.tk, c.nc
    ps = [c.ps(c.name("ps")) for _ in range(8)]
    y = mk_chunks(c, "y", F32)
    load_chunks_f32(c, "xnT", dst=y)
    rows = Rows(c, "l0f_rows", 1024)
    for l in range(2):
        p = f"l0f{l}_"
        layer_scope = scope(c)
        layer_scope.__enter__()
        qsr = load_f32(c, p + "qsr", [128, 64])
        ksr = load_f32(c, p + "ksr", [128, 64])
        t0, rt0 = rows.slot(0, 0, 64)
        m0, rm0 = rows.slot(0, 100, 1)
        tk.emit("dve", lambda: nc.vector.tensor_mul(t0, qsr[0:1, :], ksr[0:1, :]),
                reads=[R(qsr), R(ksr)], writes=[rt0])
        tk.emit("dve", lambda: nc.vector.reduce_max(
            m0, t0, axis=AX.X, apply_absolute_value=True),
            reads=[rt0], writes=[rm0])
        negc = c.sb(c.name("negc"), [128, 1], F32)
        tk.emit("pe", lambda: nc.tensor.matmul(ps[6][0:128, 0:1], C["ones_row_f"],
                                               m0, start=True, stop=True),
                reads=[rm0], writes=[R(ps[6], 0, 1)])
        tk.emit("act", lambda: nc.scalar.mul(negc[:, :], ps[6][0:128, 0:1], -8.0),
                reads=[R(ps[6], 0, 1)], writes=[R(negc)])
        qs_col = load_f32(c, p + "qs", [128, 1])
        ks_col = load_f32(c, p + "ks", [128, 1])
        qeff = mk_chunks(c, "qeff", BF16, n=4)
        keff = mk_chunks(c, "keff", BF16, n=4)
        vtok = [c.sb(c.name("vtok"), [128, 8, 65], BF16) for _ in range(16)]
        with scope(c):
            CSq = load_f32(c, "l0_CSq", [128, 2048])
            SSq = load_f32(c, "l0_SSq", [128, 2048])
            CSk = load_f32(c, "l0_CSk", [128, 2048])
            SSk = load_f32(c, "l0_SSk", [128, 2048])
            wq = load_pack_resident(c, p + "wq", 4, 128, 1024)
            wk = load_pack_resident(c, p + "wk", 4, 128, 1024)
            wv = load_pack_resident(c, p + "wv", 8, 128, 512)
            hb = mk_chunks(c, "hb", BF16, n=8, F=512)
            qraw = [c.sb(c.name("qraw"), [128, 512], BF16) for _ in range(2)]
            sqt = [c.sb(c.name("sqt"), [128, 512], BF16) for _ in range(2)]
            rln = c.sb(c.name("rln"), [2, 512], BF16)
            rBt = [c.sb(c.name("rBt"), [128, 512], BF16) for _ in range(2)]
            t0b = [c.sb(c.name("t0b"), [128, 512], BF16) for _ in range(2)]
            tmpR = [c.sb(c.name("tmpR"), [128, 512], F32) for _ in range(2)]
            for itb, tb in enumerate(TBS):
                for ci in range(8):
                    e = "dve" if ci % 2 == 0 else "pool"
                    E = nc.vector if ci % 2 == 0 else nc.gpsimd
                    if ci < 4:
                        tk.emit(e, lambda ci=ci, E=E, tb=tb: E.tensor_copy(
                            hb[ci][:, :], y[ci][:, tb:tb + 512]),
                            reads=[R(y[ci], tb, tb + 512)], writes=[R(hb[ci])])
                    elif tb > 0:
                        tk.emit(e, lambda ci=ci, E=E, tb=tb: E.tensor_copy(
                            hb[ci][:, :], y[ci][:, tb - 1:tb + 511]),
                            reads=[R(y[ci], tb - 1, tb + 511)], writes=[R(hb[ci])])
                    else:
                        tk.emit(e, lambda ci=ci, E=E: E.tensor_copy(
                            hb[ci][:, 1:512], y[ci][:, 0:511]),
                            reads=[R(y[ci], 0, 511)], writes=[R(hb[ci])])
                        tk.emit(e, lambda ci=ci, E=E: E.memset(hb[ci][:, 0:1], 0.0),
                                reads=[], writes=[R(hb[ci], 0, 1)])
                for (W, dst, scol, CSx, SSx) in ((wq, qeff, qs_col, CSq, SSq),
                                                 (wk, keff, ks_col, CSk, SSk)):
                    for oc in range(4):
                        ii = oc % 2
                        psq = ps[ii]
                        for dc in range(8):
                            tk.emit("pe", lambda W=W, oc=oc, dc=dc, psq=psq: nc.tensor.matmul(
                                psq[:, 0:512], W[oc][:, dc * 128:(dc + 1) * 128],
                                hb[dc][:, :], start=(dc == 0), stop=(dc == 7)),
                                reads=[(id(W[oc]), dc * 128, (dc + 1) * 128),
                                       R(hb[dc])],
                                writes=[R(psq, 0, 512)])
                        qr = qraw[ii]
                        tk.emit("dve", lambda psq=psq, qr=qr: nc.vector.tensor_copy(
                            qr[:, :], psq[:, 0:512]),
                            reads=[R(psq, 0, 512)], writes=[R(qr)])
                        sq = sqt[ii]
                        tk.emit("dve", lambda qr=qr, sq=sq: nc.vector.tensor_mul(
                            sq[:, :], qr[:, :], qr[:, :]),
                            reads=[R(qr)], writes=[R(sq)])
                        tk.emit("pe", lambda sq=sq: nc.tensor.matmul(
                            ps[2][0:2, 0:512], C["sel2"][:, :], sq[:, :],
                            start=True, stop=True),
                            reads=[R(sq)], writes=[R(ps[2], 0, 512)])
                        tk.emit("act", lambda: nc.scalar.activation(
                            rln[0:2, :], ps[2][0:2, 0:512], AF.Ln, bias=1e-24),
                            reads=[R(ps[2], 0, 512)], writes=[R(rln)])
                        tk.emit("act", lambda: nc.scalar.activation(
                            rln[0:2, :], rln[0:2, :], AF.Exp, scale=-0.5),
                            reads=[R(rln)], writes=[R(rln)])
                        tk.emit("pe", lambda: nc.tensor.matmul(
                            ps[3][:, 0:512], C["selB"][0:2, :], rln[0:2, :],
                            start=True, stop=True),
                            reads=[R(rln)], writes=[R(ps[3], 0, 512)])
                        rb = rBt[ii]
                        tk.emit("dve", lambda rb=rb: nc.vector.tensor_copy(
                            rb[:, :], ps[3][:, 0:512]),
                            reads=[R(ps[3], 0, 512)], writes=[R(rb)])
                        t0x = t0b[ii]
                        tk.emit("dve", lambda qr=qr, t0x=t0x, rb=rb, scol=scol: nc.vector.scalar_tensor_tensor(
                            t0x[:, :], qr[:, :], scol[:, :], rb[:, :],
                            ALU.mult, ALU.mult),
                            reads=[R(qr), R(scol), R(rb)], writes=[R(t0x)])
                        rope_tile(c, "pool" if ii else "dve", t0x, 0,
                                  dst[oc], (tb, tb + 512), CSx, SSx, tb, tmpR[ii])
                for tt in range(4 * itb, 4 * itb + 4):
                    lo = (tt - 4 * itb) * 128
                    tk.emit("dve", lambda tt=tt: nc.vector.memset(vtok[tt][:, :, :], 1.0),
                            writes=[R(vtok[tt])])
                    psv = ps[4 + tt % 2]
                    for dc in range(8):
                        tk.emit("pe", lambda tt=tt, dc=dc, psv=psv, lo=lo: nc.tensor.matmul(
                            psv[:, 0:512], hb[dc][:, lo:lo + 128],
                            wv[dc][:, :], start=(dc == 0), stop=(dc == 7)),
                            reads=[R(hb[dc], lo, lo + 128), (id(wv[dc]), 0, 512)],
                            writes=[R(psv, 0, 512)])
                    tk.emit("dve", lambda tt=tt, psv=psv: nc.vector.tensor_copy(
                        vtok[tt][:, :, 0:64],
                        psv[:, 0:512].rearrange("p (h d) -> p h d", h=8)),
                        reads=[R(psv, 0, 512)], writes=[R(vtok[tt])])
        with scope(c):
            OC = [c.sb(c.name("oc"), [128, 512], BF16) for _ in range(4)]
            ebuf = [c.sb(c.name("eb"), [128, 512], BF16) for _ in range(3)]
            aB = [c.sb(c.name("aB"), [128, 512], F32) for _ in range(2)]
            wo = load_pack_resident(c, p + "wo", 8, 128, 512)

            def _out(h, qb, psO, a):
                dst = OC[h // 2]
                b = 64 * (h % 2)
                tk.emit("dve", lambda dst=dst, b=b, psO=psO, a=a: nc.vector.tensor_mul(
                    dst[b:b + 64, :], psO[0:64, 0:512], a[0:64, :]),
                    reads=[R(psO, 0, 512), R(a)], writes=[R(dst)])

            def _ktl(qb):
                out = []
                for kt in range(max(0, 4 * qb - 4), 4 * qb + 4):
                    j = kt - 4 * qb + 4
                    out.append((kt, (C["mask8"], j * 512, (j + 1) * 512)))
                return out

            cfg = dict(nh=8,
                       qmap=lambda h: (qeff[h // 2], 64 * (h % 2)),
                       kmap=lambda h: (keff[h // 2], 64 * (h % 2)),
                       vsl=lambda kt, h: (vtok[kt][:, h, :], R(vtok[kt])),
                       ktlist=_ktl, scale=8.0, negc=negc,
                       psS=[ps[0], ps[1], ps[4]], psO=ps[2:4], psA=ps[5], rows=rows,
                       arows=Rows(c, c.name("arows"), 1024),
                       ebuf=ebuf, aB=aB, out=_out)
            for qb in range(4):
                cfg1 = dict(cfg)
                cfg1["ktlist"] = lambda q, qb=qb: _ktl(qb)
                _attn64_one_qb(c, C, cfg1, qb)
                for oc in range(8):
                    psW = ps[6 + oc % 2]
                    for dc in range(4):
                        tk.emit("pe", lambda oc=oc, dc=dc, psW=psW: nc.tensor.matmul(
                            psW[:, 0:512], wo[oc][:, dc * 128:(dc + 1) * 128],
                            OC[dc][:, :], start=(dc == 0), stop=(dc == 3)),
                            reads=[(id(wo[oc]), dc * 128, (dc + 1) * 128), R(OC[dc])],
                            writes=[R(psW, 0, 512)])
                    tk.emit("dve", lambda oc=oc, qb=qb, psW=psW: nc.vector.tensor_add(
                        y[oc][:, qb * 512:(qb + 1) * 512], psW[:, 0:512],
                        y[oc][:, qb * 512:(qb + 1) * 512]),
                        reads=[R(psW, 0, 512), R(y[oc], qb * 512, (qb + 1) * 512)],
                        writes=[R(y[oc], qb * 512, (qb + 1) * 512)])
        layer_scope.__exit__(None, None, None)
        with scope(c):
            hln = mk_chunks(c, "hln", BF16)
            with scope(c):
                h2b = mk_chunks(c, "h2b", BF16, n=8, F=512)
                lng = load_f32(c, p + "lng", [128, 8])
                lnb = load_f32(c, p + "lnb", [128, 8])
                L = LNState(c, ps[0:3])
                for itb, tb in enumerate(TBS):
                    for ci in range(8):
                        e = "dve" if ci % 2 == 0 else "pool"
                        E = nc.vector if ci % 2 == 0 else nc.gpsimd
                        if ci < 4:
                            tk.emit(e, lambda ci=ci, E=E, tb=tb: E.tensor_copy(
                                h2b[ci][:, :], y[ci][:, tb:tb + 512]),
                                reads=[R(y[ci], tb, tb + 512)], writes=[R(h2b[ci])])
                        elif tb > 0:
                            tk.emit(e, lambda ci=ci, E=E, tb=tb: E.tensor_copy(
                                h2b[ci][:, :], y[ci][:, tb - 1:tb + 511]),
                                reads=[R(y[ci], tb - 1, tb + 511)], writes=[R(h2b[ci])])
                        else:
                            tk.emit(e, lambda ci=ci, E=E: E.tensor_copy(
                                h2b[ci][:, 1:512], y[ci][:, 0:511]),
                                reads=[R(y[ci], 0, 511)], writes=[R(h2b[ci])])
                            tk.emit(e, lambda ci=ci, E=E: E.memset(h2b[ci][:, 0:1], 0.0),
                                    reads=[], writes=[R(h2b[ci], 0, 1)])
                    ln_tb(c, C, L, h2b, tb, [hln], gamma=lng, beta=lnb, src_tb=0)
            b1c = load_f32(c, p + "b1", [128, 44])
            b2c = load_f32(c, p + "b2", [128, 8])
            w1a = WStream(c, p + "w1", 22, 128, 1024, nbuf=3)
            w1g = WStream(c, p + "w1", 22, 128, 1024, nbuf=3, base=22)
            w2s = WStream(c, p + "w2", 8, 128, 2816, nbuf=3)
            h3 = [c.sb(c.name("h3"), [128, 1024], BF16) for _ in range(22)]
            at = [c.sb(c.name("at"), [128, 512], BF16) for _ in range(2)]
            gt = [c.sb(c.name("gt"), [128, 512], BF16) for _ in range(2)]
            for th in range(2):
                w1a.fetched = -1
                w1g.fetched = -1
                w2s.fetched = -1
                w1a.fetch(0); w1g.fetch(0); w1a.fetch(1); w1g.fetch(1)
                for i in range(22):
                    wa, wareg = w1a.get(i)
                    wg, wgreg = w1g.get(i)
                    for tb2 in range(2):
                        tb = th * 1024 + tb2 * 512
                        psA_, psG_ = ps[0 + tb2], ps[2 + tb2]
                        for dc in range(8):
                            tk.emit("pe", lambda wa=wa, dc=dc, tb=tb, psA_=psA_: nc.tensor.matmul(
                                psA_[:, 0:512], wa[:, dc * 128:(dc + 1) * 128],
                                hln[dc][:, tb:tb + 512], start=(dc == 0), stop=(dc == 7)),
                                reads=[(id(wa), dc * 128, (dc + 1) * 128),
                                       R(hln[dc], tb, tb + 512)],
                                writes=[R(psA_, 0, 512)])
                        for dc in range(8):
                            tk.emit("pe", lambda wg=wg, dc=dc, tb=tb, psG_=psG_: nc.tensor.matmul(
                                psG_[:, 0:512], wg[:, dc * 128:(dc + 1) * 128],
                                hln[dc][:, tb:tb + 512], start=(dc == 0), stop=(dc == 7)),
                                reads=[(id(wg), dc * 128, (dc + 1) * 128),
                                       R(hln[dc], tb, tb + 512)],
                                writes=[R(psG_, 0, 512)])
                        a_ = at[tb2]
                        g_ = gt[tb2]
                        tk.emit("act", lambda psA_=psA_, a_=a_, i=i: nc.scalar.add(
                            a_[:, :], psA_[:, 0:512], b1c[:, i:i + 1]),
                            reads=[R(psA_, 0, 512), R(b1c, i, i + 1)], writes=[R(a_)])
                        tk.emit("act", lambda psG_=psG_, g_=g_, i=i: nc.scalar.activation(
                            g_[:, :], psG_[:, 0:512], AF.Gelu, bias=b1c[:, 22 + i:23 + i]),
                            reads=[R(psG_, 0, 512), R(b1c, 22 + i, 23 + i)], writes=[R(g_)])
                        tk.emit("dve", lambda a_=a_, g_=g_, i=i, tb2=tb2: nc.vector.tensor_mul(
                            h3[i][:, tb2 * 512:(tb2 + 1) * 512], a_[:, :], g_[:, :]),
                            reads=[R(a_), R(g_)],
                            writes=[R(h3[i], tb2 * 512, (tb2 + 1) * 512)])
                    w1a.fetch(i + 2)
                    w1g.fetch(i + 2)
                w2s.fetch(0); w2s.fetch(1)
                for oc in range(8):
                    wb, wreg = w2s.get(oc)
                    for tb2 in range(2):
                        tb = th * 1024 + tb2 * 512
                        psY = ps[4 + (oc * 2 + tb2) % 2]
                        for i in range(22):
                            tk.emit("pe", lambda wb=wb, i=i, tb2=tb2, psY=psY: nc.tensor.matmul(
                                psY[:, 0:512], wb[:, i * 128:(i + 1) * 128],
                                h3[i][:, tb2 * 512:(tb2 + 1) * 512],
                                start=(i == 0), stop=(i == 21)),
                                reads=[(id(wb), i * 128, (i + 1) * 128),
                                       R(h3[i], tb2 * 512, (tb2 + 1) * 512)],
                                writes=[R(psY, 0, 512)])
                        tk.emit("dve", lambda oc=oc, tb=tb, psY=psY: nc.vector.scalar_tensor_tensor(
                            y[oc][:, tb:tb + 512], psY[:, 0:512], b2c[:, oc:oc + 1],
                            y[oc][:, tb:tb + 512], ALU.add, ALU.add),
                            reads=[R(psY, 0, 512), R(b2c, oc, oc + 1),
                                   R(y[oc], tb, tb + 512)],
                            writes=[R(y[oc], tb, tb + 512)])
                    w2s.fetch(oc + 2)
    store_chunks_f32(c, "F0T", y)


def _attn64_one_qb(c, C, cfg, qb):
    tk, nc = c.tk, c.nc
    negc = cfg["negc"]
    ktl = cfg["ktlist"](qb)
    qc0, qc1 = cfg.get("qcols", (qb * 512, (qb + 1) * 512))
    nkt = len(ktl)
    nheads = cfg["nh"]
    ebuf = cfg["ebuf"]
    ne = len(ebuf)

    def emit_S_exp(h, i):
        qt, q_base = cfg["qmap"](h)
        ktn, k_base = cfg["kmap"](h)
        kt, mask = ktl[i]
        psS = cfg["psS"][i % len(cfg["psS"])]
        tk.emit("pe", lambda: nc.tensor.matmul(
            psS[:, 0:512],
            ktn[k_base:k_base + 64, kt * 128:(kt + 1) * 128],
            qt[q_base:q_base + 64, qc0:qc1],
            start=True, stop=True),
            reads=[R(ktn, kt * 128, (kt + 1) * 128), R(qt, qc0, qc1)],
            writes=[R(psS, 0, 512)])
        E = ebuf[i % ne]
        tk.emit("act", lambda: nc.scalar.activation(
            E[:, :], psS[:, 0:512], AF.Exp, bias=negc[:, :],
            scale=float(cfg["scale"])),
            reads=[R(psS, 0, 512), R(negc)], writes=[R(E)])
        if mask is not None:
            mt, mlo, mhi = mask
            me = "dve" if i % 2 == 0 else "pool"
            Eng = nc.vector if i % 2 == 0 else nc.gpsimd
            tk.emit(me, lambda Eng=Eng: Eng.tensor_mul(
                E[:, :], E[:, :], mt[:, mlo:mlo + 512]),
                reads=[R(E), (id(mt), mlo, mhi)], writes=[R(E)])

    def emit_PV(h, i, psO):
        kt, mask = ktl[i]
        E = ebuf[i % ne]
        vap, vreg = cfg["vsl"](kt, h)
        tk.emit("pe", lambda: nc.tensor.matmul(
            psO[0:65, 0:512], vap, E[:, :],
            start=(i == 0), stop=(i == nkt - 1)),
            reads=[vreg, R(E)], writes=[R(psO, 0, 512)])

    arows = cfg.get("arows") or cfg["rows"]

    def emit_alpha(h, psO):
        arow, rarow = arows.slot(0, 512 * (h % 2))
        tk.emit("dve", lambda: nc.vector.reciprocal(arow, psO[64:65, 0:512]),
                reads=[R(psO, 0, 512)], writes=[rarow])
        psA = cfg["psA"][h % len(cfg["psA"])] if isinstance(cfg["psA"], list) \
            else cfg["psA"]
        tk.emit("pe", lambda: nc.tensor.matmul(
            psA[0:64, 0:512], C["ones_row_f"][0:1, 0:64], arow,
            start=True, stop=True),
            reads=[rarow], writes=[R(psA, 0, 512)])
        aB = cfg["aB"][h % 2]
        tk.emit("act", lambda: nc.scalar.copy(aB[0:64, :], psA[0:64, 0:512]),
                reads=[R(psA, 0, 512)], writes=[R(aB)])
        cfg["out"](h, qb, psO, aB)

    pend = None
    for h in range(nheads):
        psO = cfg["psO"][h % len(cfg["psO"])]
        for i in range(nkt):
            emit_S_exp(h, i)
            if i > 0:
                emit_PV(h, i - 1, psO)
        emit_PV(h, nkt - 1, psO)
        if pend is not None:
            emit_alpha(*pend)
        pend = (h, psO)
    emit_alpha(*pend)

def stage_mid(c, C):
    tk, nc = c.tk, c.nc
    ps = [c.ps(c.name("ps")) for _ in range(3)]
    x1 = mk_chunks(c, "x1", F32)
    load_chunks_f32(c, "xT", dst=x1)
    tmp = [c.sb(c.name("mtmp"), [128, 2048], F32) for _ in range(2)]
    for name in ("A0T", "F0T"):
        src = c.dram[name]
        for ci in range(8):
            t = tmp[ci % 2]
            tk.emit("sq", lambda ci=ci, t=t, src=src: nc.sync.dma_start(
                out=t[:, :], in_=src[ci * 128:(ci + 1) * 128, :]),
                writes=[R(t)])
            e = "dve" if ci % 2 == 0 else "pool"
            E = nc.vector if ci % 2 == 0 else nc.gpsimd
            tk.emit(e, lambda ci=ci, t=t, E=E: E.tensor_add(
                x1[ci][:, :], x1[ci][:, :], t[:, :]),
                reads=[R(x1[ci]), R(t)], writes=[R(x1[ci])])
    store_chunks_f32(c, "x1T", x1)
    xn1 = mk_chunks(c, "xn1", F32)
    L = LNState(c, ps)
    for tb in TBS:
        ln_tb(c, C, L, x1, tb, [xn1])
    store_chunks_f32(c, "xn1T", xn1)


def stage_fin(c, C):
    tk, nc = c.tk, c.nc
    acc = [c.sb(c.name("facc"), [128, 2048], F32) for _ in range(2)]
    tmp = [c.sb(c.name("ftmp"), [128, 2048], F32) for _ in range(2)]
    for ci in range(8):
        a = acc[ci % 2]
        tk.emit("sq", lambda ci=ci, a=a: nc.sync.dma_start(
            out=a[:, :], in_=c.dram["x1T"][ci * 128:(ci + 1) * 128, :]),
            writes=[R(a)])
        for name in ("A1T", "F1T"):
            t = tmp[ci % 2]
            tk.emit("sq", lambda ci=ci, t=t, name=name: nc.sync.dma_start(
                out=t[:, :], in_=c.dram[name][ci * 128:(ci + 1) * 128, :]),
                writes=[R(t)])
            e = "dve" if ci % 2 == 0 else "pool"
            E = nc.vector if ci % 2 == 0 else nc.gpsimd
            tk.emit(e, lambda a=a, t=t, E=E: E.tensor_add(a[:, :], a[:, :], t[:, :]),
                    reads=[R(a), R(t)], writes=[R(a)])
        tk.emit("sq", lambda ci=ci, a=a: nc.sync.dma_start(
            out=c.dram["outT"][ci * 128:(ci + 1) * 128, :], in_=a[:, :]),
            reads=[R(a)])


def stage_l1_attn(c, C):
    tk, nc = c.tk, c.nc
    ps = [c.ps(c.name("ps")) for _ in range(8)]
    rows = Rows(c, "l1a_rows", 1024)
    xn1b = mk_chunks(c, "xn1b", BF16)
    load_chunks_f32(c, "xn1T", cast_bf16=xn1b)
    CS = load_f32(c, "l1_CS", [128, 2048])
    SS = load_f32(c, "l1_SS", [128, 2048])
    wqa = load_pack_resident(c, "l1a_wqa", 1, 128, 512)[0]
    wkva = load_pack_resident(c, "l1a_wkva", 1, 128, 256)[0]
    wqb = load_pack_resident(c, "l1a_wqb", 8, 128, 128)
    wkrep = c.sb("wkrep", [128, 512], BF16)
    tk.emit("gq", lambda: nc.gpsimd.dma_start(out=wkrep[:, :], in_=c.dram["l1a_wk"][:, :]),
            writes=[R(wkrep)])
    wvp = c.sb("wvp", [128, 256], BF16)
    tk.emit("gq", lambda: nc.gpsimd.dma_start(out=wvp[:, :], in_=c.dram["l1a_wv"][:, :]),
            writes=[R(wvp)])
    t1T = c.sb("t1T", [64, 2048], BF16)
    t2T = c.sb("t2T", [32, 2048], BF16)
    for itb, tb in enumerate(TBS):
        ps0 = ps[itb % 2]
        for dc in range(8):
            tk.emit("pe", lambda dc=dc, tb=tb, ps0=ps0: nc.tensor.matmul(
                ps0[0:64, 0:512], wqa[:, dc * 64:(dc + 1) * 64],
                xn1b[dc][:, tb:tb + 512], start=(dc == 0), stop=(dc == 7)),
                reads=[(id(wqa), dc * 64, (dc + 1) * 64), R(xn1b[dc], tb, tb + 512)],
                writes=[R(ps0, 0, 512)])
        tk.emit("act", lambda tb=tb, ps0=ps0: nc.scalar.copy(
            t1T[:, tb:tb + 512], ps0[0:64, 0:512]),
            reads=[R(ps0, 0, 512)], writes=[R(t1T, tb, tb + 512)])
        ps1 = ps[2 + itb % 2]
        for dc in range(8):
            tk.emit("pe", lambda dc=dc, tb=tb, ps1=ps1: nc.tensor.matmul(
                ps1[0:32, 0:512], wkva[:, dc * 32:(dc + 1) * 32],
                xn1b[dc][:, tb:tb + 512], start=(dc == 0), stop=(dc == 7)),
                reads=[(id(wkva), dc * 32, (dc + 1) * 32), R(xn1b[dc], tb, tb + 512)],
                writes=[R(ps1, 0, 512)])
        tk.emit("act", lambda tb=tb, ps1=ps1: nc.scalar.copy(
            t2T[:, tb:tb + 512], ps1[0:32, 0:512]),
            reads=[R(ps1, 0, 512)], writes=[R(t2T, tb, tb + 512)])
    qeff = mk_chunks(c, "qeff", BF16, n=8)
    keff = mk_chunks(c, "keff", BF16, n=4)
    vtok = [c.sb(c.name("vtok"), [128, 4, 65], BF16) for _ in range(16)]
    qraw = [c.sb(c.name("qraw"), [128, 512], BF16) for _ in range(2)]
    tmpR = [c.sb(c.name("tmpR"), [128, 512], F32) for _ in range(2)]
    for oc in range(8):
        for itb, tb in enumerate(TBS):
            ii = (oc * 4 + itb) % 2
            psq = ps[ii]
            tk.emit("pe", lambda oc=oc, tb=tb, psq=psq: nc.tensor.matmul(
                psq[:, 0:512], wqb[oc][0:64, :], t1T[:, tb:tb + 512],
                start=True, stop=True),
                reads=[(id(wqb[oc]), 0, 128), R(t1T, tb, tb + 512)],
                writes=[R(psq, 0, 512)])
            qr = qraw[ii]
            tk.emit("dve", lambda psq=psq, qr=qr: nc.vector.tensor_copy(qr[:, :], psq[:, 0:512]),
                    reads=[R(psq, 0, 512)], writes=[R(qr)])
            rope_tile(c, "pool" if ii else "dve", qr, 0, qeff[oc],
                      (tb, tb + 512), CS, SS, tb, tmpR[ii])
    for oc in range(4):
        for itb, tb in enumerate(TBS):
            ii = (oc * 4 + itb) % 2
            psq = ps[2 + ii]
            tk.emit("pe", lambda oc=oc, tb=tb, psq=psq: nc.tensor.matmul(
                psq[:, 0:512], wkrep[0:32, oc * 128:(oc + 1) * 128],
                t2T[:, tb:tb + 512], start=True, stop=True),
                reads=[R(wkrep, oc * 128, (oc + 1) * 128), R(t2T, tb, tb + 512)],
                writes=[R(psq, 0, 512)])
            qr = qraw[ii]
            tk.emit("dve", lambda psq=psq, qr=qr: nc.vector.tensor_copy(qr[:, :], psq[:, 0:512]),
                    reads=[R(psq, 0, 512)], writes=[R(qr)])
            rope_tile(c, "pool" if ii else "dve", qr, 0, keff[oc],
                      (tb, tb + 512), CS, SS, tb, tmpR[ii])
    for tt in range(16):
        tk.emit("dve", lambda tt=tt: nc.vector.memset(vtok[tt][:, :, :], 1.0),
                writes=[R(vtok[tt])])
        psv = ps[4 + tt % 2]
        tk.emit("pe", lambda tt=tt, psv=psv: nc.tensor.matmul(
            psv[:, 0:256], t2T[:, tt * 128:(tt + 1) * 128], wvp[0:32, :],
            start=True, stop=True),
            reads=[R(t2T, tt * 128, (tt + 1) * 128), R(wvp)],
            writes=[R(psv, 0, 256)])
        tk.emit("dve", lambda tt=tt, psv=psv: nc.vector.tensor_copy(
            vtok[tt][:, :, 0:64],
            psv[:, 0:256].rearrange("p (h d) -> p h d", h=4)),
            reads=[R(psv, 0, 256)], writes=[R(vtok[tt])])
    sqt = [c.sb(c.name("sqt"), [128, 512], BF16) for _ in range(2)]
    negc = negc_from_norms(c, C, [ps[6]], rows, qeff, keff, 1.0 / 8.0, sqt)
    if "DBGT" in c.dram:
        tk.emit("gq", lambda: nc.gpsimd.dma_start(
            out=c.dram["DBGT"][0:128, 0:2048], in_=qeff[0][:, :]), reads=[R(qeff[0])])
        tk.emit("gq", lambda: nc.gpsimd.dma_start(
            out=c.dram["DBGT"][128:256, 0:2048], in_=keff[0][:, :]), reads=[R(keff[0])])
        tk.emit("gq", lambda: nc.gpsimd.dma_start(
            out=c.dram["DBGT"][256:384, 0:2048], in_=qeff[7][:, :]), reads=[R(qeff[7])])
        tk.emit("sq", lambda: nc.sync.dma_start(
            out=c.dram["DBGT"][384:512, 0:1], in_=negc[:, :],
            allow_slow_non_contiguous=True), reads=[R(negc)])
        tk.emit("gq", lambda: nc.gpsimd.dma_start(
            out=c.dram["DBGT"][512:576, 0:2048], in_=t1T[:, :]), reads=[R(t1T)])
        tk.emit("gq", lambda: nc.gpsimd.dma_start(
            out=c.dram["DBGT"][576:704, 0:260], in_=vtok[0][:, :, :]), reads=[R(vtok[0])])
    octx = mk_chunks(c, "octx", BF16, n=8, F=512)
    ebuf = [c.sb(c.name("eb"), [128, 512], BF16) for _ in range(3)]
    aBs = [c.sb(c.name("aB"), [128, 512], F32) for _ in range(2)]
    wout = load_pack_resident(c, "l1a_wout", 8, 128, 1024)
    boutc = load_f32(c, "l1a_bout", [128, 8])
    a1t = [c.sb(c.name("a1t"), [128, 512], F32) for _ in range(2)]

    def _out(h, qb, psO, a):
        dst = octx[h // 2]
        b = 64 * (h % 2)
        tk.emit("dve", lambda dst=dst, b=b, psO=psO, a=a: nc.vector.tensor_mul(
            dst[b:b + 64, :], psO[0:64, 0:512], a[0:64, :]),
            reads=[R(psO, 0, 512), R(a)], writes=[R(dst)])

    cfg = dict(nh=16,
               qmap=lambda h: (qeff[h // 2], 64 * (h % 2)),
               kmap=lambda h: (keff[h // 4], 64 * (h % 2)),
               vsl=lambda kt, h: (vtok[kt][:, h // 4, :], R(vtok[kt])),
               scale=1.0 / 8.0, negc=negc,
               psS=[ps[0], ps[1], ps[4]], psO=ps[2:4], psA=ps[5], rows=rows,
               arows=Rows(c, c.name("arows"), 1024),
               ebuf=ebuf, aB=aBs, out=_out)
    for qb in range(4):
        cfg["ktlist"] = lambda q, qb=qb: [
            (kt, (C["maskc4"], (kt - 4 * qb) * 512, (kt - 4 * qb + 1) * 512)
             if kt >= 4 * qb else None)
            for kt in range(4 * qb + 4)]
        _attn64_one_qb(c, C, cfg, qb)
        for oc in range(8):
            psW = ps[6 + oc % 2]
            for dc in range(8):
                tk.emit("pe", lambda oc=oc, dc=dc, psW=psW: nc.tensor.matmul(
                    psW[:, 0:512], wout[oc][:, dc * 128:(dc + 1) * 128],
                    octx[dc][:, :], start=(dc == 0), stop=(dc == 7)),
                    reads=[(id(wout[oc]), dc * 128, (dc + 1) * 128), R(octx[dc])],
                    writes=[R(psW, 0, 512)])
            a1 = a1t[oc % 2]
            tk.emit("act", lambda oc=oc, psW=psW, a1=a1: nc.scalar.add(
                a1[:, :], psW[:, 0:512], boutc[:, oc:oc + 1]),
                reads=[R(psW, 0, 512), R(boutc, oc, oc + 1)], writes=[R(a1)])
            tk.emit("sq", lambda oc=oc, qb=qb, a1=a1: nc.sync.dma_start(
                out=c.dram["A1T"][oc * 128:(oc + 1) * 128, qb * 512:(qb + 1) * 512],
                in_=a1[:, :]),
                reads=[R(a1)])


def stage_l1_ffn(c, C):
    tk, nc = c.tk, c.nc
    ps = [c.ps(c.name("ps")) for _ in range(8)]
    rows = Rows(c, "l1f_rows", 1024)
    aft = c.nc.dram_tensor("AFt", [1024, 2048], F32, kind="Internal").ap()
    with scope(c):
        CS = load_f32(c, "l1_CS", [128, 2048])
        SS = load_f32(c, "l1_SS", [128, 2048])
        xb = mk_chunks(c, "xb", BF16, n=8, F=512)
        qin = mk_chunks(c, "qin", BF16, n=8, F=512)
        qe = mk_chunks(c, "qe", BF16, n=8, F=512)
        ke = mk_chunks(c, "ke", BF16, n=8)
        vtok = [c.sb(c.name("vtok"), [128, 16, 65], BF16) for _ in range(16)]
        qraw = [c.sb(c.name("qraw"), [128, 512], BF16) for _ in range(2)]
        tmpR = [c.sb(c.name("tmpR"), [128, 512], F32) for _ in range(2)]
        sqt = [c.sb(c.name("sqt"), [128, 512], BF16) for _ in range(2)]
        octx = mk_chunks(c, "octx", BF16, n=8, F=512)
        ebuf = [c.sb(c.name("eb"), [128, 512], BF16) for _ in range(3)]
        aBs = [c.sb(c.name("aB"), [128, 512], F32) for _ in range(2)]
        negc = c.sb("negcl1f", [128, 1], F32)
        arows_l1f = Rows(c, c.name("arows"), 1024)
        wqs = WStream(c, "l1f_wq", 8, 128, 1024, nbuf=8)
        wks = WStream(c, "l1f_wk", 8, 128, 1024, nbuf=8)
        wos = WStream(c, "l1f_wo", 8, 128, 1024, nbuf=2)
        bqc = load_f32(c, "l1f_bq", [128, 8])
        bkc = load_f32(c, "l1f_bk", [128, 8])
        boc = load_f32(c, "l1f_bo", [128, 8])
        wvr = load_pack_resident(c, "l1f_wv", 8, 128, 1024)
        bvrow = c.sb("bvrow", [128, 1024], BF16)
        tk.emit("gq", lambda: nc.gpsimd.dma_start(out=bvrow[:, :], in_=c.dram["l1f_bv"][:, :]),
                writes=[R(bvrow)])
        kmax, rkmax = rows.slot(0, 520, 1)
        qmax, rqmax = rows.slot(0, 521, 1)
        cc, rcc = rows.slot(0, 522, 1)
        tmpm, rtmpm = rows.slot(0, 523, 1)

        def _norm_max(tiles, cols, dst, rdst, first):
            psN = ps[6]
            for i, qt in enumerate(tiles):
                sq = sqt[i % 2]
                tk.emit("act", lambda qt=qt, sq=sq: nc.scalar.activation(
                    sq[:, :], qt[:, cols[0]:cols[1]], AF.Square),
                    reads=[R(qt, cols[0], cols[1])], writes=[R(sq)])
                tk.emit("pe", lambda sq=sq, i=i, n=len(tiles): nc.tensor.matmul(
                    psN[0:1, 0:512], C["ones_col_b"], sq[:, :],
                    start=(i == 0), stop=(i == n - 1)),
                    reads=[R(sq)], writes=[R(psN, 0, 512)])
            tk.emit("dve", lambda: nc.vector.reduce_max(tmpm, psN[0:1, 0:512], axis=AX.X),
                    reads=[R(psN, 0, 512)], writes=[rtmpm])
            if first:
                tk.emit("dve", lambda: nc.vector.tensor_copy(dst, tmpm),
                        reads=[rtmpm], writes=[rdst])
            else:
                tk.emit("dve", lambda: nc.vector.tensor_max(dst, dst, tmpm),
                        reads=[rdst, rtmpm], writes=[rdst])

        for itb, tb in enumerate(TBS):
            qb = itb
            for dc in range(8):
                tk.emit("gq", lambda dc=dc, tb=tb: nc.gpsimd.dma_start(
                    out=xb[dc][:, :],
                    in_=c.dram["xn1T"][dc * 128:(dc + 1) * 128, tb:tb + 512]),
                    writes=[R(xb[dc])])
            for dc in range(8):
                ii = dc % 2
                psq = ps[ii]
                tk.emit("pe", lambda dc=dc, psq=psq: nc.tensor.matmul(
                    psq[:, 0:512], C["deintb"][:, :], xb[dc][:, :],
                    start=True, stop=True),
                    reads=[R(C["deintb"]), R(xb[dc])], writes=[R(psq, 0, 512)])
                qr = qraw[ii]
                tk.emit("dve", lambda psq=psq, qr=qr: nc.vector.tensor_copy(qr[:, :], psq[:, 0:512]),
                        reads=[R(psq, 0, 512)], writes=[R(qr)])
                rope_tile(c, "pool" if ii else "dve", qr, 0, qin[dc],
                          (0, 512), CS, SS, tb, tmpR[ii])
            for (W, dst, bc, full) in ((wqs, qe, bqc, False), (wks, ke, bkc, True)):
                W.fetch(0); W.fetch(1)
                for oc in range(8):
                    wb, wreg = W.get(oc)
                    psq = ps[oc % 2]
                    for dc in range(8):
                        tk.emit("pe", lambda wb=wb, dc=dc, psq=psq: nc.tensor.matmul(
                            psq[:, 0:512], wb[:, dc * 128:(dc + 1) * 128],
                            qin[dc][:, :], start=(dc == 0), stop=(dc == 7)),
                            reads=[(id(wb), dc * 128, (dc + 1) * 128), R(qin[dc])],
                            writes=[R(psq, 0, 512)])
                    c0, c1 = (tb, tb + 512) if full else (0, 512)
                    tk.emit("act", lambda psq=psq, dst=dst, oc=oc, bc=bc, c0=c0, c1=c1: nc.scalar.add(
                        dst[oc][:, c0:c1], psq[:, 0:512], bc[:, oc:oc + 1]),
                        reads=[R(psq, 0, 512), R(bc, oc, oc + 1)],
                        writes=[R(dst[oc], c0, c1)])
                    W.fetch(oc + 2)
            for tt in range(4 * itb, 4 * itb + 4):
                tk.emit("dve", lambda tt=tt: nc.vector.memset(vtok[tt][:, :, :], 1.0),
                        writes=[R(vtok[tt])])
                lo = (tt - 4 * itb) * 128
                for vh in range(2):
                    psv = ps[4 + (tt * 2 + vh) % 2]
                    for dc in range(8):
                        tk.emit("pe", lambda tt=tt, vh=vh, dc=dc, psv=psv, lo=lo: nc.tensor.matmul(
                            psv[:, 0:512], xb[dc][:, lo:lo + 128],
                            wvr[dc][:, vh * 512:(vh + 1) * 512],
                            start=(dc == 0), stop=False),
                            reads=[R(xb[dc], lo, lo + 128),
                                   (id(wvr[dc]), vh * 512, (vh + 1) * 512)],
                            writes=[R(psv, 0, 512)])
                    tk.emit("pe", lambda vh=vh, psv=psv: nc.tensor.matmul(
                        psv[:, 0:512], C["ones_row_b"], bvrow[0:1, vh * 512:(vh + 1) * 512],
                        start=False, stop=True),
                        reads=[R(bvrow, vh * 512, (vh + 1) * 512)],
                        writes=[R(psv, 0, 512)])
                    tk.emit("dve", lambda tt=tt, vh=vh, psv=psv: nc.vector.tensor_copy(
                        vtok[tt][:, vh * 8:(vh + 1) * 8, 0:64],
                        psv[:, 0:512].rearrange("p (h d) -> p h d", h=8)),
                        reads=[R(psv, 0, 512)], writes=[R(vtok[tt])])
            _norm_max(ke, (tb, tb + 512), kmax, rkmax, first=(itb == 0))
            _norm_max(qe, (0, 512), qmax, rqmax, first=True)
            tk.emit("dve", lambda: nc.vector.tensor_mul(cc, qmax, kmax),
                    reads=[rqmax, rkmax], writes=[rcc])
            tk.emit("act", lambda: nc.scalar.activation(cc, cc, AF.Ln, bias=1e-24),
                    reads=[rcc], writes=[rcc])
            tk.emit("act", lambda: nc.scalar.activation(cc, cc, AF.Exp, scale=0.5),
                    reads=[rcc], writes=[rcc])
            tk.emit("pe", lambda: nc.tensor.matmul(ps[6][0:128, 0:1], C["ones_row_f"],
                                                   cc, start=True, stop=True),
                    reads=[rcc], writes=[R(ps[6], 0, 1)])
            tk.emit("act", lambda: nc.scalar.mul(negc[:, :], ps[6][0:128, 0:1], -1.0 / 8.0),
                    reads=[R(ps[6], 0, 1)], writes=[R(negc)])
            cfg = dict(nh=16,
                       qmap=lambda h: (qe[h // 2], 64 * (h % 2)),
                       kmap=lambda h: (ke[h // 2], 64 * (h % 2)),
                       vsl=lambda kt, h: (vtok[kt][:, h, :], R(vtok[kt])),
                       scale=1.0 / 8.0, negc=negc,
                       psS=[ps[0], ps[1], ps[4]], psO=ps[2:4], psA=ps[5], rows=rows,
                       arows=arows_l1f,
                       ebuf=ebuf, aB=aBs, out=None,
                       ktlist=lambda q: [
                           (kt, (C["maskc4"], (kt - 4 * qb) * 512, (kt - 4 * qb + 1) * 512)
                            if kt >= 4 * qb else None)
                           for kt in range(4 * qb + 4)],
                       qcols=(0, 512))

            def _out(h, q_, psO, a):
                dst = octx[h // 2]
                b = 64 * (h % 2)
                tk.emit("dve", lambda dst=dst, b=b, psO=psO, a=a: nc.vector.tensor_mul(
                    dst[b:b + 64, :], psO[0:64, 0:512], a[0:64, :]),
                    reads=[R(psO, 0, 512), R(a)], writes=[R(dst)])
            cfg["out"] = _out
            _attn64_one_qb(c, C, cfg, qb)
            wos.fetched = -1
            wos.fetch(0); wos.fetch(1)
            aoutt = [c.sb(c.name("aoutt"), [128, 512], F32) for _ in range(2)] \
                if itb == 0 else aoutt_hold
            aoutt_hold = aoutt
            for oc in range(8):
                wb, wreg = wos.get(oc)
                psW = ps[6 + oc % 2]
                for dc in range(8):
                    tk.emit("pe", lambda wb=wb, dc=dc, psW=psW: nc.tensor.matmul(
                        psW[:, 0:512], wb[:, dc * 128:(dc + 1) * 128],
                        octx[dc][:, :], start=(dc == 0), stop=(dc == 7)),
                        reads=[(id(wb), dc * 128, (dc + 1) * 128), R(octx[dc])],
                        writes=[R(psW, 0, 512)])
                ao = aoutt[oc % 2]
                tk.emit("act", lambda psW=psW, ao=ao, oc=oc: nc.scalar.add(
                    ao[:, :], psW[:, 0:512], boc[:, oc:oc + 1]),
                    reads=[R(psW, 0, 512), R(boc, oc, oc + 1)], writes=[R(ao)])
                tk.emit("sq", lambda ao=ao, oc=oc, tb=tb: nc.sync.dma_start(
                    out=aft[oc * 128:(oc + 1) * 128, tb:tb + 512], in_=ao[:, :]),
                    reads=[R(ao)])
                wos.fetch(oc + 2)
    x1fb = mk_chunks(c, "x1fb", BF16)
    with scope(c):
        xn1 = mk_chunks(c, "z", F32)
        load_chunks_f32(c, "xn1T", dst=xn1)
        tmp = [c.sb(c.name("ztmp"), [128, 2048], F32) for _ in range(2)]
        for ci in range(8):
            t = tmp[ci % 2]
            tk.emit("sq", lambda ci=ci, t=t: nc.sync.dma_start(
                out=t[:, :], in_=aft[ci * 128:(ci + 1) * 128, :]),
                writes=[R(t)])
            e = "dve" if ci % 2 == 0 else "pool"
            E = nc.vector if ci % 2 == 0 else nc.gpsimd
            tk.emit(e, lambda ci=ci, t=t, E=E: E.tensor_add(
                xn1[ci][:, :], xn1[ci][:, :], t[:, :]),
                reads=[R(xn1[ci]), R(t)], writes=[R(xn1[ci])])
        ln1w = load_f32(c, "l1f_ln1w", [128, 8])
        ln1b = load_f32(c, "l1f_ln1b", [128, 8])
        L = LNState(c, ps[0:3])
        for tb in TBS:
            ln_tb(c, C, L, xn1, tb, [xn1, x1fb], gamma=ln1w, beta=ln1b)
        for ci in range(8):
            tk.emit("sq", lambda ci=ci: nc.sync.dma_start(
                out=aft[ci * 128:(ci + 1) * 128, :], in_=xn1[ci][:, :]),
                reads=[R(xn1[ci])])
    glog = c.sb("glog", [8, 2048], F32)
    w8all = c.sb("w8all", [128, 128], F32)
    wrowb = c.sb("wrowb", [8, 2048], BF16)
    with scope(c):
        gw = load_pack_resident(c, "l1f_gw", 8, 128, 8)
        gb = load_f32(c, "l1f_gb", [128, 1])
        identf = load_f32(c, "ident", [128, 128])
        for itb, tb in enumerate(TBS):
            psG = ps[itb % 2]
            for dc in range(8):
                tk.emit("pe", lambda dc=dc, tb=tb, psG=psG: nc.tensor.matmul(
                    psG[0:8, 0:512], gw[dc][:, :], x1fb[dc][:, tb:tb + 512],
                    start=(dc == 0), stop=(dc == 7)),
                    reads=[(id(gw[dc]), 0, 8), R(x1fb[dc], tb, tb + 512)],
                    writes=[R(psG, 0, 512)])
            tk.emit("act", lambda tb=tb, psG=psG: nc.scalar.add(
                glog[0:8, tb:tb + 512], psG[0:8, 0:512], gb[0:8, 0:1]),
                reads=[R(psG, 0, 512), R(gb)], writes=[R(glog, tb, tb + 512)])
        g8 = c.sb("g8", [128, 8], F32)
        eqt = c.sb("eqt", [128, 8], F32)
        gtmp = c.sb("gtmp", [128, 4], F32)
        for tt in range(16):
            psT = ps[2 + tt % 2]
            tk.emit("pe", lambda tt=tt, psT=psT: nc.tensor.transpose(
                psT[0:128, 0:8], glog[0:8, tt * 128:(tt + 1) * 128],
                identf[0:8, 0:8]),
                reads=[R(glog, tt * 128, (tt + 1) * 128), R(identf)],
                writes=[R(psT, 0, 8)])
            v = nc.vector
            tk.emit("dve", lambda psT=psT: v.tensor_copy(g8[:, :], psT[0:128, 0:8]),
                    reads=[R(psT, 0, 8)], writes=[R(g8)])
            tk.emit("dve", lambda: v.reduce_max(gtmp[:, 0:1], g8[:, :], axis=AX.X),
                    reads=[R(g8)], writes=[R(gtmp, 0, 1)])
            tk.emit("dve", lambda: v.tensor_scalar_mul(gtmp[:, 1:2], gtmp[:, 0:1], -1.0),
                    reads=[R(gtmp, 0, 1)], writes=[R(gtmp, 1, 2)])
            tk.emit("act", lambda: nc.scalar.activation(
                g8[:, :], g8[:, :], AF.Exp, bias=gtmp[:, 1:2]),
                reads=[R(g8), R(gtmp, 1, 2)], writes=[R(g8)])
            tk.emit("dve", lambda: v.reduce_sum(gtmp[:, 2:3], g8[:, :], axis=AX.X),
                    reads=[R(g8)], writes=[R(gtmp, 2, 3)])
            tk.emit("dve", lambda: v.reciprocal(gtmp[:, 3:4], gtmp[:, 2:3]),
                    reads=[R(gtmp, 2, 3)], writes=[R(gtmp, 3, 4)])
            tk.emit("dve", lambda: v.tensor_scalar_mul(g8[:, :], g8[:, :], gtmp[:, 3:4]),
                    reads=[R(g8), R(gtmp, 3, 4)], writes=[R(g8)])
            tk.emit("dve", lambda: v.reduce_max(gtmp[:, 0:1], g8[:, :], axis=AX.X),
                    reads=[R(g8)], writes=[R(gtmp, 0, 1)])
            tk.emit("dve", lambda: v.tensor_single_scalar(
                eqt[:, :], g8[:, :], gtmp[:, 0:1], ALU.is_ge),
                reads=[R(g8), R(gtmp, 0, 1)], writes=[R(eqt)])
            tk.emit("dve", lambda: v.scalar_tensor_tensor(
                eqt[:, :], eqt[:, :], -2.0, g8[:, :], ALU.mult, ALU.add),
                reads=[R(eqt), R(g8)], writes=[R(eqt)])
            tk.emit("dve", lambda: v.reduce_max(gtmp[:, 1:2], eqt[:, :], axis=AX.X),
                    reads=[R(eqt)], writes=[R(gtmp, 1, 2)])
            tk.emit("dve", lambda: v.tensor_single_scalar(
                eqt[:, :], g8[:, :], gtmp[:, 1:2], ALU.is_ge),
                reads=[R(g8), R(gtmp, 1, 2)], writes=[R(eqt)])
            tk.emit("dve", lambda tt=tt: v.tensor_mul(
                w8all[:, tt * 8:(tt + 1) * 8], g8[:, :], eqt[:, :]),
                reads=[R(g8), R(eqt)], writes=[R(w8all, tt * 8, (tt + 1) * 8)])
            psT2 = ps[4 + tt % 2]
            tk.emit("pe", lambda tt=tt, psT2=psT2: nc.tensor.transpose(
                psT2[0:8, 0:128], w8all[:, tt * 8:(tt + 1) * 8], identf[:, :]),
                reads=[R(w8all, tt * 8, (tt + 1) * 8), R(identf)],
                writes=[R(psT2, 0, 128)])
            tk.emit("act", lambda tt=tt, psT2=psT2: nc.scalar.copy(
                wrowb[0:8, tt * 128:(tt + 1) * 128], psT2[0:8, 0:128]),
                reads=[R(psT2, 0, 128)], writes=[R(wrowb, tt * 128, (tt + 1) * 128)])
    macc = mk_chunks(c, "macc", F32)
    with scope(c):
        sel8 = c.sb("sel8_s", [128, 1024], BF16)
        tk.emit("gq", lambda: nc.gpsimd.dma_start(out=sel8[:, :], in_=c.dram["sel8"][:, :]),
                writes=[R(sel8)])
        ebc = load_f32(c, "l1f_eb", [128, 64])
        ewS = WStream(c, "l1f_ew", 64, 128, 1024, nbuf=4)
        wBfull = c.sb("wBfull", [128, 2048], BF16)
        tmpE = [c.sb(c.name("tmpE"), [128, 512], BF16) for _ in range(2)]
        ewS.fetch(0); ewS.fetch(1); ewS.fetch(2); ewS.fetch(3)
        for e in range(8):
            for itb, tb in enumerate(TBS):
                psB = ps[6]
                tk.emit("pe", lambda e=e, tb=tb, psB=psB: nc.tensor.matmul(
                    psB[:, 0:512], sel8[0:8, e * 128:(e + 1) * 128],
                    wrowb[0:8, tb:tb + 512], start=True, stop=True),
                    reads=[R(sel8, e * 128, (e + 1) * 128), R(wrowb, tb, tb + 512)],
                    writes=[R(psB, 0, 512)])
                tk.emit("dve", lambda tb=tb, psB=psB: nc.vector.tensor_copy(
                    wBfull[:, tb:tb + 512], psB[:, 0:512]),
                    reads=[R(psB, 0, 512)], writes=[R(wBfull, tb, tb + 512)])
            for oc in range(8):
                wb, wreg = ewS.get(e * 8 + oc)
                for itb, tb in enumerate(TBS):
                    psE = ps[(oc * 4 + itb) % 2]
                    for dc in range(8):
                        tk.emit("pe", lambda wb=wb, dc=dc, tb=tb, psE=psE: nc.tensor.matmul(
                            psE[:, 0:512], wb[:, dc * 128:(dc + 1) * 128],
                            x1fb[dc][:, tb:tb + 512], start=(dc == 0), stop=(dc == 7)),
                            reads=[(id(wb), dc * 128, (dc + 1) * 128),
                                   R(x1fb[dc], tb, tb + 512)],
                            writes=[R(psE, 0, 512)])
                    col = e * 8 + oc
                    if e == 0:
                        tk.emit("dve", lambda oc=oc, tb=tb, psE=psE, col=col: nc.vector.scalar_tensor_tensor(
                            macc[oc][:, tb:tb + 512], psE[:, 0:512], ebc[:, col:col + 1],
                            wBfull[:, tb:tb + 512], ALU.add, ALU.mult),
                            reads=[R(psE, 0, 512), R(ebc, col, col + 1),
                                   R(wBfull, tb, tb + 512)],
                            writes=[R(macc[oc], tb, tb + 512)])
                    else:
                        tE = tmpE[itb % 2]
                        tk.emit("dve", lambda tE=tE, psE=psE, col=col, tb=tb: nc.vector.scalar_tensor_tensor(
                            tE[:, :], psE[:, 0:512], ebc[:, col:col + 1],
                            wBfull[:, tb:tb + 512], ALU.add, ALU.mult),
                            reads=[R(psE, 0, 512), R(ebc, col, col + 1),
                                   R(wBfull, tb, tb + 512)],
                            writes=[R(tE)])
                        tk.emit("pool", lambda oc=oc, tb=tb, tE=tE: nc.gpsimd.tensor_add(
                            macc[oc][:, tb:tb + 512], macc[oc][:, tb:tb + 512], tE[:, :]),
                            reads=[R(macc[oc], tb, tb + 512), R(tE)],
                            writes=[R(macc[oc], tb, tb + 512)])
                ewS.fetch(e * 8 + oc + 4)
    with scope(c):
        tmp = [c.sb(c.name("z2tmp"), [128, 2048], F32) for _ in range(2)]
        for ci in range(8):
            t = tmp[ci % 2]
            tk.emit("sq", lambda ci=ci, t=t: nc.sync.dma_start(
                out=t[:, :], in_=aft[ci * 128:(ci + 1) * 128, :]),
                writes=[R(t)])
            e = "dve" if ci % 2 == 0 else "pool"
            E = nc.vector if ci % 2 == 0 else nc.gpsimd
            tk.emit(e, lambda ci=ci, t=t, E=E: E.tensor_add(
                macc[ci][:, :], macc[ci][:, :], t[:, :]),
                reads=[R(macc[ci]), R(t)], writes=[R(macc[ci])])
    with scope(c):
        ln2w = load_f32(c, "l1f_ln2w", [128, 8])
        ln2b = load_f32(c, "l1f_ln2b", [128, 8])
        L = LNState(c, ps[0:3])
        for tb in TBS:
            ln_tb(c, C, L, macc, tb, [macc], gamma=ln2w, beta=ln2b)
    store_chunks_f32(c, "F1T", macc)



_L0F_W = []
for _l in range(2):
    _L0F_W += [f"l0f{_l}_{s}" for s in
               ("wq", "wk", "wv", "wo", "qs", "ks", "qsr", "ksr",
                "lng", "lnb", "w1", "b1", "w2", "b2")]

STAGES = {
    "ln0": (stage_ln0, ["xT"], (), ["xnT"]),
    "l0attn": (stage_l0_attn, ["xnT", "l0a_qw", "l0a_kw", "l0a_vw", "maskc4"],
               ("maskc4",), ["A0T"]),
    "l0ffn": (stage_l0_ffn,
              ["xnT", "mask8", "sel2", "selB",
               "l0_CSq", "l0_SSq", "l0_CSk", "l0_SSk"] + _L0F_W,
              ("mask8", "sel2"), ["F0T"]),
    "mid": (stage_mid, ["xT", "A0T", "F0T"], (), ["x1T", "xn1T"]),
    "l1attn": (stage_l1_attn,
               ["xn1T", "l1a_wqa", "l1a_wqb", "l1a_wkva", "l1a_wk", "l1a_wv",
                "l1a_wout", "l1a_bout", "l1_CS", "l1_SS", "maskc4"],
               ("maskc4",), ["A1T"]),
    "l1ffn": (stage_l1_ffn,
              ["xn1T", "l1_CS", "l1_SS", "maskc4", "deint", "ident", "sel8",
               "l1f_wq", "l1f_wk", "l1f_wv", "l1f_bq", "l1f_bk", "l1f_bv",
               "l1f_wo", "l1f_bo", "l1f_gw", "l1f_gb", "l1f_ew", "l1f_eb",
               "l1f_ln1w", "l1f_ln1b", "l1f_ln2w", "l1f_ln2b"],
              ("maskc4", "deintb"), ["F1T"]),
    "fin": (stage_fin, ["x1T", "A1T", "F1T"], (), ["outT"]),
}

STAGE_ORDER = ["ln0", "l0attn", "l0ffn", "mid", "l1attn", "l1ffn", "fin"]

_INTERMEDIATE = {"xnT", "A0T", "F0T", "x1T", "xn1T", "A1T", "F1T", "outT"}

_NC_CACHE = {}


def _np_dtype_to_mybir(dt):
    if dt == np.dtype("float32"):
        return F32
    if dt == BFNP:
        return BF16
    raise ValueError(dt)


def build_stage_nc(stage, shapes):
    key = stage
    if key in _NC_CACHE:
        return _NC_CACHE[key]
    builder, ins, cwhich, outs = STAGES[stage]
    nc = bass.Bass("TRN2", target_bir_lowering=False)
    stack = ExitStack()
    c = Ctx(nc, stack)
    for name in ins:
        shp, dt = shapes[name]
        c.dram[name] = nc.dram_tensor(name, list(shp), dt, kind="ExternalInput").ap()
    for name in outs:
        c.dram[name] = nc.dram_tensor(name, [1024, 2048], F32, kind="ExternalOutput").ap()
    C = load_consts(c, cwhich)
    c.tk.barrier()
    builder(c, C)
    c.tk.fin()
    stack.close()
    _NC_CACHE[key] = nc
    return nc


def kernel(**inputs):
    consts = host_consts()
    pms = [prep_core_inputs(b, inputs) for b in range(8)]
    inter = [dict() for _ in range(8)]
    shapes = {}
    for k, v in list(pms[0].items()) + list(consts.items()):
        shapes[k] = (v.shape, _np_dtype_to_mybir(v.dtype))
    for k in _INTERMEDIATE:
        shapes[k] = ((1024, 2048), F32)
    for stage in STAGE_ORDER:
        builder, ins, cwhich, outs = STAGES[stage]
        nc = build_stage_nc(stage, shapes)
        in_maps = []
        for b in range(8):
            im = {}
            for name in ins:
                if name in consts:
                    im[name] = consts[name]
                elif name in _INTERMEDIATE:
                    im[name] = inter[b][name]
                else:
                    im[name] = pms[b][name]
            in_maps.append(im)
        res = run_bass_kernel_spmd(nc, in_maps, core_ids=list(range(8)))
        for b in range(8):
            for name in outs:
                inter[b][name] = res.results[b][name]
    out = np.stack([inter[b]["outT"].T for b in range(8)])
    return np.ascontiguousarray(out.astype(np.float32))



FUSED = True

_FUSED_CONSTS = ("maskc4", "mask8", "sel2", "deintb")


def build_fused_nc(shapes):
    if "fused" in _NC_CACHE:
        return _NC_CACHE["fused"]
    nc = bass.Bass("TRN2", target_bir_lowering=False)
    stack = ExitStack()
    c = Ctx(nc, stack)
    seen = set()
    for stage in STAGE_ORDER:
        builder, ins, cwhich, outs = STAGES[stage]
        for name in ins:
            if name in seen or name in _INTERMEDIATE:
                continue
            seen.add(name)
            shp, dt = shapes[name]
            c.dram[name] = nc.dram_tensor(name, list(shp), dt,
                                          kind="ExternalInput").ap()
    for name in _INTERMEDIATE:
        kind = "ExternalOutput" if name == "outT" else "Internal"
        c.dram[name] = nc.dram_tensor(name, [1024, 2048], F32, kind=kind).ap()
    C = load_consts(c, _FUSED_CONSTS)
    c.tk.barrier()
    for stage in STAGE_ORDER:
        builder, ins, cwhich, outs = STAGES[stage]
        with scope(c):
            builder(c, C)
    c.tk.fin()
    stack.close()
    _NC_CACHE["fused"] = nc
    return nc


def _kernel_fused(inputs):
    consts = host_consts()
    pms = [prep_core_inputs(b, inputs) for b in range(8)]
    shapes = {}
    for k, v in list(pms[0].items()) + list(consts.items()):
        shapes[k] = (v.shape, _np_dtype_to_mybir(v.dtype))
    nc = build_fused_nc(shapes)
    names = set()
    for stage in STAGE_ORDER:
        for name in STAGES[stage][1]:
            if name not in _INTERMEDIATE:
                names.add(name)
    in_maps = []
    for b in range(8):
        im = {}
        for name in names:
            im[name] = consts[name] if name in consts else pms[b][name]
        in_maps.append(im)
    res = run_bass_kernel_spmd(nc, in_maps, core_ids=list(range(8)))
    out = np.stack([res.results[b]["outT"].T for b in range(8)])
    return np.ascontiguousarray(out.astype(np.float32))


_kernel_multilaunch = kernel


def kernel(**inputs):
    if FUSED:
        return _kernel_fused(inputs)
    return _kernel_multilaunch(**inputs)
